# version 04
# baseline (speedup 1.0000x reference)
import copy
import numpy as np
import ml_dtypes
import concourse.bass as bass
import concourse.mybir as mybir
from concourse.alu_op_type import AluOpType as OP
from contextlib import ExitStack

F32 = mybir.dt.float32
BF16 = mybir.dt.bfloat16
I32 = mybir.dt.int32
U8 = mybir.dt.uint8
AF = mybir.ActivationFunctionType

D = 2048
T = 4096
DI = 4096
NM = 16384
KC = D // 128
FCH = 70
ICH = DI // 128
CCH = NM // 128
CT = 512
NCH = T // CT
NT = T // CT
RMS_EPS = 1.1920929e-07
LN_EPS = 1e-5

C_ONE, C_NEG1, C_NEGHALF, C_INV2048, C_RMSEPS, C_LNEPS, C_7P5, C_N8, C_N10 = range(9)


class Ctx:
    DMA_SEMS = None

    def __init__(self, src=None):
        object.__setattr__(self, "c", dict(src.c) if src is not None else {})

    def __getattr__(self, k):
        if k == "c":
            raise AttributeError(k)
        return self.c.get(k, 0)

    def __setattr__(self, k, v):
        self.c[k] = v


OVERSYNC = False
P6LITE = 0

def build_nc(debug=False, nphases=8):
    nc = bass.Bass("TRN2", target_bir_lowering=False)

    x_dev = nc.dram_tensor("x_dev", [D, T], F32, kind="ExternalInput")
    w_inT = nc.dram_tensor("w_inT", [D, FCH * 128], F32, kind="ExternalInput")
    w_outT = nc.dram_tensor("w_outT", [DI, D], F32, kind="ExternalInput")
    w_h2i = nc.dram_tensor("w_h2i", [256, DI], F32, kind="ExternalInput")
    wnorm = nc.dram_tensor("wnorm", [128, KC], F32, kind="ExternalInput")
    dtb = nc.dram_tensor("dtb", [128, 2], F32, kind="ExternalInput")
    alog = nc.dram_tensor("alog", [128, 2], F32, kind="ExternalInput")
    wcv = nc.dram_tensor("wcv", [128, 128], F32, kind="ExternalInput")
    gate_pT = nc.dram_tensor("gate_pT", [1024, NM], I32, kind="ExternalInput")
    up_pT = nc.dram_tensor("up_pT", [1024, NM], I32, kind="ExternalInput")
    down_pT = nc.dram_tensor("down_pT", [8192, D], I32, kind="ExternalInput")
    gate_scT = nc.dram_tensor("gate_scT", [64, NM], F32, kind="ExternalInput")
    up_scT = nc.dram_tensor("up_scT", [64, NM], F32, kind="ExternalInput")
    down_scT = nc.dram_tensor("down_scT", [512, D], F32, kind="ExternalInput")
    cst_in = nc.dram_tensor("cst_in", [128, 12], F32, kind="ExternalInput")
    ones1 = nc.dram_tensor("ones1", [1, 128], F32, kind="ExternalInput")
    onesb = nc.dram_tensor("onesb", [128, 1], BF16, kind="ExternalInput")
    ind_in = nc.dram_tensor("ind_in", [128, 1024], BF16, kind="ExternalInput")
    identb_in = nc.dram_tensor("identb_in", [128, 128], BF16, kind="ExternalInput")
    identf_in = nc.dram_tensor("identf_in", [128, 128], F32, kind="ExternalInput")

    out_dev = nc.dram_tensor("out_dev", [T, D], F32, kind="ExternalOutput")

    spill = "ExternalOutput" if debug else "Internal"
    proj_h = nc.dram_tensor("proj_h", [FCH * 128, T], BF16, kind=spill)
    yg_h = nc.dram_tensor("yg_h", [DI, T], BF16, kind=spill)
    attn_h = nc.dram_tensor("attn_h", [D, T], F32, kind=spill)
    x2_h = nc.dram_tensor("x2_h", [D, T], BF16, kind=spill)
    inter_h = nc.dram_tensor("inter_h", [NM, T], BF16, kind=spill)
    ddec_h = nc.dram_tensor("ddec_h", [NM, D], BF16, kind=spill)
    if debug:
        dbg_dt = nc.dram_tensor("dbg_dt", [128, NCH * 1024], F32, kind="ExternalOutput")
        dbg_dA = nc.dram_tensor("dbg_dA", [128, NCH * 1024], F32, kind="ExternalOutput")
        dbg_U = nc.dram_tensor("dbg_U", [128, NCH * 1024], F32, kind="ExternalOutput")
        dbg_H = nc.dram_tensor("dbg_H", [128, NCH * 1024], F32, kind="ExternalOutput")
        dbg_Yt = nc.dram_tensor("dbg_Yt", [128, NCH * 1024], BF16, kind="ExternalOutput")
        dbg_r6 = nc.dram_tensor("dbg_r6", [128, NT * 8192], BF16, kind="ExternalOutput")
        dbg_st6 = nc.dram_tensor("dbg_st6", [128, NT * 8], F32, kind="ExternalOutput")
    scgr_h = nc.dram_tensor("scgr_h", [1024, NM], BF16, kind="Internal")
    scur_h = nc.dram_tensor("scur_h", [1024, NM], BF16, kind="Internal")
    scdr_h = nc.dram_tensor("scdr_h", [8192, D], BF16, kind="Internal")

    ctx = Ctx()
    _open = []
    stp = ExitStack()
    sbp = lambda n_, s_, d_: stp.enter_context(nc.sbuf_tensor(n_, s_, d_))

    s_cst = sbp("s_cst", [128, 12], F32)
    s_ones1 = sbp("s_ones1", [1, 128], F32)
    s_onesb = sbp("s_onesb", [128, 1], BF16)
    s_identb = sbp("s_identb", [128, 128], BF16)
    s_ind = sbp("s_ind", [128, 1024], BF16)
    s_identf = sbp("s_identf", [128, 128], F32)
    s_wn = sbp("s_wn", [128, KC], F32)
    s_dtb = sbp("s_dtb", [128, 2], F32)
    s_nA = sbp("s_nA", [128, 2], F32)
    s_wcv = sbp("s_wcv", [128, 128], F32)
    s_carry = sbp("s_carry", [128, 2], F32)

    DMA_SEMS = ["ld", "scexp", "x1l", "wc0", "wc1", "pst0", "pst1",
                "dbc0", "dbc1", "xil0", "xil1", "zl", "h2i", "ygst", "dbgst",
                "ygr", "wo0", "wo1", "ast0", "ast1", "p4l", "x2st",
                "p5g0", "p5g1", "x2r", "intst", "ddst",
                "x2t", "inA", "inB", "ddl0", "ddl1", "ost0", "ost1"]
    ENG_SEMS = ["pe", "v", "s", "p"]
    Ctx.DMA_SEMS = set(DMA_SEMS)
    sem = {n: stp.enter_context(nc.semaphore("sem_" + n)) for n in DMA_SEMS + ENG_SEMS}

    def cst(i):
        return s_cst[:, i:i + 1]

    def cst1(i):
        return s_cst[0:1, i:i + 1]

    def barrier(eng, c):
        for n, v_ in c.c.items():
            if v_ > 0:
                eng.wait_ge(sem[n], 16 * v_ if n in Ctx.DMA_SEMS else v_)

    with nc.Block("p0") as b:
        @b.sync
        def _(sync):
            for dst, src in [(s_cst, cst_in), (s_ones1, ones1), (s_onesb, onesb),
                             (s_identb, identb_in), (s_identf, identf_in), (s_ind, ind_in),
                             (s_wn, wnorm), (s_dtb, dtb), (s_nA, alog), (s_wcv, wcv)]:
                sync.dma_start(out=dst[:], in_=src[:, :]).then_inc(sem["ld"], 16)
        ctx.ld += 10

        @b.scalar
        def _(scalar):
            scalar.wait_ge(sem["ld"], 16 * ctx.ld)
            scalar.activation(s_nA[:], s_nA[:], AF.Exp).then_inc(sem["s"], 1)
        ctx.s += 1

        @b.vector
        def _(vector):
            vector.wait_ge(sem["s"], ctx.s)
            vector.tensor_single_scalar(s_nA[:], s_nA[:], -1.0, OP.mult).then_inc(sem["v"], 1)
        ctx.v += 1


    if nphases < 2:
        with nc.Block("fin") as b:
            @b.gpsimd
            def _(gpsimd):
                barrier(gpsimd, ctx)
        for _st in reversed(_open):
            _st.close()
        stp.close()
        return nc

    st1 = ExitStack(); _open.append(st1)
    sb1 = lambda n_, s_, d_: st1.enter_context(nc.sbuf_tensor(n_, s_, d_))
    s_xn = sb1("s_xn", [128, KC * T], BF16)
    s_shared = sb1("s_shared", [128, 8192], BF16)
    s_x1b = sb1("s_x1b", [128, KC * CT], BF16)
    s_rstd = sb1("s_rstd", [1, 2 * CT], F32)

    base = Ctx(ctx)
    with nc.Block("p1a") as b:
        sta = ExitStack()
        ps_ss = [sta.enter_context(nc.psum_tensor(f"ps_ss{i}", [1, CT], F32)) for i in range(2)]
        ps_b = [sta.enter_context(nc.psum_tensor(f"ps_b{i}", [128, CT], F32)) for i in range(2)]

        @b.gpsimd
        def _(gpsimd):
            barrier(gpsimd, base)
            for i in range(NCH):
                if i >= 1:
                    gpsimd.wait_ge(sem["v"], base.v + 16 * i)
                gpsimd.dma_start(
                    out=s_x1b[:].rearrange("p (c t) -> p c t", c=KC),
                    in_=x_dev[:, i * CT:(i + 1) * CT].rearrange("(c p) t -> p c t", p=128),
                ).then_inc(sem["x1l"], 16)
        ctx.x1l += NCH

        @b.scalar
        def _(scalar):
            barrier(scalar, base)
            for i in range(NCH):
                scalar.wait_ge(sem["x1l"], 16 * (base.x1l + i + 1))
                if i >= 1:
                    scalar.wait_ge(sem["pe"], base.pe + 2 * i - 1)
                scalar.activation(s_shared[:], s_x1b[:], AF.Square).then_inc(sem["s"], 1)
                scalar.wait_ge(sem["pe"], base.pe + 2 * i + 1)
                sl = s_rstd[0:1, (i % 2) * CT:(i % 2 + 1) * CT]
                scalar.activation(sl, ps_ss[i % 2][0:1, :], AF.Ln,
                                  bias=cst1(C_RMSEPS), scale=cst1(C_INV2048)).then_inc(sem["s"], 1)
                scalar.activation(sl, sl, AF.Exp, scale=cst1(C_NEGHALF)).then_inc(sem["s"], 1)
        ctx.s += 3 * NCH

        @b.tensor
        def _(tensor):
            barrier(tensor, base)
            for i in range(NCH):
                tensor.wait_ge(sem["s"], base.s + 3 * i + 1)
                if i >= 2:
                    tensor.wait_ge(sem["s"], base.s + 3 * (i - 2) + 2)
                for k in range(KC):
                    mm = tensor.matmul(ps_ss[i % 2][0:1, :], s_onesb[:, 0:1],
                                       s_shared[:, k * CT:(k + 1) * CT],
                                       start=(k == 0), stop=(k == KC - 1))
                mm.then_inc(sem["pe"], 1)
                tensor.wait_ge(sem["s"], base.s + 3 * i + 3)
                if i >= 2:
                    tensor.wait_ge(sem["v"], base.v + 16 * (i - 1))
                tensor.matmul(ps_b[i % 2][:, :], s_ones1[0:1, :],
                              s_rstd[0:1, (i % 2) * CT:(i % 2 + 1) * CT],
                              start=True, stop=True).then_inc(sem["pe"], 1)
        ctx.pe += 2 * NCH

        @b.vector
        def _(vector):
            barrier(vector, base)
            for i in range(NCH):
                vector.wait_ge(sem["pe"], base.pe + 2 * i + 2)
                for k in range(KC):
                    vector.scalar_tensor_tensor(
                        s_xn[:, k * T + i * CT:k * T + (i + 1) * CT],
                        s_x1b[:, k * CT:(k + 1) * CT], s_wn[:, k:k + 1],
                        ps_b[i % 2][:, :], OP.mult, OP.mult).then_inc(sem["v"], 1)
        ctx.v += 16 * NCH
        sta.close()

    if nphases < 3:
        with nc.Block("fin") as b:
            @b.gpsimd
            def _(gpsimd):
                barrier(gpsimd, ctx)
        for _st in reversed(_open):
            _st.close()
        stp.close()
        return nc

    s_wcol = [sb1(f"s_wcol{i}", [128, KC * 128], BF16) for i in range(2)]
    base = Ctx(ctx)
    with nc.Block("p1b") as b:
        sta = ExitStack()
        ps1 = [sta.enter_context(nc.psum_tensor(f"ps1_{i}", [128, CT], F32)) for i in range(8)]

        @b.gpsimd
        def _(gpsimd):
            for f in range(FCH):
                if f >= 2:
                    gpsimd.wait_ge(sem["pe"], base.pe + (f - 1) * NT)
                gpsimd.dma_start(
                    out=s_wcol[f % 2][:].rearrange("p (c f) -> p c f", f=128),
                    in_=w_inT[:, f * 128:(f + 1) * 128].rearrange("(c p) f -> p c f", p=128),
                ).then_inc(sem["wc0" if f % 2 == 0 else "wc1"], 16)
        ctx.wc0 += (FCH + 1) // 2
        ctx.wc1 += FCH // 2

        @b.tensor
        def _(tensor):
            barrier(tensor, base)
            for f in range(FCH):
                wsem = "wc0" if f % 2 == 0 else "wc1"
                tensor.wait_ge(sem[wsem], 16 * (getattr(base, wsem) + f // 2 + 1))
                for t in range(NT):
                    if f >= 1:
                        if t % 2 == 0:
                            tensor.wait_ge(sem["v"], base.v + (f - 1) * 4 + t // 2 + 1)
                        else:
                            tensor.wait_ge(sem["s"], base.s + (f - 1) * 4 + (t - 1) // 2 + 1)
                    for k in range(KC):
                        mm = tensor.matmul(ps1[t][:, :],
                                           s_wcol[f % 2][:, k * 128:(k + 1) * 128],
                                           s_xn[:, k * T + t * CT:k * T + (t + 1) * CT],
                                           start=(k == 0), stop=(k == KC - 1))
                    mm.then_inc(sem["pe"], 1)
        ctx.pe += FCH * NT

        @b.vector
        def _(vector):
            barrier(vector, base)
            for f in range(FCH):
                if f >= 2:
                    psem = "pst0" if f % 2 == 0 else "pst1"
                    vector.wait_ge(sem[psem], 16 * (getattr(base, psem) + (f - 2) // 2 + 1))
                for t in range(0, NT, 2):
                    vector.wait_ge(sem["pe"], base.pe + f * NT + t + 1)
                    vector.tensor_copy(
                        s_shared[:, (f % 2) * 4096 + t * CT:(f % 2) * 4096 + (t + 1) * CT],
                        ps1[t][:, :]).then_inc(sem["v"], 1)
        ctx.v += FCH * 4

        @b.scalar
        def _(scalar):
            barrier(scalar, base)
            for f in range(FCH):
                if f >= 2:
                    psem = "pst0" if f % 2 == 0 else "pst1"
                    scalar.wait_ge(sem[psem], 16 * (getattr(base, psem) + (f - 2) // 2 + 1))
                for t in range(1, NT, 2):
                    scalar.wait_ge(sem["pe"], base.pe + f * NT + t + 1)
                    scalar.copy(
                        s_shared[:, (f % 2) * 4096 + t * CT:(f % 2) * 4096 + (t + 1) * CT],
                        ps1[t][:, :]).then_inc(sem["s"], 1)
        ctx.s += FCH * 4

        @b.sync
        def _(sync):
            barrier(sync, base)
            for f in range(FCH):
                sync.wait_ge(sem["v"], base.v + (f + 1) * 4)
                sync.wait_ge(sem["s"], base.s + (f + 1) * 4)
                sync.dma_start(out=proj_h[f * 128:(f + 1) * 128, :],
                               in_=s_shared[:, (f % 2) * 4096:(f % 2) * 4096 + 4096],
                               ).then_inc(sem["pst0" if f % 2 == 0 else "pst1"], 16)
        ctx.pst0 += (FCH + 1) // 2
        ctx.pst1 += FCH // 2
        sta.close()
    st1.close(); _open.remove(st1)

    if nphases < 4:
        with nc.Block("fin") as b:
            @b.gpsimd
            def _(gpsimd):
                barrier(gpsimd, ctx)
        for _st in reversed(_open):
            _st.close()
        stp.close()
        return nc

    st2 = ExitStack(); _open.append(st2)
    sb2 = lambda n_, s_, d_: st2.enter_context(nc.sbuf_tensor(n_, s_, d_))
    s_h2i = sb2("s_h2i", [128, 2 * DI], BF16)
    s_xr = [sb2(f"s_xr{i}", [128, ICH * 520], BF16) for i in range(2)]
    s_zr = sb2("s_zr", [128, ICH * CT], BF16)
    s_dbc = [sb2(f"s_dbc{i}", [128, 3072], BF16) for i in range(2)]
    s_dt = sb2("s_dt", [128, 1024], F32)
    s_dt2 = sb2("s_dt2", [128, 1024], F32)
    s_dt3 = sb2("s_dt3", [128, 1024], F32)
    s_dA = sb2("s_dA", [128, 1024], F32)
    s_du = sb2("s_du", [128, 1024], F32)
    s_U = sb2("s_U", [128, 1024], F32)
    s_H = sb2("s_H", [128, 1024], F32)
    s_Yt = sb2("s_Yt", [128, 1024], BF16)
    s_xc = sb2("s_xc", [128, 2 * CT], F32)
    s_sil = sb2("s_sil", [128, 4 * CT], BF16)
    s_silz = sb2("s_silz", [128, 2 * CT], BF16)
    s_yg = sb2("s_yg", [128, ICH * CT], BF16)

    VP, SP, PP = 72, 66, 64
    V_DTPRE, V_CONV0, V_DT, V_DTA, V_DUU, V_SCAN0, V_CARRY, V_YT, V_YG0 = \
        1, 2, 34, 35, 36, 37, 39, 40, 41
    S_SP, S_SIL0, S_DA, S_SILZ0 = 1, 2, 34, 35

    base = Ctx(ctx)
    with nc.Block("p2") as b:
        sta = ExitStack()
        ps_xs = [sta.enter_context(nc.psum_tensor(f"ps_xs{i}", [128, CT], F32)) for i in range(2)]
        ps_y = [sta.enter_context(nc.psum_tensor(f"ps_y{i}", [128, CT], F32)) for i in range(4)]

        projr = proj_h[:, :].rearrange("(c p) t -> p c t", p=128)

        def vat(i, m):
            return base.v + i * VP + m

        def sat(i, m):
            return base.s + i * SP + m

        def pat(i, m):
            return base.pe + i * PP + m

        @b.gpsimd
        def _(gpsimd):
            barrier(gpsimd, base)
            gpsimd.dma_start(out=s_h2i[:].rearrange("p (c f) -> p c f", f=DI),
                             in_=w_h2i[:, :].rearrange("(c p) f -> p c f", p=128)
                             ).then_inc(sem["h2i"], 16)
            gpsimd.dma_start(
                out=scgr_h[:, :].rearrange("(j r) n -> j r n", r=16),
                in_=gate_scT[:, :].unsqueeze(1).broadcast_to([64, 16, NM])).then_inc(sem["scexp"], 16)
            gpsimd.dma_start(
                out=scur_h[:, :].rearrange("(j r) n -> j r n", r=16),
                in_=up_scT[:, :].unsqueeze(1).broadcast_to([64, 16, NM])).then_inc(sem["scexp"], 16)
            gpsimd.dma_start(
                out=scdr_h[:, :].rearrange("(j r) n -> j r n", r=16),
                in_=down_scT[:, :].unsqueeze(1).broadcast_to([512, 16, D])).then_inc(sem["scexp"], 16)
            for i in range(NCH):
                gpsimd.wait_ge(sem["v"], vat(i, VP))
                gpsimd.dma_start(
                    out=yg_h[:, i * CT:(i + 1) * CT].rearrange("(c p) t -> p c t", p=128),
                    in_=s_yg[:].rearrange("p (c t) -> p c t", c=ICH)).then_inc(sem["ygst"], 16)
                if debug:
                    for dbg, tile in [(dbg_dt, s_dt), (dbg_dA, s_dA), (dbg_U, s_U), (dbg_H, s_H),
                                      (dbg_Yt, s_Yt)]:
                        gpsimd.dma_start(out=dbg[:, i * 1024:(i + 1) * 1024],
                                         in_=tile[:]).then_inc(sem["dbgst"], 16)
        ctx.h2i += 1
        ctx.scexp += 3
        ctx.ygst += NCH
        if debug:
            ctx.dbgst += NCH * 5

        @b.sync
        def _(sync):
            barrier(sync, base)
            for i in range(NCH):
                ldb = base.ld + i * 3
                if i >= 2:
                    sync.wait_ge(sem["v"], vat(i - 2, V_YT))
                sync.dma_start(out=s_dbc[i % 2][:].rearrange("p (r c t) -> p r c t", r=3, c=2),
                               in_=projr[:, 64:70, i * CT:(i + 1) * CT].rearrange(
                                   "p (r c) t -> p r c t", r=3)
                               ).then_inc(sem["dbc0" if i % 2 == 0 else "dbc1"], 16)
                t0 = i * CT
                xrb = s_xr[i % 2][:].rearrange("p (g t) -> p g t", g=ICH)
                if i >= 2:
                    sync.wait_ge(sem["v"], vat(i - 2, V_CONV0 + ICH - 1))
                xsm = "xil0" if i % 2 == 0 else "xil1"
                if t0 == 0:
                    sync.dma_start(out=xrb[:, :, 3:515],
                                   in_=projr[:, 32:64, 0:CT]).then_inc(sem[xsm], 16)
                else:
                    sync.dma_start(out=xrb[:, :, 0:515],
                                   in_=projr[:, 32:64, t0 - 3:t0 + CT]).then_inc(sem[xsm], 16)
                if i >= 1:
                    sync.wait_ge(sem["s"], sat(i - 1, S_SILZ0 + ICH - 1))
                sync.dma_start(out=s_zr[:].rearrange("p (g t) -> p g t", g=ICH),
                               in_=projr[:, 0:32, i * CT:(i + 1) * CT]).then_inc(sem["zl"], 16)
        ctx.dbc0 += (NCH + 1) // 2
        ctx.dbc1 += NCH // 2
        ctx.xil0 += (NCH + 1) // 2
        ctx.xil1 += NCH // 2
        ctx.zl += NCH

        @b.vector
        def _(vector):
            barrier(vector, base)
            for g in range(ICH):
                vector.memset(s_xr[0][:, g * 520:g * 520 + 3], 0.0)
            for i in range(NCH):
                ldb = base.ld + i * 3
                dsm = "dbc0" if i % 2 == 0 else "dbc1"
                vector.wait_ge(sem[dsm], 16 * (getattr(base, dsm) + i // 2 + 1))
                if debug and i > 0:
                    vector.wait_ge(sem["dbgst"], 16 * (base.dbgst + i * 5))
                for h in range(2):
                    op = vector.tensor_scalar(
                        s_dt[:, h * CT:(h + 1) * CT],
                        s_dbc[i % 2][:, h * CT:(h + 1) * CT], s_dtb[:, h:h + 1], None, OP.add)
                op.then_inc(sem["v"], 1)
                xsm = "xil0" if i % 2 == 0 else "xil1"
                vector.wait_ge(sem[xsm], 16 * (getattr(base, xsm) + i // 2 + 1))
                for g in range(ICH):
                    gi = i * ICH + g
                    if gi >= 2:
                        pi, pg = (gi - 2) // ICH, (gi - 2) % ICH
                        vector.wait_ge(sem["s"], sat(pi, S_SIL0 + pg))
                    sl = s_xr[i % 2][:, g * 520:g * 520 + 515]
                    xcs = s_xc[:, (g % 2) * CT:(g % 2 + 1) * CT]
                    w4 = lambda k: s_wcv[:, 4 * g + k:4 * g + k + 1]
                    vector.tensor_scalar(xcs, sl[:, 0:CT], w4(0), None, OP.mult)
                    vector.scalar_tensor_tensor(xcs, sl[:, 1:1 + CT], w4(1), xcs, OP.mult, OP.add)
                    vector.scalar_tensor_tensor(xcs, sl[:, 2:2 + CT], w4(2), xcs, OP.mult, OP.add)
                    vector.scalar_tensor_tensor(xcs, sl[:, 3:3 + CT], w4(3), xcs, OP.mult, OP.add
                                                ).then_inc(sem["v"], 1)
                vector.wait_ge(sem["s"], sat(i, S_SP))
                vector.tensor_tensor(s_dt[:], s_dt2[:], s_dt3[:], OP.add).then_inc(sem["v"], 1)
                for h in range(2):
                    op = vector.tensor_scalar(s_dt2[:, h * CT:(h + 1) * CT],
                                              s_dt[:, h * CT:(h + 1) * CT],
                                              s_nA[:, h:h + 1], None, OP.mult)
                op.then_inc(sem["v"], 1)
                for h in range(2):
                    vector.wait_ge(sem["pe"], pat(i, 16 * (h + 1)))
                    vector.tensor_tensor(s_du[:, h * CT:(h + 1) * CT],
                                         s_dt[:, h * CT:(h + 1) * CT], ps_xs[h][:, :], OP.mult)
                    op = vector.tensor_tensor(s_U[:, h * CT:(h + 1) * CT],
                                              s_du[:, h * CT:(h + 1) * CT],
                                              s_dbc[i % 2][:, 1024 + h * CT:1024 + (h + 1) * CT],
                                              OP.mult)
                op.then_inc(sem["v"], 1)
                vector.wait_ge(sem["s"], sat(i, S_DA))
                for h in range(2):
                    init = 0.0 if i == 0 else s_carry[:, h:h + 1]
                    vector.tensor_tensor_scan(s_H[:, h * CT:(h + 1) * CT],
                                              s_dA[:, h * CT:(h + 1) * CT],
                                              s_U[:, h * CT:(h + 1) * CT],
                                              init, OP.mult, OP.add).then_inc(sem["v"], 1)
                for h in range(2):
                    op = vector.tensor_copy(s_carry[:, h:h + 1],
                                            s_H[:, h * CT + CT - 1:h * CT + CT])
                op.then_inc(sem["v"], 1)
                for h in range(2):
                    op = vector.tensor_tensor(s_Yt[:, h * CT:(h + 1) * CT],
                                              s_H[:, h * CT:(h + 1) * CT],
                                              s_dbc[i % 2][:, 2048 + h * CT:2048 + (h + 1) * CT],
                                              OP.mult)
                op.then_inc(sem["v"], 1)
                for ic in range(ICH):
                    vector.wait_ge(sem["pe"], pat(i, 32 + ic + 1))
                    vector.wait_ge(sem["s"], sat(i, S_SILZ0 + ic) if not OVERSYNC else sat(i, SP))
                    if i > 0 and ic == 0:
                        vector.wait_ge(sem["ygst"], 16 * (base.ygst + i))
                    vector.tensor_tensor(s_yg[:, ic * CT:(ic + 1) * CT], ps_y[ic % 4][:, :],
                                         s_silz[:, (ic % 2) * CT:(ic % 2 + 1) * CT], OP.mult
                                         ).then_inc(sem["v"], 1)
        ctx.v += NCH * VP

        @b.scalar
        def _(scalar):
            barrier(scalar, base)
            for i in range(NCH):
                ldb = base.ld + i * 3
                scalar.wait_ge(sem["v"], vat(i, V_DTPRE))
                scalar.activation(s_dt2[:], s_dt[:], AF.Relu)
                scalar.activation(s_dt3[:], s_dt[:], AF.Abs)
                scalar.activation(s_dt3[:], s_dt3[:], AF.Exp, scale=cst(C_NEG1))
                scalar.activation(s_dt3[:], s_dt3[:], AF.Ln, bias=cst(C_ONE)).then_inc(sem["s"], 1)
                for g in range(ICH):
                    scalar.wait_ge(sem["v"], vat(i, V_CONV0 + g))
                    gi = i * ICH + g
                    if gi >= 4:
                        pi, pg = (gi - 4) // ICH, (gi - 4) % ICH
                        scalar.wait_ge(sem["pe"], pat(pi, pg + 1))
                    scalar.activation(s_sil[:, (g % 4) * CT:(g % 4 + 1) * CT],
                                      s_xc[:, (g % 2) * CT:(g % 2 + 1) * CT],
                                      AF.Silu).then_inc(sem["s"], 1)
                scalar.wait_ge(sem["v"], vat(i, V_DTA))
                if i >= 1:
                    scalar.wait_ge(sem["v"], vat(i - 1, V_SCAN0 + 1))
                scalar.activation(s_dA[:], s_dt2[:], AF.Exp).then_inc(sem["s"], 1)
                scalar.wait_ge(sem["zl"], 16 * (base.zl + i + 1))
                for ic in range(ICH):
                    gi = i * ICH + ic
                    if gi >= 2:
                        pi, pg = (gi - 2) // ICH, (gi - 2) % ICH
                        scalar.wait_ge(sem["v"], vat(pi, V_YG0 + pg))
                    scalar.activation(s_silz[:, (ic % 2) * CT:(ic % 2 + 1) * CT],
                                      s_zr[:, ic * CT:(ic + 1) * CT],
                                      AF.Silu).then_inc(sem["s"], 1)
        ctx.s += NCH * SP

        @b.tensor
        def _(tensor):
            barrier(tensor, base)
            tensor.wait_ge(sem["h2i"], 16 * (base.h2i + 1))
            for i in range(NCH):
                for g in range(ICH):
                    tensor.wait_ge(sem["s"], sat(i, S_SIL0 + g))
                    if g in (0, 16) and i >= 1:
                        tensor.wait_ge(sem["v"], vat(i - 1, V_DUU))
                    mm = tensor.matmul(ps_xs[g // 16][:, :],
                                       s_ind[:, ((g // 2) % 8) * 128:((g // 2) % 8 + 1) * 128],
                                       s_sil[:, (g % 4) * CT:(g % 4 + 1) * CT],
                                       start=(g % 16 == 0), stop=(g % 16 == 15))
                    mm.then_inc(sem["pe"], 1)
                tensor.wait_ge(sem["v"], vat(i, V_YT))
                for ic in range(ICH):
                    gi = i * ICH + ic
                    if gi >= 4:
                        pi, pg = (gi - 4) // ICH, (gi - 4) % ICH
                        tensor.wait_ge(sem["v"], vat(pi, V_YG0 + pg))
                    for h in range(2):
                        mm = tensor.matmul(ps_y[ic % 4][:, :],
                                           s_h2i[:, h * DI + ic * 128:h * DI + (ic + 1) * 128],
                                           s_Yt[:, h * CT:(h + 1) * CT],
                                           start=(h == 0), stop=(h == 1))
                    mm.then_inc(sem["pe"], 1)
        ctx.pe += NCH * PP
        sta.close()
    st2.close(); _open.remove(st2)

    if nphases < 5:
        with nc.Block("fin") as b:
            @b.gpsimd
            def _(gpsimd):
                barrier(gpsimd, ctx)
        for _st in reversed(_open):
            _st.close()
        stp.close()
        return nc

    st3 = ExitStack(); _open.append(st3)
    sb3 = lambda n_, s_, d_: st3.enter_context(nc.sbuf_tensor(n_, s_, d_))
    s_ygr = sb3("s_ygr", [128, ICH * 2048], BF16)
    s_wo = [sb3(f"s_wo{i}", [128, ICH * 128], BF16) for i in range(2)]
    s_ev3 = [sb3(f"s_ev3{i}", [128, 4 * CT], F32) for i in range(2)]
    base = Ctx(ctx)
    HT = 2048
    with nc.Block("p3") as b:
        sta = ExitStack()
        ps3 = [sta.enter_context(nc.psum_tensor(f"ps3_{i}", [128, CT], F32)) for i in range(8)]

        @b.sync
        def _(sync):
            barrier(sync, base)
            for hf in range(2):
                if hf == 1:
                    sync.wait_ge(sem["pe"], base.pe + 16 * 4)
                sync.dma_start(out=s_ygr[:].rearrange("p (c t) -> p c t", c=ICH),
                               in_=yg_h[:, hf * HT:(hf + 1) * HT].rearrange("(c p) t -> p c t", p=128),
                               ).then_inc(sem["ygr"], 16)
                for dc in range(16):
                    j = hf * 16 + dc
                    sync.wait_ge(sem["v"], base.v + j * 2 + 2)
                    sync.wait_ge(sem["s"], base.s + j * 2 + 2)
                    sync.dma_start(out=attn_h[dc * 128:(dc + 1) * 128, hf * HT:(hf + 1) * HT],
                                   in_=s_ev3[j % 2][:, :]
                                   ).then_inc(sem["ast0" if j % 2 == 0 else "ast1"], 16)
        ctx.ygr += 2
        ctx.ast0 += 16
        ctx.ast1 += 16

        @b.gpsimd
        def _(gpsimd):
            barrier(gpsimd, base)
            for hf in range(2):
                for dc in range(16):
                    j = hf * 16 + dc
                    if j >= 2:
                        gpsimd.wait_ge(sem["pe"], base.pe + (j - 1) * 4)
                    gpsimd.dma_start(
                        out=s_wo[j % 2][:].rearrange("p (c f) -> p c f", f=128),
                        in_=w_outT[:, dc * 128:(dc + 1) * 128].rearrange("(c p) f -> p c f", p=128),
                    ).then_inc(sem["wo0" if j % 2 == 0 else "wo1"], 16)
        ctx.wo0 += 16
        ctx.wo1 += 16

        @b.tensor
        def _(tensor):
            barrier(tensor, base)
            for hf in range(2):
                tensor.wait_ge(sem["ygr"], 16 * (base.ygr + hf + 1))
                for dc in range(16):
                    j = hf * 16 + dc
                    wsm = "wo0" if j % 2 == 0 else "wo1"
                    tensor.wait_ge(sem[wsm], 16 * (getattr(base, wsm) + j // 2 + 1))
                    for t in range(4):
                        bank = (j % 2) * 4 + t
                        if j >= 2:
                            if t % 2 == 0:
                                tensor.wait_ge(sem["v"], base.v + (j - 2) * 2 + t // 2 + 1)
                            else:
                                tensor.wait_ge(sem["s"], base.s + (j - 2) * 2 + (t - 1) // 2 + 1)
                        for k in range(ICH):
                            mm = tensor.matmul(ps3[bank][:, :],
                                               s_wo[j % 2][:, k * 128:(k + 1) * 128],
                                               s_ygr[:, k * HT + t * CT:k * HT + (t + 1) * CT],
                                               start=(k == 0), stop=(k == ICH - 1))
                        mm.then_inc(sem["pe"], 1)
        ctx.pe += 32 * 4

        @b.vector
        def _(vector):
            barrier(vector, base)
            for j in range(32):
                if j >= 2:
                    asm_ = "ast0" if j % 2 == 0 else "ast1"
                    vector.wait_ge(sem[asm_], 16 * (getattr(base, asm_) + (j - 2) // 2 + 1))
                for t in range(0, 4, 2):
                    vector.wait_ge(sem["pe"], base.pe + j * 4 + t + 1)
                    vector.tensor_copy(s_ev3[j % 2][:, t * CT:(t + 1) * CT],
                                       ps3[(j % 2) * 4 + t][:, :]).then_inc(sem["v"], 1)
        ctx.v += 64

        @b.scalar
        def _(scalar):
            barrier(scalar, base)
            for j in range(32):
                if j >= 2:
                    asm_ = "ast0" if j % 2 == 0 else "ast1"
                    scalar.wait_ge(sem[asm_], 16 * (getattr(base, asm_) + (j - 2) // 2 + 1))
                for t in range(1, 4, 2):
                    scalar.wait_ge(sem["pe"], base.pe + j * 4 + t + 1)
                    scalar.copy(s_ev3[j % 2][:, t * CT:(t + 1) * CT],
                                ps3[(j % 2) * 4 + t][:, :]).then_inc(sem["s"], 1)
        ctx.s += 64
        sta.close()
    st3.close(); _open.remove(st3)

    if nphases < 6:
        with nc.Block("fin") as b:
            @b.gpsimd
            def _(gpsimd):
                barrier(gpsimd, ctx)
        for _st in reversed(_open):
            _st.close()
        stp.close()
        return nc

    st4 = ExitStack(); _open.append(st4)
    sb4 = lambda n_, s_, d_: st4.enter_context(nc.sbuf_tensor(n_, s_, d_))
    s_ap4 = sb4("s_ap4", [128, KC * CT], F32)
    s_x14 = sb4("s_x14", [128, KC * CT], F32)
    s_r4 = sb4("s_r4", [128, KC * CT], F32)
    s_rb4 = sb4("s_rb4", [128, KC * CT], BF16)
    s_x24 = sb4("s_x24", [128, KC * CT], BF16)
    s_mu4 = sb4("s_mu4", [1, 2 * CT], F32)
    s_rs4 = sb4("s_rs4", [1, 2 * CT], F32)
    base = Ctx(ctx)
    V4, S4, P4N = 5, 4, 4
    with nc.Block("p4") as b:
        sta = ExitStack()
        ps_s1 = [sta.enter_context(nc.psum_tensor(f"ps4a{i}", [1, CT], F32)) for i in range(2)]
        ps_s2 = [sta.enter_context(nc.psum_tensor(f"ps4b{i}", [1, CT], F32)) for i in range(2)]
        ps_mb = [sta.enter_context(nc.psum_tensor(f"ps4c{i}", [128, CT], F32)) for i in range(2)]
        ps_rb = [sta.enter_context(nc.psum_tensor(f"ps4d{i}", [128, CT], F32)) for i in range(2)]

        @b.sync
        def _(sync):
            barrier(sync, base)
            for i in range(NCH):
                if i >= 1:
                    sync.wait_ge(sem["v"], base.v + (i - 1) * V4 + 2)
                sync.dma_start(out=s_ap4[:].rearrange("p (c t) -> p c t", c=KC),
                               in_=attn_h[:, i * CT:(i + 1) * CT].rearrange("(c p) t -> p c t", p=128),
                               ).then_inc(sem["p4l"], 16)
                sync.dma_start(out=s_x14[:].rearrange("p (c t) -> p c t", c=KC),
                               in_=x_dev[:, i * CT:(i + 1) * CT].rearrange("(c p) t -> p c t", p=128),
                               ).then_inc(sem["p4l"], 16)
        ctx.p4l += 2 * NCH

        @b.vector
        def _(vector):
            barrier(vector, base)
            for i in range(NCH):
                vector.wait_ge(sem["p4l"], 16 * (base.p4l + 2 * i + 2))
                vector.tensor_tensor(s_r4[:], s_x14[:], s_ap4[:], OP.add).then_inc(sem["v"], 1)
                if i >= 1:
                    vector.wait_ge(sem["pe"], base.pe + (i - 1) * P4N + 2)
                vector.tensor_copy(s_rb4[:], s_r4[:]).then_inc(sem["v"], 1)
                mu = s_mu4[0:1, (i % 2) * CT:(i % 2 + 1) * CT]
                rs = s_rs4[0:1, (i % 2) * CT:(i % 2 + 1) * CT]
                vector.wait_ge(sem["s"], base.s + i * S4 + 2)
                vector.tensor_tensor(rs, mu, mu, OP.mult).then_inc(sem["v"], 1)
                vector.wait_ge(sem["pe"], base.pe + i * P4N + 2)
                vector.scalar_tensor_tensor(rs, ps_s2[i % 2][0:1, :], 1.0 / 2048.0, rs,
                                            OP.mult, OP.subtract).then_inc(sem["v"], 1)
                vector.wait_ge(sem["pe"], base.pe + i * P4N + 4)
                if i >= 1:
                    vector.wait_ge(sem["x2st"], 16 * (base.x2st + i))
                for k in range(KC):
                    vector.tensor_tensor(s_r4[:, k * CT:(k + 1) * CT], s_r4[:, k * CT:(k + 1) * CT],
                                         ps_mb[i % 2][:, :], OP.subtract)
                    op = vector.tensor_tensor(s_x24[:, k * CT:(k + 1) * CT],
                                              s_r4[:, k * CT:(k + 1) * CT],
                                              ps_rb[i % 2][:, :], OP.mult)
                op.then_inc(sem["v"], 1)
        ctx.v += NCH * V4

        @b.scalar
        def _(scalar):
            barrier(scalar, base)
            for i in range(NCH):
                scalar.wait_ge(sem["v"], base.v + i * V4 + 2)
                scalar.wait_ge(sem["pe"], base.pe + i * P4N + 1)
                scalar.activation(s_rb4[:], s_rb4[:], AF.Square).then_inc(sem["s"], 1)
                mu = s_mu4[0:1, (i % 2) * CT:(i % 2 + 1) * CT]
                rs = s_rs4[0:1, (i % 2) * CT:(i % 2 + 1) * CT]
                scalar.activation(mu, ps_s1[i % 2][0:1, :], AF.Identity,
                                  scale=cst1(C_INV2048)).then_inc(sem["s"], 1)
                scalar.wait_ge(sem["v"], base.v + i * V4 + 4)
                scalar.activation(rs, rs, AF.Ln, bias=cst1(C_LNEPS)).then_inc(sem["s"], 1)
                scalar.activation(rs, rs, AF.Exp, scale=cst1(C_NEGHALF)).then_inc(sem["s"], 1)
        ctx.s += NCH * S4

        @b.tensor
        def _(tensor):
            barrier(tensor, base)
            for i in range(NCH):
                tensor.wait_ge(sem["v"], base.v + i * V4 + 2)
                if i >= 2:
                    tensor.wait_ge(sem["s"], base.s + (i - 2) * S4 + 2)
                for k in range(KC):
                    mm = tensor.matmul(ps_s1[i % 2][0:1, :], s_onesb[:, 0:1],
                                       s_rb4[:, k * CT:(k + 1) * CT],
                                       start=(k == 0), stop=(k == KC - 1))
                mm.then_inc(sem["pe"], 1)
                tensor.wait_ge(sem["s"], base.s + i * S4 + 1)
                if i >= 2:
                    tensor.wait_ge(sem["v"], base.v + (i - 2) * V4 + 4)
                for k in range(KC):
                    mm = tensor.matmul(ps_s2[i % 2][0:1, :], s_onesb[:, 0:1],
                                       s_rb4[:, k * CT:(k + 1) * CT],
                                       start=(k == 0), stop=(k == KC - 1))
                mm.then_inc(sem["pe"], 1)
                tensor.wait_ge(sem["s"], base.s + i * S4 + 2)
                if i >= 2:
                    tensor.wait_ge(sem["v"], base.v + (i - 2) * V4 + 5)
                tensor.matmul(ps_mb[i % 2][:, :], s_ones1[0:1, :],
                              s_mu4[0:1, (i % 2) * CT:(i % 2 + 1) * CT],
                              start=True, stop=True).then_inc(sem["pe"], 1)
                tensor.wait_ge(sem["s"], base.s + i * S4 + 4)
                if i >= 2:
                    tensor.wait_ge(sem["v"], base.v + (i - 2) * V4 + 5)
                tensor.matmul(ps_rb[i % 2][:, :], s_ones1[0:1, :],
                              s_rs4[0:1, (i % 2) * CT:(i % 2 + 1) * CT],
                              start=True, stop=True).then_inc(sem["pe"], 1)
        ctx.pe += NCH * P4N

        @b.gpsimd
        def _(gpsimd):
            barrier(gpsimd, base)
            for i in range(NCH):
                gpsimd.wait_ge(sem["v"], base.v + (i + 1) * V4)
                gpsimd.dma_start(out=x2_h[:, i * CT:(i + 1) * CT].rearrange("(c p) t -> p c t", p=128),
                                 in_=s_x24[:].rearrange("p (c t) -> p c t", c=KC)).then_inc(sem["x2st"], 16)
        ctx.x2st += NCH
        sta.close()
    st4.close(); _open.remove(st4)

    if nphases < 7:
        with nc.Block("fin") as b:
            @b.gpsimd
            def _(gpsimd):
                barrier(gpsimd, ctx)
        for _st in reversed(_open):
            _st.close()
        stp.close()
        return nc

    st5 = ExitStack(); _open.append(st5)
    sb5 = lambda n_, s_, d_: st5.enter_context(nc.sbuf_tensor(n_, s_, d_))
    s_x2r = sb5("s_x2r", [128, KC * T], BF16)
    s_pkg = [sb5(f"s_pkg{i}", [128, 1024], U8) for i in range(2)]
    s_pku = [sb5(f"s_pku{i}", [128, 1024], U8) for i in range(2)]
    s_pkd = [sb5(f"s_pkd{i}", [128, 2048], U8) for i in range(2)]
    s_scg = sb5("s_scg", [128, 1024], BF16)
    s_scu = sb5("s_scu", [128, 1024], BF16)
    s_scd = sb5("s_scd", [128, 2048], BF16)
    s_dg = [sb5(f"s_dg{i}", [128, 2048], BF16) for i in range(2)]
    s_dup = [sb5(f"s_dup{i}", [128, 2048], BF16) for i in range(2)]
    s_dd = sb5("s_dd", [128, 4096], BF16)
    s_gel = sb5("s_gel", [128, 4 * CT], BF16)
    s_int = sb5("s_int", [128, T], BF16)
    s_i1 = sb5("s_i1", [128, 1024], U8)
    s_jb = [sb5(f"s_jb{i}", [128, 1024], BF16) for i in range(2)]
    s_sg5 = [sb5(f"s_sg5{i}", [128, 1024], BF16) for i in range(2)]
    s_j7p = sb5("s_j7p", [128, 1024], BF16)
    s_t1 = sb5("s_t1", [128, 1024], BF16)
    s_t2 = sb5("s_t2", [128, 1024], BF16)
    s_val = [sb5(f"s_val{i}", [128, 1024], BF16) for i in range(2)]

    def nh(c):
        return 8 if c < 64 else 4

    QOFF = [0]
    for c in range(CCH):
        QOFF.append(QOFF[-1] + nh(c))

    def vcn(c):
        return 2 * nh(c) + 8

    def scn(c):
        return nh(c) + 8

    VOFF5 = [0]
    SOFF5 = [0]
    for c in range(CCH):
        VOFF5.append(VOFF5[-1] + vcn(c))
        SOFF5.append(SOFF5[-1] + scn(c))

    def gcnt(c):
        return 4 + (2 if c < 64 else 0)

    GP = [[0], [0]]
    for c in range(CCH):
        GP[0].append(GP[0][-1] + (gcnt(c) if c % 2 == 0 else 0))
        GP[1].append(GP[1][-1] + (gcnt(c) if c % 2 == 1 else 0))

    def halves(c):
        pg, pu, pd = s_pkg[c % 2], s_pku[c % 2], s_pkd[c % 2]
        cg, cu, cd = s_scg, s_scu, s_scd
        out = [(pg[:, 0:1024], 0, cg[:, 0:1024], s_dg[c % 2][:, 0:1024]),
               (pg[:, 0:1024], 4, cg[:, 0:1024], s_dg[c % 2][:, 1024:2048]),
               (pu[:, 0:1024], 0, cu[:, 0:1024], s_dup[c % 2][:, 0:1024]),
               (pu[:, 0:1024], 4, cu[:, 0:1024], s_dup[c % 2][:, 1024:2048])]
        if c < 64:
            for w0 in (0, 1024):
                out.append((pd[:, w0:w0 + 1024], 0, cd[:, w0:w0 + 1024], s_dd[:, w0:w0 + 1024]))
            for w0 in (0, 1024):
                out.append((pd[:, w0:w0 + 1024], 4, cd[:, w0:w0 + 1024], s_dd[:, 2048 + w0:2048 + w0 + 1024]))
        return out

    base = Ctx(ctx)
    with nc.Block("p5") as b:
        sta = ExitStack()
        ps5 = [sta.enter_context(nc.psum_tensor(f"ps5_{i}", [128, CT], F32)) for i in range(8)]

        def v5(c, m):
            return base.v + VOFF5[c] + m

        def vidx_cvt(c, h):
            return base.v + VOFF5[c] + (1 if h == 0 else 2 * h)

        def vidx_val(c, h):
            return base.v + VOFF5[c] + (2 * h + 3 if h < nh(c) - 1 else 2 * nh(c))

        def vidx_val_q(q):
            import bisect
            c = bisect.bisect_right(QOFF, q) - 1
            return vidx_val(c, q - QOFF[c])

        def s5(c, m):
            return base.s + SOFF5[c] + m

        @b.sync
        def _(sync):
            barrier(sync, base)
            sync.dma_start(out=s_x2r[:].rearrange("p (c t) -> p c t", c=KC),
                           in_=x2_h[:, :].rearrange("(c p) t -> p c t", p=128)).then_inc(sem["x2r"], 16)
            for c in range(CCH):
                sync.wait_ge(sem["v"], v5(c, vcn(c)))
                sync.dma_start(out=inter_h[c * 128:(c + 1) * 128, :],
                               in_=s_int[:, :]).then_inc(sem["intst"], 16)
                if c < 64:
                    sync.wait_ge(sem["p"], base.p + QOFF[c] + nh(c))
                    sync.dma_start(out=ddec_h[c * 128:(c + 1) * 128, :],
                                   in_=s_dd[:, 0:2048]).then_inc(sem["ddst"], 16)
                    sync.dma_start(out=ddec_h[(64 + c) * 128:(64 + c + 1) * 128, :],
                                   in_=s_dd[:, 2048:4096]).then_inc(sem["ddst"], 16)
        ctx.x2r += 1
        ctx.intst += CCH
        ctx.ddst += 128

        @b.gpsimd
        def _(gpsimd):
            barrier(gpsimd, base)
            for c in range(CCH + 1):
                if c >= 1:
                    cc = c - 1
                    psm2 = "p5g0" if cc % 2 == 0 else "p5g1"
                    gpsimd.wait_ge(sem[psm2], 16 * (getattr(base, psm2) + GP[cc % 2][cc + 1]))
                    if cc >= 2:
                        gpsimd.wait_ge(sem["pe"], base.pe + (cc - 1) * 16)
                    for h, (pk_ap, shift, sc_ap, out_ap) in enumerate(halves(cc)):
                        if h == 4 and cc >= 1:
                            gpsimd.wait_ge(sem["ddst"], 16 * (base.ddst + 2 * cc))
                        gpsimd.wait_ge(sem["v"], vidx_val(cc, h))
                        gpsimd.tensor_tensor(out_ap, s_val[(QOFF[cc] + h) % 2][:, 0:1024],
                                             sc_ap, OP.mult).then_inc(sem["p"], 1)
                if c < CCH:
                    psm = "p5g0" if c % 2 == 0 else "p5g1"
                    gpsimd.dma_start(out=s_pkg[c % 2][:],
                                     in_=gate_pT[:, c * 128:(c + 1) * 128].rearrange(
                                         "(a p) n -> p a n", p=128)).then_inc(sem[psm], 16)
                    gpsimd.dma_start(out=s_pku[c % 2][:],
                                     in_=up_pT[:, c * 128:(c + 1) * 128].rearrange(
                                         "(a p) n -> p a n", p=128)).then_inc(sem[psm], 16)
                    gpsimd.dma_start(out=s_scg[:],
                                     in_=scgr_h[:, c * 128:(c + 1) * 128].rearrange(
                                         "(a p) n -> p a n", p=128)).then_inc(sem[psm], 16)
                    gpsimd.dma_start(out=s_scu[:],
                                     in_=scur_h[:, c * 128:(c + 1) * 128].rearrange(
                                         "(a p) n -> p a n", p=128)).then_inc(sem[psm], 16)
                    if c < 64:
                        gpsimd.dma_start(out=s_pkd[c % 2][:],
                                         in_=down_pT[c * 128:(c + 1) * 128, :]).then_inc(sem[psm], 16)
                        gpsimd.dma_start(out=s_scd[:],
                                         in_=scdr_h[c * 128:(c + 1) * 128, :]).then_inc(sem[psm], 16)
        ctx.p5g0 += GP[0][CCH]
        ctx.p5g1 += GP[1][CCH]
        ctx.p += QOFF[CCH]

        @b.vector
        def _(vector):
            barrier(vector, base)
            for c in range(CCH):
                psm = "p5g0" if c % 2 == 0 else "p5g1"
                vector.wait_ge(sem[psm], 16 * (getattr(base, psm) + GP[c % 2][c + 1]))
                hs = halves(c)

                def stageA(h):
                    pk_ap, shift, _, _ = hs[h]
                    q = QOFF[c] + h
                    if shift:
                        vector.tensor_single_scalar(s_i1[:], pk_ap, 4, OP.logical_shift_right)
                    else:
                        vector.tensor_single_scalar(s_i1[:], pk_ap, 15, OP.bitwise_and)
                    vector.tensor_copy(s_jb[q % 2][:], s_i1[:]).then_inc(sem["v"], 1)

                def stageB(h):
                    q = QOFF[c] + h
                    vector.wait_ge(sem["s"], s5(c, h + 1))
                    vector.scalar_tensor_tensor(s_j7p[:], s_sg5[q % 2][:], 4.0, s_jb[q % 2][:],
                                                OP.mult, OP.add)
                    vector.tensor_scalar_max(s_t1[:], s_j7p[:], 8.0)
                    vector.tensor_scalar_max(s_t2[:], s_j7p[:], 10.0)
                    vector.scalar_tensor_tensor(s_t1[:], s_t1[:], 0.5, s_t2[:], OP.mult, OP.add)
                    vector.scalar_tensor_tensor(s_t1[:], s_j7p[:], 0.5, s_t1[:], OP.mult, OP.add)
                    if q >= 2:
                        vector.wait_ge(sem["p"], base.p + q - 1)
                    vector.scalar_tensor_tensor(s_val[q % 2][:, 0:1024], s_t1[:], -16.0,
                                                s_sg5[q % 2][:], OP.add, OP.mult
                                                ).then_inc(sem["v"], 1)

                for h in range(nh(c)):
                    stageA(h)
                    if h >= 1:
                        stageB(h - 1)
                stageB(nh(c) - 1)
                for t in range(NT):
                    hgrp, tq = t // 4, t % 4
                    vector.wait_ge(sem["pe"], base.pe + c * 16 + hgrp * 8 + 4 + tq + 1)
                    vector.wait_ge(sem["s"], s5(c, nh(c) + hgrp * 4 + tq + 1))
                    if c >= 1 and t == 0:
                        vector.wait_ge(sem["intst"], 16 * (base.intst + c))
                    vector.tensor_tensor(s_int[:, t * CT:(t + 1) * CT], ps5[4 + tq][:, :],
                                         s_gel[:, tq * CT:(tq + 1) * CT], OP.mult).then_inc(sem["v"], 1)
        ctx.v += VOFF5[CCH]

        @b.scalar
        def _(scalar):
            barrier(scalar, base)
            for c in range(CCH):
                for h in range(nh(c)):
                    q = QOFF[c] + h
                    scalar.wait_ge(sem["v"], vidx_cvt(c, h))
                    if q >= 2:
                        scalar.wait_ge(sem["v"], vidx_val_q(q - 2))
                    scalar.activation(s_sg5[q % 2][:], s_jb[q % 2][:], AF.Sign, bias=cst(C_7P5),
                                      scale=cst(C_NEG1)).then_inc(sem["s"], 1)
                for t in range(NT):
                    hgrp, tq = t // 4, t % 4
                    scalar.wait_ge(sem["pe"], base.pe + c * 16 + hgrp * 8 + tq + 1)
                    if c * 8 + hgrp * 4 + tq >= 4:
                        pidx = c * 8 + hgrp * 4 + tq - 4
                        pc, prem = pidx // 8, pidx % 8
                        scalar.wait_ge(sem["v"], v5(pc, vcn(pc) - 8 + prem + 1))
                    scalar.activation(s_gel[:, tq * CT:(tq + 1) * CT], ps5[tq][:, :],
                                      AF.Gelu_apprx_tanh).then_inc(sem["s"], 1)
        ctx.s += SOFF5[CCH]

        @b.tensor
        def _(tensor):
            barrier(tensor, base)
            tensor.wait_ge(sem["x2r"], 16 * (base.x2r + 1))
            for c in range(CCH):
                tensor.wait_ge(sem["p"], base.p + QOFF[c] + 2)
                for hgrp in range(2):
                    for tq in range(4):
                        t = 4 * hgrp + tq
                        gidx = c * 8 + hgrp * 4 + tq
                        if gidx >= 4:
                            pidx = gidx - 4
                            pc, prem = pidx // 8, pidx % 8
                            tensor.wait_ge(sem["s"], base.s + SOFF5[pc] + nh(pc) + prem + 1)
                        for k in range(KC):
                            mm = tensor.matmul(ps5[tq][:, :], s_dg[c % 2][:, k * 128:(k + 1) * 128],
                                               s_x2r[:, k * T + t * CT:k * T + (t + 1) * CT],
                                               start=(k == 0), stop=(k == KC - 1))
                        mm.then_inc(sem["pe"], 1)
                    if hgrp == 0:
                        tensor.wait_ge(sem["p"], base.p + QOFF[c] + 4)
                    for tq in range(4):
                        t = 4 * hgrp + tq
                        uidx = c * 8 + hgrp * 4 + tq
                        if uidx >= 4:
                            pidx = uidx - 4
                            pc, prem = pidx // 8, pidx % 8
                            tensor.wait_ge(sem["v"], v5(pc, vcn(pc) - 8 + prem + 1))
                        tensor.wait_ge(sem["s"], s5(c, nh(c) + hgrp * 4 + tq + 1))
                        for k in range(KC):
                            mm = tensor.matmul(ps5[tq + 4][:, :],
                                               s_dup[c % 2][:, k * 128:(k + 1) * 128],
                                               s_x2r[:, k * T + t * CT:k * T + (t + 1) * CT],
                                               start=(k == 0), stop=(k == KC - 1))
                        mm.then_inc(sem["pe"], 1)
        ctx.pe += CCH * 16
        sta.close()
    st5.close(); _open.remove(st5)

    st6 = ExitStack(); _open.append(st6)
    sb6 = lambda n_, s_, d_: st6.enter_context(nc.sbuf_tensor(n_, s_, d_))
    s_intr = [sb6(f"s_intr{i}", [128, 64 * CT], BF16) for i in range(2)]
    s_tmp6 = sb6("s_tmp6", [128, KC * CT], BF16)
    s_r6 = sb6("s_r6", [128, KC * CT], BF16)
    s_ddl = [sb6(f"s_ddl{i}", [128, 4 * 1024], BF16) for i in range(2)]
    s_mu6 = sb6("s_mu6", [1, CT], F32)
    s_rs6 = sb6("s_rs6", [1, CT], F32)
    s_st6 = sb6("s_st6", [128, 8], F32)
    s_rt = sb6("s_rt", [128, 2048], BF16)
    s_ot = [sb6(f"s_ot{i}", [128, 2048], BF16) for i in range(2)]

    base = Ctx(ctx)
    PE6, V6, S6, LD6 = 138, 55, 35, 67
    with nc.Block("p6") as b:
        sta = ExitStack()
        ps6 = [sta.enter_context(nc.psum_tensor(f"ps6_{i}", [128, CT], F32)) for i in range(8)]
        tp_alias = [ps6[3 + k][:, 0:64].bitcast(BF16) for k in range(4)]

        def pe6(tau, m):
            return base.pe + tau * PE6 + m

        def v6(tau, m):
            return base.v + tau * V6 + m

        def s6(tau, m):
            return base.s + tau * S6 + m

        @b.sync
        def _(sync):
            barrier(sync, base)
            for tau in range(NT):
                ldb = base.ld + tau * LD6
                if tau >= 1:
                    sync.wait_ge(sem["pe"], pe6(tau - 1, 66))
                sync.dma_start(out=s_tmp6[:].rearrange("p (c t) -> p c t", c=KC),
                               in_=x2_h[:, tau * CT:(tau + 1) * CT].rearrange(
                                   "(c p) t -> p c t", p=128)).then_inc(sem["x2t"], 16)
                if tau >= 1:
                    sync.wait_ge(sem["pe"], pe6(tau - 1, 48))
                sync.dma_start(out=s_intr[0][:].rearrange("p (c t) -> p c t", c=64),
                               in_=inter_h[0:8192, tau * CT:(tau + 1) * CT].rearrange(
                                   "(c p) t -> p c t", p=128)).then_inc(sem["inA"], 16)
                if tau >= 1:
                    sync.wait_ge(sem["pe"], pe6(tau - 1, 64))
                sync.dma_start(out=s_intr[1][:].rearrange("p (c t) -> p c t", c=64),
                               in_=inter_h[8192:NM, tau * CT:(tau + 1) * CT].rearrange(
                                   "(c p) t -> p c t", p=128)).then_inc(sem["inB"], 16)
                for grp in range(2):
                    for G in range(32):
                        gg = tau * 64 + grp * 32 + G
                        if gg >= 2:
                            pgg = gg - 2
                            pt, prem = pgg // 64, pgg % 64
                            sync.wait_ge(sem["pe"], pe6(pt, prem + 1))
                        sync.dma_start(
                            out=s_ddl[G % 2][:].rearrange("p (c n) -> p c n", c=4),
                            in_=ddec_h[G * 512:(G + 1) * 512,
                                       grp * 1024:(grp + 1) * 1024].rearrange(
                                "(c p) n -> p c n", p=128)
                        ).then_inc(sem["ddl0" if gg % 2 == 0 else "ddl1"], 16)
        ctx.x2t += NT
        ctx.inA += NT
        ctx.inB += NT
        ctx.ddl0 += NT * 32
        ctx.ddl1 += NT * 32

        @b.tensor
        def _(tensor):
            barrier(tensor, base)
            for tau in range(NT):
                ldb = base.ld + tau * LD6
                for grp in range(2):
                    if tau >= 1 and grp == 0:
                        tensor.wait_ge(sem["v"], v6(tau - 1, V6))
                        tensor.wait_ge(sem["s"], s6(tau - 1, S6))
                    if grp == 1:
                        tensor.wait_ge(sem["v"], v6(tau, 8))
                    tensor.wait_ge(sem["inA" if grp == 0 else "inB"],
                                   16 * (getattr(base, "inA" if grp == 0 else "inB") + tau + 1))
                    for G in range(32):
                        gg = tau * 64 + grp * 32 + G
                        dsm = "ddl0" if gg % 2 == 0 else "ddl1"
                        tensor.wait_ge(sem[dsm], 16 * (getattr(base, dsm) + gg // 2 + 1))
                        for cc in range(4):
                            c = G * 4 + cc
                            intr = s_intr[0] if c < 64 else s_intr[1]
                            for dc in range(8):
                                mm = tensor.matmul(
                                    ps6[dc][:, :],
                                    s_ddl[G % 2][:, cc * 1024 + dc * 128:cc * 1024 + (dc + 1) * 128],
                                    intr[:, (c % 64) * CT:(c % 64 + 1) * CT],
                                    start=(c == 0), stop=(c == CCH - 1))
                        mm.then_inc(sem["pe"], 1)
                tensor.wait_ge(sem["v"], v6(tau, 16))
                for k in range(KC):
                    mm = tensor.matmul(ps6[0][0:1, :], s_onesb[:, 0:1],
                                       s_r6[:, k * CT:(k + 1) * CT],
                                       start=(k == 0), stop=(k == KC - 1))
                mm.then_inc(sem["pe"], 1)
                tensor.wait_ge(sem["s"], s6(tau, 2))
                for k in range(KC):
                    mm = tensor.matmul(ps6[1][0:1, :], s_onesb[:, 0:1],
                                       s_tmp6[:, k * CT:(k + 1) * CT],
                                       start=(k == 0), stop=(k == KC - 1))
                mm.then_inc(sem["pe"], 1)
                tensor.wait_ge(sem["s"], s6(tau, 1))
                for j in range(4):
                    tensor.matmul(ps6[2][:, j:j + 1],
                                  s_mu6[0:1, j * 128:(j + 1) * 128], s_ones1[0:1, 0:1],
                                  start=True, stop=True).then_inc(sem["pe"], 1)
                tensor.wait_ge(sem["s"], s6(tau, 3))
                for j in range(4):
                    tensor.matmul(ps6[2][:, 4 + j:5 + j],
                                  s_rs6[0:1, j * 128:(j + 1) * 128], s_ones1[0:1, 0:1],
                                  start=True, stop=True).then_inc(sem["pe"], 1)
                cnt = 0
                for j in range(4):
                    for dc in range(16):
                        if cnt >= 4:
                            prev = cnt - 4
                            pj, pdc = prev // 16, prev % 16
                            if pdc % 2 == 0:
                                tensor.wait_ge(sem["v"], v6(tau, 19 + pj * 9 + pdc // 2 + 1))
                            else:
                                tensor.wait_ge(sem["s"], s6(tau, 3 + pj * 8 + (pdc - 1) // 2 + 1))
                        tensor.transpose(tp_alias[cnt % 4],
                                         s_r6[:, dc * CT + j * 128:dc * CT + (j + 1) * 128],
                                         s_identb[:, :]).then_inc(sem["pe"], 1)
                        cnt += 1
        ctx.pe += NT * PE6

        @b.vector
        def _(vector):
            barrier(vector, base)
            for tau in range(NT):
                vector.wait_ge(sem["x2t"], 16 * (base.x2t + tau + 1))
                if debug and tau >= 1:
                    vector.wait_ge(sem["dbgst"], 16 * (base.dbgst + 2 * tau))
                for grp in range(2):
                    vector.wait_ge(sem["pe"], pe6(tau, 32 * (grp + 1)))
                    for dc in range(8):
                        vector.tensor_tensor(
                            s_r6[:, (grp * 8 + dc) * CT:(grp * 8 + dc + 1) * CT],
                            ps6[dc][:, :],
                            s_tmp6[:, (grp * 8 + dc) * CT:(grp * 8 + dc + 1) * CT],
                            OP.add).then_inc(sem["v"], 1)
                vector.wait_ge(sem["s"], s6(tau, 1))
                vector.tensor_tensor(s_rs6[0:1, :], s_mu6[0:1, :], s_mu6[0:1, :],
                                     OP.mult).then_inc(sem["v"], 1)
                vector.wait_ge(sem["pe"], pe6(tau, 66))
                vector.scalar_tensor_tensor(s_rs6[0:1, :], ps6[1][0:1, :], 1.0 / 2048.0,
                                            s_rs6[0:1, :], OP.mult, OP.subtract
                                            ).then_inc(sem["v"], 1)
                vector.wait_ge(sem["pe"], pe6(tau, 74))
                vector.tensor_copy(s_st6[:, 0:8], ps6[2][:, 0:8]).then_inc(sem["v"], 1)
                for j in range(4):
                    for dc in range(0, 16, 2):
                        cnt = j * 16 + dc
                        vector.wait_ge(sem["pe"], pe6(tau, 74 + cnt + 1))
                        vector.tensor_copy(s_rt[:, dc * 128:(dc + 1) * 128],
                                           tp_alias[cnt % 4]).then_inc(sem["v"], 1)
                    vector.wait_ge(sem["s"], s6(tau, 3 + (j + 1) * 8))
                    gj = tau * 4 + j
                    if gj >= 2:
                        osm = "ost0" if gj % 2 == 0 else "ost1"
                        vector.wait_ge(sem[osm], 16 * (getattr(base, osm) + (gj - 2) // 2 + 1))
                    vector.tensor_scalar(s_ot[j % 2][:, :], s_rt[:, :],
                                         s_st6[:, j:j + 1], s_st6[:, 4 + j:5 + j],
                                         OP.subtract, OP.mult).then_inc(sem["v"], 1)
        ctx.v += NT * V6

        @b.scalar
        def _(scalar):
            barrier(scalar, base)
            for tau in range(NT):
                scalar.wait_ge(sem["pe"], pe6(tau, 65))
                scalar.activation(s_mu6[0:1, :], ps6[0][0:1, :], AF.Identity,
                                  scale=cst1(C_INV2048)).then_inc(sem["s"], 1)
                scalar.wait_ge(sem["v"], v6(tau, 16))
                scalar.activation(s_tmp6[:], s_r6[:], AF.Square).then_inc(sem["s"], 1)
                scalar.wait_ge(sem["v"], v6(tau, 18))
                scalar.activation(s_rs6[0:1, :], s_rs6[0:1, :], AF.Ln, bias=cst1(C_LNEPS))
                scalar.activation(s_rs6[0:1, :], s_rs6[0:1, :], AF.Exp,
                                  scale=cst1(C_NEGHALF)).then_inc(sem["s"], 1)
                for j in range(4):
                    for dc in range(1, 16, 2):
                        cnt = j * 16 + dc
                        if j >= 1 and dc == 1:
                            scalar.wait_ge(sem["v"], v6(tau, 28 + 9 * (j - 1)))
                        scalar.wait_ge(sem["pe"], pe6(tau, 74 + cnt + 1))
                        scalar.copy(s_rt[:, dc * 128:(dc + 1) * 128],
                                    tp_alias[cnt % 4]).then_inc(sem["s"], 1)
        ctx.s += NT * S6

        @b.gpsimd
        def _(gpsimd):
            barrier(gpsimd, base)
            for tau in range(NT):
                if debug:
                    gpsimd.wait_ge(sem["v"], v6(tau, 16))
                    gpsimd.dma_start(out=dbg_r6[:, tau * 8192:(tau + 1) * 8192],
                                     in_=s_r6[:]).then_inc(sem["dbgst"], 16)
                    gpsimd.wait_ge(sem["v"], v6(tau, 19))
                    gpsimd.dma_start(out=dbg_st6[:, tau * 8:(tau + 1) * 8],
                                     in_=s_st6[:, 0:8]).then_inc(sem["dbgst"], 16)
                for j in range(4):
                    gj = tau * 4 + j
                    gpsimd.wait_ge(sem["v"], v6(tau, 28 + 9 * j))
                    gpsimd.dma_start(out=out_dev[tau * CT + j * 128:tau * CT + (j + 1) * 128, :],
                                     in_=s_ot[j % 2][:, :]
                                     ).then_inc(sem["ost0" if gj % 2 == 0 else "ost1"], 16)
            gpsimd.wait_ge(sem["ost0"], 16 * (base.ost0 + NT * 2))
            gpsimd.wait_ge(sem["ost1"], 16 * (base.ost1 + NT * 2))
            if debug:
                gpsimd.wait_ge(sem["dbgst"], 16 * (base.dbgst + NT * 2))
        ctx.ost0 += NT * 2
        ctx.ost1 += NT * 2
        if debug:
            ctx.dbgst += NT * 2
        sta.close()
    st6.close(); _open.remove(st6)
    stp.close()
    return nc


def prep_shared(inp):
    perm = np.concatenate([np.arange(0, D, 2), np.arange(1, D, 2)])
    perm2 = np.concatenate([np.arange(0, NM, 2), np.arange(1, NM, 2)])
    W_in = np.asarray(inp["L0_attn_W_in"], np.float32)
    Wf = np.concatenate([
        W_in[0:4096],
        W_in[4096:8192],
        np.repeat(W_in[8192:8208], 16, axis=0),
        W_in[8208:8464],
        W_in[8464:8720],
    ], axis=0)
    d = {}
    d["w_inT"] = np.ascontiguousarray(Wf.T[perm])
    d["w_outT"] = np.ascontiguousarray(np.asarray(inp["L0_attn_W_out"], np.float32).T[:, perm])
    d["w_h2i"] = np.ascontiguousarray(np.repeat(np.asarray(inp["L0_attn_W_h2i"], np.float32).T, 16, axis=0))
    d["wnorm"] = np.ascontiguousarray(np.asarray(inp["L0_attn_w_norm"], np.float32)[perm].reshape(KC, 128).T)
    d["dtb"] = np.ascontiguousarray(np.repeat(np.asarray(inp["L0_attn_dt_bias"], np.float32), 16).reshape(2, 128).T)
    d["alog"] = np.ascontiguousarray(np.repeat(np.asarray(inp["L0_attn_A_log"], np.float32), 16).reshape(2, 128).T)
    d["wcv"] = np.ascontiguousarray(
        np.asarray(inp["L0_attn_w_conv"], np.float32).reshape(DI, 4).reshape(ICH, 128, 4)
        .transpose(1, 0, 2).reshape(128, 128))
    d["gate_pT"] = np.ascontiguousarray(np.asarray(inp["L0_ffn_gate_W_packed"], np.int32).T[:, perm2])
    d["up_pT"] = np.ascontiguousarray(np.asarray(inp["L0_ffn_up_W_packed"], np.int32).T[:, perm2])
    d["down_pT"] = np.ascontiguousarray(np.asarray(inp["L0_ffn_down_W_packed"], np.int32).T[:, perm])
    d["gate_scT"] = np.ascontiguousarray(np.asarray(inp["L0_ffn_gate_scales"], np.float32).T[:, perm2])
    d["up_scT"] = np.ascontiguousarray(np.asarray(inp["L0_ffn_up_scales"], np.float32).T[:, perm2])
    d["down_scT"] = np.ascontiguousarray(np.asarray(inp["L0_ffn_down_scales"], np.float32).T[:, perm])
    cstv = np.zeros(12, np.float32)
    cstv[C_ONE] = 1.0
    cstv[C_NEG1] = -1.0
    cstv[C_NEGHALF] = -0.5
    cstv[C_INV2048] = 1.0 / 2048.0
    cstv[C_RMSEPS] = RMS_EPS
    cstv[C_LNEPS] = LN_EPS
    cstv[C_7P5] = 7.5
    cstv[C_N8] = -8.0
    cstv[C_N10] = -10.0
    d["cst_in"] = np.tile(cstv, (128, 1))
    d["ones1"] = np.ones((1, 128), np.float32)
    d["onesb"] = np.ones((128, 1), ml_dtypes.bfloat16)
    ind = np.zeros((128, 8, 128), np.float32)
    for bblk in range(8):
        ind[:, bblk, 16 * bblk:16 * bblk + 16] = 1.0 / 256.0
    d["ind_in"] = ind.reshape(128, 1024).astype(ml_dtypes.bfloat16)
    d["identb_in"] = np.eye(128, dtype=ml_dtypes.bfloat16)
    d["identf_in"] = np.eye(128, dtype=np.float32)
    d["_perm"] = perm
    return d


def prep_core_x(inp, core, perm):
    x = np.asarray(inp["L0_attn_x"], np.float32)[core]
    return np.ascontiguousarray(x.T[perm])


def unpermute_out(res_out, perm):
    inv = np.empty_like(perm)
    inv[perm] = np.arange(D)
    return np.ascontiguousarray(np.asarray(res_out, np.float32)[:, inv])


def kernel(**inputs):
    from concourse.bass_utils import run_bass_kernel_spmd
    shared = prep_shared(inputs)
    perm = shared.pop("_perm")
    nc = build_nc(debug=False)
    in_maps = []
    for core in range(8):
        m = dict(shared)
        m["x_dev"] = prep_core_x(inputs, core, perm)
        in_maps.append(m)
    res = run_bass_kernel_spmd(nc, in_maps, core_ids=list(range(8)))
    out = np.stack([unpermute_out(r["out_dev"], perm) for r in res.results], axis=0)
    return out


# version 05
# speedup vs baseline: 1.2196x; 1.2196x over previous
import copy
import numpy as np
import ml_dtypes
import concourse.bass as bass
import concourse.mybir as mybir
from concourse.alu_op_type import AluOpType as OP
from contextlib import ExitStack

F32 = mybir.dt.float32
BF16 = mybir.dt.bfloat16
I32 = mybir.dt.int32
U8 = mybir.dt.uint8
AF = mybir.ActivationFunctionType

D = 2048
T = 4096
DI = 4096
NM = 16384
KC = D // 128
FCH = 70
ICH = DI // 128
CCH = NM // 128
CT = 512
NCH = T // CT
NT = T // CT
RMS_EPS = 1.1920929e-07
LN_EPS = 1e-5

C_ONE, C_NEG1, C_NEGHALF, C_INV2048, C_RMSEPS, C_LNEPS, C_7P5, C_N8, C_N10 = range(9)


class Ctx:
    DMA_SEMS = None

    def __init__(self, src=None):
        object.__setattr__(self, "c", dict(src.c) if src is not None else {})

    def __getattr__(self, k):
        if k == "c":
            raise AttributeError(k)
        return self.c.get(k, 0)

    def __setattr__(self, k, v):
        self.c[k] = v


OVERSYNC = False
P6LITE = 0

def build_nc(debug=False, nphases=8):
    nc = bass.Bass("TRN2", target_bir_lowering=False)

    x_dev = nc.dram_tensor("x_dev", [D, T], F32, kind="ExternalInput")
    w_inT = nc.dram_tensor("w_inT", [D, FCH * 128], F32, kind="ExternalInput")
    w_outT = nc.dram_tensor("w_outT", [DI, D], F32, kind="ExternalInput")
    w_h2i = nc.dram_tensor("w_h2i", [256, DI], F32, kind="ExternalInput")
    wnorm = nc.dram_tensor("wnorm", [128, KC], F32, kind="ExternalInput")
    dtb = nc.dram_tensor("dtb", [128, 2], F32, kind="ExternalInput")
    alog = nc.dram_tensor("alog", [128, 2], F32, kind="ExternalInput")
    wcv = nc.dram_tensor("wcv", [128, 128], F32, kind="ExternalInput")
    gate_pT = nc.dram_tensor("gate_pT", [1024, NM], I32, kind="ExternalInput")
    up_pT = nc.dram_tensor("up_pT", [1024, NM], I32, kind="ExternalInput")
    down_pT = nc.dram_tensor("down_pT", [8192, D], I32, kind="ExternalInput")
    gate_scT = nc.dram_tensor("gate_scT", [64, NM], F32, kind="ExternalInput")
    up_scT = nc.dram_tensor("up_scT", [64, NM], F32, kind="ExternalInput")
    down_scT = nc.dram_tensor("down_scT", [512, D], F32, kind="ExternalInput")
    cst_in = nc.dram_tensor("cst_in", [128, 12], F32, kind="ExternalInput")
    ones1 = nc.dram_tensor("ones1", [1, 128], F32, kind="ExternalInput")
    onesb = nc.dram_tensor("onesb", [128, 1], BF16, kind="ExternalInput")
    ind_in = nc.dram_tensor("ind_in", [128, 1024], BF16, kind="ExternalInput")
    identb_in = nc.dram_tensor("identb_in", [128, 128], BF16, kind="ExternalInput")
    identf_in = nc.dram_tensor("identf_in", [128, 128], F32, kind="ExternalInput")

    out_dev = nc.dram_tensor("out_dev", [T, D], F32, kind="ExternalOutput")

    spill = "ExternalOutput" if debug else "Internal"
    proj_h = nc.dram_tensor("proj_h", [FCH * 128, T], BF16, kind=spill)
    yg_h = nc.dram_tensor("yg_h", [DI, T], BF16, kind=spill)
    attn_h = nc.dram_tensor("attn_h", [D, T], F32, kind=spill)
    x2_h = nc.dram_tensor("x2_h", [D, T], BF16, kind=spill)
    inter_h = nc.dram_tensor("inter_h", [NM, T], BF16, kind=spill)
    ddec_h = nc.dram_tensor("ddec_h", [NM, D], BF16, kind=spill)
    if debug:
        dbg_dt = nc.dram_tensor("dbg_dt", [128, NCH * 1024], F32, kind="ExternalOutput")
        dbg_dA = nc.dram_tensor("dbg_dA", [128, NCH * 1024], F32, kind="ExternalOutput")
        dbg_U = nc.dram_tensor("dbg_U", [128, NCH * 1024], F32, kind="ExternalOutput")
        dbg_H = nc.dram_tensor("dbg_H", [128, NCH * 1024], F32, kind="ExternalOutput")
        dbg_Yt = nc.dram_tensor("dbg_Yt", [128, NCH * 1024], BF16, kind="ExternalOutput")
        dbg_r6 = nc.dram_tensor("dbg_r6", [128, NT * 8192], BF16, kind="ExternalOutput")
        dbg_st6 = nc.dram_tensor("dbg_st6", [128, NT * 8], F32, kind="ExternalOutput")
    scgr_h = nc.dram_tensor("scgr_h", [1024, NM], BF16, kind="Internal")
    scur_h = nc.dram_tensor("scur_h", [1024, NM], BF16, kind="Internal")
    scdr_h = nc.dram_tensor("scdr_h", [8192, D], BF16, kind="Internal")

    ctx = Ctx()
    _open = []
    stp = ExitStack()
    sbp = lambda n_, s_, d_: stp.enter_context(nc.sbuf_tensor(n_, s_, d_))

    s_cst = sbp("s_cst", [128, 12], F32)
    s_ones1 = sbp("s_ones1", [1, 128], F32)
    s_onesb = sbp("s_onesb", [128, 1], BF16)
    s_identb = sbp("s_identb", [128, 128], BF16)
    s_ind = sbp("s_ind", [128, 1024], BF16)
    s_identf = sbp("s_identf", [128, 128], F32)
    s_wn = sbp("s_wn", [128, KC], F32)
    s_dtb = sbp("s_dtb", [128, 2], F32)
    s_nA = sbp("s_nA", [128, 2], F32)
    s_wcv = sbp("s_wcv", [128, 128], F32)
    s_carry = sbp("s_carry", [128, 2], F32)

    DMA_SEMS = ["ld", "scexp", "x1l", "wc0", "wc1", "pst0", "pst1",
                "dbc0", "dbc1", "xil0", "xil1", "zl", "h2i", "ygst", "dbgst",
                "ygr", "wo0", "wo1", "ast0", "ast1", "p4l", "x2st",
                "p5g", "p5g0", "p5g1", "x2r", "intst", "ddst",
                "x2t", "inA", "inB", "ddl0", "ddl1", "ost0", "ost1"]
    ENG_SEMS = ["pe", "v", "s", "p"]
    Ctx.DMA_SEMS = set(DMA_SEMS)
    sem = {n: stp.enter_context(nc.semaphore("sem_" + n)) for n in DMA_SEMS + ENG_SEMS}

    def cst(i):
        return s_cst[:, i:i + 1]

    def cst1(i):
        return s_cst[0:1, i:i + 1]

    def barrier(eng, c):
        for n, v_ in c.c.items():
            if v_ > 0:
                eng.wait_ge(sem[n], 16 * v_ if n in Ctx.DMA_SEMS else v_)

    with nc.Block("p0") as b:
        @b.sync
        def _(sync):
            for dst, src in [(s_cst, cst_in), (s_ones1, ones1), (s_onesb, onesb),
                             (s_identb, identb_in), (s_identf, identf_in), (s_ind, ind_in),
                             (s_wn, wnorm), (s_dtb, dtb), (s_nA, alog), (s_wcv, wcv)]:
                sync.dma_start(out=dst[:], in_=src[:, :]).then_inc(sem["ld"], 16)
        ctx.ld += 10

        @b.scalar
        def _(scalar):
            scalar.wait_ge(sem["ld"], 16 * ctx.ld)
            scalar.activation(s_nA[:], s_nA[:], AF.Exp).then_inc(sem["s"], 1)
        ctx.s += 1

        @b.vector
        def _(vector):
            vector.wait_ge(sem["s"], ctx.s)
            vector.tensor_single_scalar(s_nA[:], s_nA[:], -1.0, OP.mult).then_inc(sem["v"], 1)
        ctx.v += 1


    if nphases < 2:
        with nc.Block("fin") as b:
            @b.gpsimd
            def _(gpsimd):
                barrier(gpsimd, ctx)
        for _st in reversed(_open):
            _st.close()
        stp.close()
        return nc

    st1 = ExitStack(); _open.append(st1)
    sb1 = lambda n_, s_, d_: st1.enter_context(nc.sbuf_tensor(n_, s_, d_))
    s_xn = sb1("s_xn", [128, KC * T], BF16)
    s_shared = sb1("s_shared", [128, 8192], BF16)
    s_x1b = sb1("s_x1b", [128, KC * CT], BF16)
    s_rstd = sb1("s_rstd", [1, 2 * CT], F32)

    base = Ctx(ctx)
    with nc.Block("p1a") as b:
        sta = ExitStack()
        ps_ss = [sta.enter_context(nc.psum_tensor(f"ps_ss{i}", [1, CT], F32)) for i in range(2)]
        ps_b = [sta.enter_context(nc.psum_tensor(f"ps_b{i}", [128, CT], F32)) for i in range(2)]

        @b.gpsimd
        def _(gpsimd):
            barrier(gpsimd, base)
            for i in range(NCH):
                if i >= 1:
                    gpsimd.wait_ge(sem["v"], base.v + 16 * i)
                gpsimd.dma_start(
                    out=s_x1b[:].rearrange("p (c t) -> p c t", c=KC),
                    in_=x_dev[:, i * CT:(i + 1) * CT].rearrange("(c p) t -> p c t", p=128),
                ).then_inc(sem["x1l"], 16)
        ctx.x1l += NCH

        @b.scalar
        def _(scalar):
            barrier(scalar, base)
            for i in range(NCH):
                scalar.wait_ge(sem["x1l"], 16 * (base.x1l + i + 1))
                if i >= 1:
                    scalar.wait_ge(sem["pe"], base.pe + 2 * i - 1)
                scalar.activation(s_shared[:], s_x1b[:], AF.Square).then_inc(sem["s"], 1)
                scalar.wait_ge(sem["pe"], base.pe + 2 * i + 1)
                sl = s_rstd[0:1, (i % 2) * CT:(i % 2 + 1) * CT]
                scalar.activation(sl, ps_ss[i % 2][0:1, :], AF.Ln,
                                  bias=cst1(C_RMSEPS), scale=cst1(C_INV2048)).then_inc(sem["s"], 1)
                scalar.activation(sl, sl, AF.Exp, scale=cst1(C_NEGHALF)).then_inc(sem["s"], 1)
        ctx.s += 3 * NCH

        @b.tensor
        def _(tensor):
            barrier(tensor, base)
            for i in range(NCH):
                tensor.wait_ge(sem["s"], base.s + 3 * i + 1)
                if i >= 2:
                    tensor.wait_ge(sem["s"], base.s + 3 * (i - 2) + 2)
                for k in range(KC):
                    mm = tensor.matmul(ps_ss[i % 2][0:1, :], s_onesb[:, 0:1],
                                       s_shared[:, k * CT:(k + 1) * CT],
                                       start=(k == 0), stop=(k == KC - 1))
                mm.then_inc(sem["pe"], 1)
                tensor.wait_ge(sem["s"], base.s + 3 * i + 3)
                if i >= 2:
                    tensor.wait_ge(sem["v"], base.v + 16 * (i - 1))
                tensor.matmul(ps_b[i % 2][:, :], s_ones1[0:1, :],
                              s_rstd[0:1, (i % 2) * CT:(i % 2 + 1) * CT],
                              start=True, stop=True).then_inc(sem["pe"], 1)
        ctx.pe += 2 * NCH

        @b.vector
        def _(vector):
            barrier(vector, base)
            for i in range(NCH):
                vector.wait_ge(sem["pe"], base.pe + 2 * i + 2)
                for k in range(KC):
                    vector.scalar_tensor_tensor(
                        s_xn[:, k * T + i * CT:k * T + (i + 1) * CT],
                        s_x1b[:, k * CT:(k + 1) * CT], s_wn[:, k:k + 1],
                        ps_b[i % 2][:, :], OP.mult, OP.mult).then_inc(sem["v"], 1)
        ctx.v += 16 * NCH
        sta.close()

    if nphases < 3:
        with nc.Block("fin") as b:
            @b.gpsimd
            def _(gpsimd):
                barrier(gpsimd, ctx)
        for _st in reversed(_open):
            _st.close()
        stp.close()
        return nc

    s_wcol = [sb1(f"s_wcol{i}", [128, KC * 128], BF16) for i in range(2)]
    base = Ctx(ctx)
    with nc.Block("p1b") as b:
        sta = ExitStack()
        ps1 = [sta.enter_context(nc.psum_tensor(f"ps1_{i}", [128, CT], F32)) for i in range(8)]

        @b.gpsimd
        def _(gpsimd):
            for f in range(FCH):
                if f >= 2:
                    gpsimd.wait_ge(sem["pe"], base.pe + (f - 1) * NT)
                gpsimd.dma_start(
                    out=s_wcol[f % 2][:].rearrange("p (c f) -> p c f", f=128),
                    in_=w_inT[:, f * 128:(f + 1) * 128].rearrange("(c p) f -> p c f", p=128),
                ).then_inc(sem["wc0" if f % 2 == 0 else "wc1"], 16)
        ctx.wc0 += (FCH + 1) // 2
        ctx.wc1 += FCH // 2

        @b.tensor
        def _(tensor):
            barrier(tensor, base)
            for f in range(FCH):
                wsem = "wc0" if f % 2 == 0 else "wc1"
                tensor.wait_ge(sem[wsem], 16 * (getattr(base, wsem) + f // 2 + 1))
                for t in range(NT):
                    if f >= 1:
                        if t % 2 == 0:
                            tensor.wait_ge(sem["v"], base.v + (f - 1) * 4 + t // 2 + 1)
                        else:
                            tensor.wait_ge(sem["s"], base.s + (f - 1) * 4 + (t - 1) // 2 + 1)
                    for k in range(KC):
                        mm = tensor.matmul(ps1[t][:, :],
                                           s_wcol[f % 2][:, k * 128:(k + 1) * 128],
                                           s_xn[:, k * T + t * CT:k * T + (t + 1) * CT],
                                           start=(k == 0), stop=(k == KC - 1))
                    mm.then_inc(sem["pe"], 1)
        ctx.pe += FCH * NT

        @b.vector
        def _(vector):
            barrier(vector, base)
            for f in range(FCH):
                if f >= 2:
                    psem = "pst0" if f % 2 == 0 else "pst1"
                    vector.wait_ge(sem[psem], 16 * (getattr(base, psem) + (f - 2) // 2 + 1))
                for t in range(0, NT, 2):
                    vector.wait_ge(sem["pe"], base.pe + f * NT + t + 1)
                    vector.tensor_copy(
                        s_shared[:, (f % 2) * 4096 + t * CT:(f % 2) * 4096 + (t + 1) * CT],
                        ps1[t][:, :]).then_inc(sem["v"], 1)
        ctx.v += FCH * 4

        @b.scalar
        def _(scalar):
            barrier(scalar, base)
            for f in range(FCH):
                if f >= 2:
                    psem = "pst0" if f % 2 == 0 else "pst1"
                    scalar.wait_ge(sem[psem], 16 * (getattr(base, psem) + (f - 2) // 2 + 1))
                for t in range(1, NT, 2):
                    scalar.wait_ge(sem["pe"], base.pe + f * NT + t + 1)
                    scalar.copy(
                        s_shared[:, (f % 2) * 4096 + t * CT:(f % 2) * 4096 + (t + 1) * CT],
                        ps1[t][:, :]).then_inc(sem["s"], 1)
        ctx.s += FCH * 4

        @b.sync
        def _(sync):
            barrier(sync, base)
            for f in range(FCH):
                sync.wait_ge(sem["v"], base.v + (f + 1) * 4)
                sync.wait_ge(sem["s"], base.s + (f + 1) * 4)
                sync.dma_start(out=proj_h[f * 128:(f + 1) * 128, :],
                               in_=s_shared[:, (f % 2) * 4096:(f % 2) * 4096 + 4096],
                               ).then_inc(sem["pst0" if f % 2 == 0 else "pst1"], 16)
        ctx.pst0 += (FCH + 1) // 2
        ctx.pst1 += FCH // 2
        sta.close()
    st1.close(); _open.remove(st1)

    if nphases < 4:
        with nc.Block("fin") as b:
            @b.gpsimd
            def _(gpsimd):
                barrier(gpsimd, ctx)
        for _st in reversed(_open):
            _st.close()
        stp.close()
        return nc

    st2 = ExitStack(); _open.append(st2)
    sb2 = lambda n_, s_, d_: st2.enter_context(nc.sbuf_tensor(n_, s_, d_))
    s_h2i = sb2("s_h2i", [128, 2 * DI], BF16)
    s_xr = [sb2(f"s_xr{i}", [128, ICH * 520], BF16) for i in range(2)]
    s_zr = sb2("s_zr", [128, ICH * CT], BF16)
    s_dbc = [sb2(f"s_dbc{i}", [128, 3072], BF16) for i in range(2)]
    s_dt = sb2("s_dt", [128, 1024], F32)
    s_dt2 = sb2("s_dt2", [128, 1024], F32)
    s_dt3 = sb2("s_dt3", [128, 1024], F32)
    s_dA = sb2("s_dA", [128, 1024], F32)
    s_du = sb2("s_du", [128, 1024], F32)
    s_U = sb2("s_U", [128, 1024], F32)
    s_H = sb2("s_H", [128, 1024], F32)
    s_Yt = sb2("s_Yt", [128, 1024], BF16)
    s_xc = sb2("s_xc", [128, 2 * CT], F32)
    s_sil = sb2("s_sil", [128, 4 * CT], BF16)
    s_silz = sb2("s_silz", [128, 2 * CT], BF16)
    s_yg = sb2("s_yg", [128, ICH * CT], BF16)

    VP, SP, PP = 72, 66, 64
    V_DTPRE, V_CONV0, V_DT, V_DTA, V_DUU, V_SCAN0, V_CARRY, V_YT, V_YG0 = \
        1, 2, 34, 35, 36, 37, 39, 40, 41
    S_SP, S_SIL0, S_DA, S_SILZ0 = 1, 2, 34, 35

    base = Ctx(ctx)
    with nc.Block("p2") as b:
        sta = ExitStack()
        ps_xs = [sta.enter_context(nc.psum_tensor(f"ps_xs{i}", [128, CT], F32)) for i in range(2)]
        ps_y = [sta.enter_context(nc.psum_tensor(f"ps_y{i}", [128, CT], F32)) for i in range(4)]

        projr = proj_h[:, :].rearrange("(c p) t -> p c t", p=128)

        def vat(i, m):
            return base.v + i * VP + m

        def sat(i, m):
            return base.s + i * SP + m

        def pat(i, m):
            return base.pe + i * PP + m

        @b.gpsimd
        def _(gpsimd):
            barrier(gpsimd, base)
            gpsimd.dma_start(out=s_h2i[:].rearrange("p (c f) -> p c f", f=DI),
                             in_=w_h2i[:, :].rearrange("(c p) f -> p c f", p=128)
                             ).then_inc(sem["h2i"], 16)
            gpsimd.dma_start(
                out=scgr_h[:, :].rearrange("(j r) n -> j r n", r=16),
                in_=gate_scT[:, :].unsqueeze(1).broadcast_to([64, 16, NM])).then_inc(sem["scexp"], 16)
            gpsimd.dma_start(
                out=scur_h[:, :].rearrange("(j r) n -> j r n", r=16),
                in_=up_scT[:, :].unsqueeze(1).broadcast_to([64, 16, NM])).then_inc(sem["scexp"], 16)
            gpsimd.dma_start(
                out=scdr_h[:, :].rearrange("(j r) n -> j r n", r=16),
                in_=down_scT[:, :].unsqueeze(1).broadcast_to([512, 16, D])).then_inc(sem["scexp"], 16)
            for i in range(NCH):
                gpsimd.wait_ge(sem["v"], vat(i, VP))
                gpsimd.dma_start(
                    out=yg_h[:, i * CT:(i + 1) * CT].rearrange("(c p) t -> p c t", p=128),
                    in_=s_yg[:].rearrange("p (c t) -> p c t", c=ICH)).then_inc(sem["ygst"], 16)
                if debug:
                    for dbg, tile in [(dbg_dt, s_dt), (dbg_dA, s_dA), (dbg_U, s_U), (dbg_H, s_H),
                                      (dbg_Yt, s_Yt)]:
                        gpsimd.dma_start(out=dbg[:, i * 1024:(i + 1) * 1024],
                                         in_=tile[:]).then_inc(sem["dbgst"], 16)
        ctx.h2i += 1
        ctx.scexp += 3
        ctx.ygst += NCH
        if debug:
            ctx.dbgst += NCH * 5

        @b.sync
        def _(sync):
            barrier(sync, base)
            for i in range(NCH):
                ldb = base.ld + i * 3
                if i >= 2:
                    sync.wait_ge(sem["v"], vat(i - 2, V_YT))
                sync.dma_start(out=s_dbc[i % 2][:].rearrange("p (r c t) -> p r c t", r=3, c=2),
                               in_=projr[:, 64:70, i * CT:(i + 1) * CT].rearrange(
                                   "p (r c) t -> p r c t", r=3)
                               ).then_inc(sem["dbc0" if i % 2 == 0 else "dbc1"], 16)
                t0 = i * CT
                xrb = s_xr[i % 2][:].rearrange("p (g t) -> p g t", g=ICH)
                if i >= 2:
                    sync.wait_ge(sem["v"], vat(i - 2, V_CONV0 + ICH - 1))
                xsm = "xil0" if i % 2 == 0 else "xil1"
                if t0 == 0:
                    sync.dma_start(out=xrb[:, :, 3:515],
                                   in_=projr[:, 32:64, 0:CT]).then_inc(sem[xsm], 16)
                else:
                    sync.dma_start(out=xrb[:, :, 0:515],
                                   in_=projr[:, 32:64, t0 - 3:t0 + CT]).then_inc(sem[xsm], 16)
                if i >= 1:
                    sync.wait_ge(sem["s"], sat(i - 1, S_SILZ0 + ICH - 1))
                sync.dma_start(out=s_zr[:].rearrange("p (g t) -> p g t", g=ICH),
                               in_=projr[:, 0:32, i * CT:(i + 1) * CT]).then_inc(sem["zl"], 16)
        ctx.dbc0 += (NCH + 1) // 2
        ctx.dbc1 += NCH // 2
        ctx.xil0 += (NCH + 1) // 2
        ctx.xil1 += NCH // 2
        ctx.zl += NCH

        @b.vector
        def _(vector):
            barrier(vector, base)
            for g in range(ICH):
                vector.memset(s_xr[0][:, g * 520:g * 520 + 3], 0.0)
            for i in range(NCH):
                ldb = base.ld + i * 3
                dsm = "dbc0" if i % 2 == 0 else "dbc1"
                vector.wait_ge(sem[dsm], 16 * (getattr(base, dsm) + i // 2 + 1))
                if debug and i > 0:
                    vector.wait_ge(sem["dbgst"], 16 * (base.dbgst + i * 5))
                for h in range(2):
                    op = vector.tensor_scalar(
                        s_dt[:, h * CT:(h + 1) * CT],
                        s_dbc[i % 2][:, h * CT:(h + 1) * CT], s_dtb[:, h:h + 1], None, OP.add)
                op.then_inc(sem["v"], 1)
                xsm = "xil0" if i % 2 == 0 else "xil1"
                vector.wait_ge(sem[xsm], 16 * (getattr(base, xsm) + i // 2 + 1))
                for g in range(ICH):
                    gi = i * ICH + g
                    if gi >= 2:
                        pi, pg = (gi - 2) // ICH, (gi - 2) % ICH
                        vector.wait_ge(sem["s"], sat(pi, S_SIL0 + pg))
                    sl = s_xr[i % 2][:, g * 520:g * 520 + 515]
                    xcs = s_xc[:, (g % 2) * CT:(g % 2 + 1) * CT]
                    w4 = lambda k: s_wcv[:, 4 * g + k:4 * g + k + 1]
                    vector.tensor_scalar(xcs, sl[:, 0:CT], w4(0), None, OP.mult)
                    vector.scalar_tensor_tensor(xcs, sl[:, 1:1 + CT], w4(1), xcs, OP.mult, OP.add)
                    vector.scalar_tensor_tensor(xcs, sl[:, 2:2 + CT], w4(2), xcs, OP.mult, OP.add)
                    vector.scalar_tensor_tensor(xcs, sl[:, 3:3 + CT], w4(3), xcs, OP.mult, OP.add
                                                ).then_inc(sem["v"], 1)
                vector.wait_ge(sem["s"], sat(i, S_SP))
                vector.tensor_tensor(s_dt[:], s_dt2[:], s_dt3[:], OP.add).then_inc(sem["v"], 1)
                for h in range(2):
                    op = vector.tensor_scalar(s_dt2[:, h * CT:(h + 1) * CT],
                                              s_dt[:, h * CT:(h + 1) * CT],
                                              s_nA[:, h:h + 1], None, OP.mult)
                op.then_inc(sem["v"], 1)
                for h in range(2):
                    vector.wait_ge(sem["pe"], pat(i, 16 * (h + 1)))
                    vector.tensor_tensor(s_du[:, h * CT:(h + 1) * CT],
                                         s_dt[:, h * CT:(h + 1) * CT], ps_xs[h][:, :], OP.mult)
                    op = vector.tensor_tensor(s_U[:, h * CT:(h + 1) * CT],
                                              s_du[:, h * CT:(h + 1) * CT],
                                              s_dbc[i % 2][:, 1024 + h * CT:1024 + (h + 1) * CT],
                                              OP.mult)
                op.then_inc(sem["v"], 1)
                vector.wait_ge(sem["s"], sat(i, S_DA))
                for h in range(2):
                    init = 0.0 if i == 0 else s_carry[:, h:h + 1]
                    vector.tensor_tensor_scan(s_H[:, h * CT:(h + 1) * CT],
                                              s_dA[:, h * CT:(h + 1) * CT],
                                              s_U[:, h * CT:(h + 1) * CT],
                                              init, OP.mult, OP.add).then_inc(sem["v"], 1)
                for h in range(2):
                    op = vector.tensor_copy(s_carry[:, h:h + 1],
                                            s_H[:, h * CT + CT - 1:h * CT + CT])
                op.then_inc(sem["v"], 1)
                for h in range(2):
                    op = vector.tensor_tensor(s_Yt[:, h * CT:(h + 1) * CT],
                                              s_H[:, h * CT:(h + 1) * CT],
                                              s_dbc[i % 2][:, 2048 + h * CT:2048 + (h + 1) * CT],
                                              OP.mult)
                op.then_inc(sem["v"], 1)
                for ic in range(ICH):
                    vector.wait_ge(sem["pe"], pat(i, 32 + ic + 1))
                    vector.wait_ge(sem["s"], sat(i, S_SILZ0 + ic) if not OVERSYNC else sat(i, SP))
                    if i > 0 and ic == 0:
                        vector.wait_ge(sem["ygst"], 16 * (base.ygst + i))
                    vector.tensor_tensor(s_yg[:, ic * CT:(ic + 1) * CT], ps_y[ic % 4][:, :],
                                         s_silz[:, (ic % 2) * CT:(ic % 2 + 1) * CT], OP.mult
                                         ).then_inc(sem["v"], 1)
        ctx.v += NCH * VP

        @b.scalar
        def _(scalar):
            barrier(scalar, base)
            for i in range(NCH):
                ldb = base.ld + i * 3
                scalar.wait_ge(sem["v"], vat(i, V_DTPRE))
                scalar.activation(s_dt2[:], s_dt[:], AF.Relu)
                scalar.activation(s_dt3[:], s_dt[:], AF.Abs)
                scalar.activation(s_dt3[:], s_dt3[:], AF.Exp, scale=cst(C_NEG1))
                scalar.activation(s_dt3[:], s_dt3[:], AF.Ln, bias=cst(C_ONE)).then_inc(sem["s"], 1)
                for g in range(ICH):
                    scalar.wait_ge(sem["v"], vat(i, V_CONV0 + g))
                    gi = i * ICH + g
                    if gi >= 4:
                        pi, pg = (gi - 4) // ICH, (gi - 4) % ICH
                        scalar.wait_ge(sem["pe"], pat(pi, pg + 1))
                    scalar.activation(s_sil[:, (g % 4) * CT:(g % 4 + 1) * CT],
                                      s_xc[:, (g % 2) * CT:(g % 2 + 1) * CT],
                                      AF.Silu).then_inc(sem["s"], 1)
                scalar.wait_ge(sem["v"], vat(i, V_DTA))
                if i >= 1:
                    scalar.wait_ge(sem["v"], vat(i - 1, V_SCAN0 + 1))
                scalar.activation(s_dA[:], s_dt2[:], AF.Exp).then_inc(sem["s"], 1)
                scalar.wait_ge(sem["zl"], 16 * (base.zl + i + 1))
                for ic in range(ICH):
                    gi = i * ICH + ic
                    if gi >= 2:
                        pi, pg = (gi - 2) // ICH, (gi - 2) % ICH
                        scalar.wait_ge(sem["v"], vat(pi, V_YG0 + pg))
                    scalar.activation(s_silz[:, (ic % 2) * CT:(ic % 2 + 1) * CT],
                                      s_zr[:, ic * CT:(ic + 1) * CT],
                                      AF.Silu).then_inc(sem["s"], 1)
        ctx.s += NCH * SP

        @b.tensor
        def _(tensor):
            barrier(tensor, base)
            tensor.wait_ge(sem["h2i"], 16 * (base.h2i + 1))
            for i in range(NCH):
                for g in range(ICH):
                    tensor.wait_ge(sem["s"], sat(i, S_SIL0 + g))
                    if g in (0, 16) and i >= 1:
                        tensor.wait_ge(sem["v"], vat(i - 1, V_DUU))
                    mm = tensor.matmul(ps_xs[g // 16][:, :],
                                       s_ind[:, ((g // 2) % 8) * 128:((g // 2) % 8 + 1) * 128],
                                       s_sil[:, (g % 4) * CT:(g % 4 + 1) * CT],
                                       start=(g % 16 == 0), stop=(g % 16 == 15))
                    mm.then_inc(sem["pe"], 1)
                tensor.wait_ge(sem["v"], vat(i, V_YT))
                for ic in range(ICH):
                    gi = i * ICH + ic
                    if gi >= 4:
                        pi, pg = (gi - 4) // ICH, (gi - 4) % ICH
                        tensor.wait_ge(sem["v"], vat(pi, V_YG0 + pg))
                    for h in range(2):
                        mm = tensor.matmul(ps_y[ic % 4][:, :],
                                           s_h2i[:, h * DI + ic * 128:h * DI + (ic + 1) * 128],
                                           s_Yt[:, h * CT:(h + 1) * CT],
                                           start=(h == 0), stop=(h == 1))
                    mm.then_inc(sem["pe"], 1)
        ctx.pe += NCH * PP
        sta.close()
    st2.close(); _open.remove(st2)

    if nphases < 5:
        with nc.Block("fin") as b:
            @b.gpsimd
            def _(gpsimd):
                barrier(gpsimd, ctx)
        for _st in reversed(_open):
            _st.close()
        stp.close()
        return nc

    st3 = ExitStack(); _open.append(st3)
    sb3 = lambda n_, s_, d_: st3.enter_context(nc.sbuf_tensor(n_, s_, d_))
    s_ygr = sb3("s_ygr", [128, ICH * 2048], BF16)
    s_wo = [sb3(f"s_wo{i}", [128, ICH * 128], BF16) for i in range(2)]
    s_ev3 = [sb3(f"s_ev3{i}", [128, 4 * CT], F32) for i in range(2)]
    base = Ctx(ctx)
    HT = 2048
    with nc.Block("p3") as b:
        sta = ExitStack()
        ps3 = [sta.enter_context(nc.psum_tensor(f"ps3_{i}", [128, CT], F32)) for i in range(8)]

        @b.sync
        def _(sync):
            barrier(sync, base)
            for hf in range(2):
                if hf == 1:
                    sync.wait_ge(sem["pe"], base.pe + 16 * 4)
                sync.dma_start(out=s_ygr[:].rearrange("p (c t) -> p c t", c=ICH),
                               in_=yg_h[:, hf * HT:(hf + 1) * HT].rearrange("(c p) t -> p c t", p=128),
                               ).then_inc(sem["ygr"], 16)
                for dc in range(16):
                    j = hf * 16 + dc
                    sync.wait_ge(sem["v"], base.v + j * 2 + 2)
                    sync.wait_ge(sem["s"], base.s + j * 2 + 2)
                    sync.dma_start(out=attn_h[dc * 128:(dc + 1) * 128, hf * HT:(hf + 1) * HT],
                                   in_=s_ev3[j % 2][:, :]
                                   ).then_inc(sem["ast0" if j % 2 == 0 else "ast1"], 16)
        ctx.ygr += 2
        ctx.ast0 += 16
        ctx.ast1 += 16

        @b.gpsimd
        def _(gpsimd):
            barrier(gpsimd, base)
            for hf in range(2):
                for dc in range(16):
                    j = hf * 16 + dc
                    if j >= 2:
                        gpsimd.wait_ge(sem["pe"], base.pe + (j - 1) * 4)
                    gpsimd.dma_start(
                        out=s_wo[j % 2][:].rearrange("p (c f) -> p c f", f=128),
                        in_=w_outT[:, dc * 128:(dc + 1) * 128].rearrange("(c p) f -> p c f", p=128),
                    ).then_inc(sem["wo0" if j % 2 == 0 else "wo1"], 16)
        ctx.wo0 += 16
        ctx.wo1 += 16

        @b.tensor
        def _(tensor):
            barrier(tensor, base)
            for hf in range(2):
                tensor.wait_ge(sem["ygr"], 16 * (base.ygr + hf + 1))
                for dc in range(16):
                    j = hf * 16 + dc
                    wsm = "wo0" if j % 2 == 0 else "wo1"
                    tensor.wait_ge(sem[wsm], 16 * (getattr(base, wsm) + j // 2 + 1))
                    for t in range(4):
                        bank = (j % 2) * 4 + t
                        if j >= 2:
                            if t % 2 == 0:
                                tensor.wait_ge(sem["v"], base.v + (j - 2) * 2 + t // 2 + 1)
                            else:
                                tensor.wait_ge(sem["s"], base.s + (j - 2) * 2 + (t - 1) // 2 + 1)
                        for k in range(ICH):
                            mm = tensor.matmul(ps3[bank][:, :],
                                               s_wo[j % 2][:, k * 128:(k + 1) * 128],
                                               s_ygr[:, k * HT + t * CT:k * HT + (t + 1) * CT],
                                               start=(k == 0), stop=(k == ICH - 1))
                        mm.then_inc(sem["pe"], 1)
        ctx.pe += 32 * 4

        @b.vector
        def _(vector):
            barrier(vector, base)
            for j in range(32):
                if j >= 2:
                    asm_ = "ast0" if j % 2 == 0 else "ast1"
                    vector.wait_ge(sem[asm_], 16 * (getattr(base, asm_) + (j - 2) // 2 + 1))
                for t in range(0, 4, 2):
                    vector.wait_ge(sem["pe"], base.pe + j * 4 + t + 1)
                    vector.tensor_copy(s_ev3[j % 2][:, t * CT:(t + 1) * CT],
                                       ps3[(j % 2) * 4 + t][:, :]).then_inc(sem["v"], 1)
        ctx.v += 64

        @b.scalar
        def _(scalar):
            barrier(scalar, base)
            for j in range(32):
                if j >= 2:
                    asm_ = "ast0" if j % 2 == 0 else "ast1"
                    scalar.wait_ge(sem[asm_], 16 * (getattr(base, asm_) + (j - 2) // 2 + 1))
                for t in range(1, 4, 2):
                    scalar.wait_ge(sem["pe"], base.pe + j * 4 + t + 1)
                    scalar.copy(s_ev3[j % 2][:, t * CT:(t + 1) * CT],
                                ps3[(j % 2) * 4 + t][:, :]).then_inc(sem["s"], 1)
        ctx.s += 64
        sta.close()
    st3.close(); _open.remove(st3)

    if nphases < 6:
        with nc.Block("fin") as b:
            @b.gpsimd
            def _(gpsimd):
                barrier(gpsimd, ctx)
        for _st in reversed(_open):
            _st.close()
        stp.close()
        return nc

    st4 = ExitStack(); _open.append(st4)
    sb4 = lambda n_, s_, d_: st4.enter_context(nc.sbuf_tensor(n_, s_, d_))
    s_ap4 = sb4("s_ap4", [128, KC * CT], F32)
    s_x14 = sb4("s_x14", [128, KC * CT], F32)
    s_r4 = sb4("s_r4", [128, KC * CT], F32)
    s_rb4 = sb4("s_rb4", [128, KC * CT], BF16)
    s_x24 = sb4("s_x24", [128, KC * CT], BF16)
    s_mu4 = sb4("s_mu4", [1, 2 * CT], F32)
    s_rs4 = sb4("s_rs4", [1, 2 * CT], F32)
    base = Ctx(ctx)
    V4, S4, P4N = 5, 4, 4
    with nc.Block("p4") as b:
        sta = ExitStack()
        ps_s1 = [sta.enter_context(nc.psum_tensor(f"ps4a{i}", [1, CT], F32)) for i in range(2)]
        ps_s2 = [sta.enter_context(nc.psum_tensor(f"ps4b{i}", [1, CT], F32)) for i in range(2)]
        ps_mb = [sta.enter_context(nc.psum_tensor(f"ps4c{i}", [128, CT], F32)) for i in range(2)]
        ps_rb = [sta.enter_context(nc.psum_tensor(f"ps4d{i}", [128, CT], F32)) for i in range(2)]

        @b.sync
        def _(sync):
            barrier(sync, base)
            for i in range(NCH):
                if i >= 1:
                    sync.wait_ge(sem["v"], base.v + (i - 1) * V4 + 2)
                sync.dma_start(out=s_ap4[:].rearrange("p (c t) -> p c t", c=KC),
                               in_=attn_h[:, i * CT:(i + 1) * CT].rearrange("(c p) t -> p c t", p=128),
                               ).then_inc(sem["p4l"], 16)
                sync.dma_start(out=s_x14[:].rearrange("p (c t) -> p c t", c=KC),
                               in_=x_dev[:, i * CT:(i + 1) * CT].rearrange("(c p) t -> p c t", p=128),
                               ).then_inc(sem["p4l"], 16)
        ctx.p4l += 2 * NCH

        @b.vector
        def _(vector):
            barrier(vector, base)
            for i in range(NCH):
                vector.wait_ge(sem["p4l"], 16 * (base.p4l + 2 * i + 2))
                vector.tensor_tensor(s_r4[:], s_x14[:], s_ap4[:], OP.add).then_inc(sem["v"], 1)
                if i >= 1:
                    vector.wait_ge(sem["pe"], base.pe + (i - 1) * P4N + 2)
                vector.tensor_copy(s_rb4[:], s_r4[:]).then_inc(sem["v"], 1)
                mu = s_mu4[0:1, (i % 2) * CT:(i % 2 + 1) * CT]
                rs = s_rs4[0:1, (i % 2) * CT:(i % 2 + 1) * CT]
                vector.wait_ge(sem["s"], base.s + i * S4 + 2)
                vector.tensor_tensor(rs, mu, mu, OP.mult).then_inc(sem["v"], 1)
                vector.wait_ge(sem["pe"], base.pe + i * P4N + 2)
                vector.scalar_tensor_tensor(rs, ps_s2[i % 2][0:1, :], 1.0 / 2048.0, rs,
                                            OP.mult, OP.subtract).then_inc(sem["v"], 1)
                vector.wait_ge(sem["pe"], base.pe + i * P4N + 4)
                if i >= 1:
                    vector.wait_ge(sem["x2st"], 16 * (base.x2st + i))
                for k in range(KC):
                    vector.tensor_tensor(s_r4[:, k * CT:(k + 1) * CT], s_r4[:, k * CT:(k + 1) * CT],
                                         ps_mb[i % 2][:, :], OP.subtract)
                    op = vector.tensor_tensor(s_x24[:, k * CT:(k + 1) * CT],
                                              s_r4[:, k * CT:(k + 1) * CT],
                                              ps_rb[i % 2][:, :], OP.mult)
                op.then_inc(sem["v"], 1)
        ctx.v += NCH * V4

        @b.scalar
        def _(scalar):
            barrier(scalar, base)
            for i in range(NCH):
                scalar.wait_ge(sem["v"], base.v + i * V4 + 2)
                scalar.wait_ge(sem["pe"], base.pe + i * P4N + 1)
                scalar.activation(s_rb4[:], s_rb4[:], AF.Square).then_inc(sem["s"], 1)
                mu = s_mu4[0:1, (i % 2) * CT:(i % 2 + 1) * CT]
                rs = s_rs4[0:1, (i % 2) * CT:(i % 2 + 1) * CT]
                scalar.activation(mu, ps_s1[i % 2][0:1, :], AF.Identity,
                                  scale=cst1(C_INV2048)).then_inc(sem["s"], 1)
                scalar.wait_ge(sem["v"], base.v + i * V4 + 4)
                scalar.activation(rs, rs, AF.Ln, bias=cst1(C_LNEPS)).then_inc(sem["s"], 1)
                scalar.activation(rs, rs, AF.Exp, scale=cst1(C_NEGHALF)).then_inc(sem["s"], 1)
        ctx.s += NCH * S4

        @b.tensor
        def _(tensor):
            barrier(tensor, base)
            for i in range(NCH):
                tensor.wait_ge(sem["v"], base.v + i * V4 + 2)
                if i >= 2:
                    tensor.wait_ge(sem["s"], base.s + (i - 2) * S4 + 2)
                for k in range(KC):
                    mm = tensor.matmul(ps_s1[i % 2][0:1, :], s_onesb[:, 0:1],
                                       s_rb4[:, k * CT:(k + 1) * CT],
                                       start=(k == 0), stop=(k == KC - 1))
                mm.then_inc(sem["pe"], 1)
                tensor.wait_ge(sem["s"], base.s + i * S4 + 1)
                if i >= 2:
                    tensor.wait_ge(sem["v"], base.v + (i - 2) * V4 + 4)
                for k in range(KC):
                    mm = tensor.matmul(ps_s2[i % 2][0:1, :], s_onesb[:, 0:1],
                                       s_rb4[:, k * CT:(k + 1) * CT],
                                       start=(k == 0), stop=(k == KC - 1))
                mm.then_inc(sem["pe"], 1)
                tensor.wait_ge(sem["s"], base.s + i * S4 + 2)
                if i >= 2:
                    tensor.wait_ge(sem["v"], base.v + (i - 2) * V4 + 5)
                tensor.matmul(ps_mb[i % 2][:, :], s_ones1[0:1, :],
                              s_mu4[0:1, (i % 2) * CT:(i % 2 + 1) * CT],
                              start=True, stop=True).then_inc(sem["pe"], 1)
                tensor.wait_ge(sem["s"], base.s + i * S4 + 4)
                if i >= 2:
                    tensor.wait_ge(sem["v"], base.v + (i - 2) * V4 + 5)
                tensor.matmul(ps_rb[i % 2][:, :], s_ones1[0:1, :],
                              s_rs4[0:1, (i % 2) * CT:(i % 2 + 1) * CT],
                              start=True, stop=True).then_inc(sem["pe"], 1)
        ctx.pe += NCH * P4N

        @b.gpsimd
        def _(gpsimd):
            barrier(gpsimd, base)
            for i in range(NCH):
                gpsimd.wait_ge(sem["v"], base.v + (i + 1) * V4)
                gpsimd.dma_start(out=x2_h[:, i * CT:(i + 1) * CT].rearrange("(c p) t -> p c t", p=128),
                                 in_=s_x24[:].rearrange("p (c t) -> p c t", c=KC)).then_inc(sem["x2st"], 16)
        ctx.x2st += NCH
        sta.close()
    st4.close(); _open.remove(st4)

    if nphases < 7:
        with nc.Block("fin") as b:
            @b.gpsimd
            def _(gpsimd):
                barrier(gpsimd, ctx)
        for _st in reversed(_open):
            _st.close()
        stp.close()
        return nc

    st5 = ExitStack(); _open.append(st5)
    sb5 = lambda n_, s_, d_: st5.enter_context(nc.sbuf_tensor(n_, s_, d_))
    s_x2r = sb5("s_x2r", [128, KC * T], BF16)
    s_pkg = sb5("s_pkg", [128, 1024], U8)
    s_pku = sb5("s_pku", [128, 1024], U8)
    s_pkd = sb5("s_pkd", [128, 2048], U8)
    s_scg = sb5("s_scg", [128, 1024], BF16)
    s_scu = sb5("s_scu", [128, 1024], BF16)
    s_scd = sb5("s_scd", [128, 2048], BF16)
    s_dg = [sb5(f"s_dg{i}", [128, 2048], BF16) for i in range(2)]
    s_dup = [sb5(f"s_dup{i}", [128, 2048], BF16) for i in range(2)]
    s_dd = sb5("s_dd", [128, 4096], BF16)
    s_gel = sb5("s_gel", [128, 4 * CT], BF16)
    s_int = sb5("s_int", [128, T], BF16)
    s_j7 = sb5("s_j7", [128, 1024], BF16)
    s_sg5 = sb5("s_sg5", [128, 1024], BF16)
    s_t1 = sb5("s_t1", [128, 1024], BF16)
    s_t2 = sb5("s_t2", [128, 1024], BF16)
    s_i1 = sb5("s_i1", [128, 1024], U8)

    def vc(c):
        return 12 + (4 if c < 64 else 0)

    VOFF = [0]
    for c in range(CCH):
        VOFF.append(VOFF[-1] + vc(c))

    def gcnt(c):
        return 4 + (2 if c < 64 else 0)

    GOFF = [0]
    for c in range(CCH):
        GOFF.append(GOFF[-1] + gcnt(c))

    def stcnt(c):
        return 1 + (2 if c < 64 else 0)

    STOFF = [0]
    for c in range(CCH):
        STOFF.append(STOFF[-1] + stcnt(c))

    def decode_half(vector, pk_ap, shift, sc_ap, out_ap, width):
        j = s_i1[:, 0:width]
        if shift:
            vector.tensor_single_scalar(j, pk_ap, shift, OP.logical_shift_right)
            vector.tensor_single_scalar(j, j, 7, OP.bitwise_and)
        else:
            vector.tensor_single_scalar(j, pk_ap, 7, OP.bitwise_and)
        jb = s_j7[:, 0:width]
        vector.tensor_copy(jb, j)
        vector.tensor_single_scalar(j, pk_ap, shift + 3, OP.logical_shift_right)
        vector.tensor_single_scalar(j, j, 1, OP.bitwise_and)
        vector.tensor_scalar(s_sg5[:, 0:width], j, -2.0, 1.0, OP.mult, OP.add)
        vector.tensor_scalar_max(s_t1[:, 0:width], jb, 4.0)
        vector.tensor_scalar_max(s_t2[:, 0:width], jb, 6.0)
        vector.scalar_tensor_tensor(s_t1[:, 0:width], s_t1[:, 0:width], 0.5,
                                    s_t2[:, 0:width], OP.mult, OP.add)
        vector.scalar_tensor_tensor(s_t1[:, 0:width], jb, 0.5, s_t1[:, 0:width], OP.mult, OP.add)
        vector.scalar_tensor_tensor(s_t1[:, 0:width], s_t1[:, 0:width], -8.0,
                                    s_sg5[:, 0:width], OP.add, OP.mult)
        return vector.tensor_tensor(out_ap, s_t1[:, 0:width], sc_ap, OP.mult)

    base = Ctx(ctx)
    with nc.Block("p5") as b:
        sta = ExitStack()
        ps5 = [sta.enter_context(nc.psum_tensor(f"ps5_{i}", [128, CT], F32)) for i in range(8)]

        @b.sync
        def _(sync):
            barrier(sync, base)
            sync.dma_start(out=s_x2r[:].rearrange("p (c t) -> p c t", c=KC),
                           in_=x2_h[:, :].rearrange("(c p) t -> p c t", p=128)).then_inc(sem["x2r"], 16)
            for c in range(CCH):
                sync.wait_ge(sem["v"], base.v + VOFF[c] + vc(c))
                sync.dma_start(out=inter_h[c * 128:(c + 1) * 128, :],
                               in_=s_int[:, :]).then_inc(sem["intst"], 16)
                if c < 64:
                    sync.dma_start(out=ddec_h[c * 128:(c + 1) * 128, :],
                                   in_=s_dd[:, 0:2048]).then_inc(sem["ddst"], 16)
                    sync.dma_start(out=ddec_h[(64 + c) * 128:(64 + c + 1) * 128, :],
                                   in_=s_dd[:, 2048:4096]).then_inc(sem["ddst"], 16)
        ctx.x2r += 1
        ctx.intst += CCH
        ctx.ddst += 128

        @b.gpsimd
        def _(gpsimd):
            barrier(gpsimd, base)
            for c in range(CCH):
                if c >= 1:
                    gpsimd.wait_ge(sem["v"], base.v + VOFF[c - 1] + vc(c - 1) - 8)
                gpsimd.dma_start(out=s_pkg[:],
                                 in_=gate_pT[:, c * 128:(c + 1) * 128].rearrange(
                                     "(a p) n -> p a n", p=128)).then_inc(sem["p5g"], 16)
                gpsimd.dma_start(out=s_pku[:],
                                 in_=up_pT[:, c * 128:(c + 1) * 128].rearrange(
                                     "(a p) n -> p a n", p=128)).then_inc(sem["p5g"], 16)
                gpsimd.dma_start(
                    out=s_scg[:],
                    in_=scgr_h[:, c * 128:(c + 1) * 128].rearrange(
                        "(a p) n -> p a n", p=128)).then_inc(sem["p5g"], 16)
                gpsimd.dma_start(
                    out=s_scu[:],
                    in_=scur_h[:, c * 128:(c + 1) * 128].rearrange(
                        "(a p) n -> p a n", p=128)).then_inc(sem["p5g"], 16)
                if c < 64:
                    gpsimd.dma_start(out=s_pkd[:],
                                     in_=down_pT[c * 128:(c + 1) * 128, :]).then_inc(sem["p5g"], 16)
                    gpsimd.dma_start(out=s_scd[:],
                                     in_=scdr_h[c * 128:(c + 1) * 128, :]).then_inc(sem["p5g"], 16)
        ctx.p5g += GOFF[CCH]

        @b.vector
        def _(vector):
            barrier(vector, base)
            for c in range(CCH):
                vb = base.v + VOFF[c]
                vector.wait_ge(sem["p5g"], 16 * (base.p5g + GOFF[c] + gcnt(c)))
                if c >= 2:
                    vector.wait_ge(sem["pe"], base.pe + (c - 1) * 16)
                decode_half(vector, s_pkg[:, 0:1024], 0, s_scg[:, 0:1024],
                            s_dg[c % 2][:, 0:1024], 1024).then_inc(sem["v"], 1)
                decode_half(vector, s_pkg[:, 0:1024], 4, s_scg[:, 0:1024],
                            s_dg[c % 2][:, 1024:2048], 1024).then_inc(sem["v"], 1)
                decode_half(vector, s_pku[:, 0:1024], 0, s_scu[:, 0:1024],
                            s_dup[c % 2][:, 0:1024], 1024).then_inc(sem["v"], 1)
                decode_half(vector, s_pku[:, 0:1024], 4, s_scu[:, 0:1024],
                            s_dup[c % 2][:, 1024:2048], 1024).then_inc(sem["v"], 1)
                if c < 64:
                    if c >= 1:
                        vector.wait_ge(sem["ddst"], 16 * (base.ddst + 2 * c))
                    for half in range(2):
                        w0 = half * 1024
                        decode_half(vector, s_pkd[:, w0:w0 + 1024], 0, s_scd[:, w0:w0 + 1024],
                                    s_dd[:, w0:w0 + 1024], 1024).then_inc(sem["v"], 1)
                        decode_half(vector, s_pkd[:, w0:w0 + 1024], 4, s_scd[:, w0:w0 + 1024],
                                    s_dd[:, 2048 + w0:2048 + w0 + 1024], 1024).then_inc(sem["v"], 1)
                for t in range(NT):
                    h, tq = t // 4, t % 4
                    vector.wait_ge(sem["pe"], base.pe + c * 16 + h * 8 + 4 + tq + 1)
                    vector.wait_ge(sem["s"], base.s + c * 8 + h * 4 + tq + 1)
                    if c >= 1 and t == 0:
                        vector.wait_ge(sem["intst"], 16 * (base.intst + c))
                    vector.tensor_tensor(s_int[:, t * CT:(t + 1) * CT], ps5[4 + tq][:, :],
                                         s_gel[:, tq * CT:(tq + 1) * CT], OP.mult).then_inc(sem["v"], 1)
        ctx.v += VOFF[CCH]

        @b.scalar
        def _(scalar):
            barrier(scalar, base)
            for c in range(CCH):
                for t in range(NT):
                    h, tq = t // 4, t % 4
                    scalar.wait_ge(sem["pe"], base.pe + c * 16 + h * 8 + tq + 1)
                    if c * 8 + h * 4 + tq >= 4:
                        pidx = c * 8 + h * 4 + tq - 4
                        pc, prem = pidx // 8, pidx % 8
                        scalar.wait_ge(sem["v"], base.v + VOFF[pc] + vc(pc) - 8 + prem + 1)
                    scalar.activation(s_gel[:, tq * CT:(tq + 1) * CT], ps5[tq][:, :],
                                      AF.Gelu_apprx_tanh).then_inc(sem["s"], 1)
        ctx.s += CCH * 8

        @b.tensor
        def _(tensor):
            barrier(tensor, base)
            tensor.wait_ge(sem["x2r"], 16 * (base.x2r + 1))
            for c in range(CCH):
                vb = base.v + VOFF[c]
                tensor.wait_ge(sem["v"], vb + 2)
                for h in range(2):
                    for tq in range(4):
                        t = 4 * h + tq
                        gidx = c * 8 + h * 4 + tq
                        if gidx >= 4:
                            pidx = gidx - 4
                            pc, prem = pidx // 8, pidx % 8
                            tensor.wait_ge(sem["s"], base.s + pc * 8 + prem + 1)
                        for k in range(KC):
                            mm = tensor.matmul(ps5[tq][:, :], s_dg[c % 2][:, k * 128:(k + 1) * 128],
                                               s_x2r[:, k * T + t * CT:k * T + (t + 1) * CT],
                                               start=(k == 0), stop=(k == KC - 1))
                        mm.then_inc(sem["pe"], 1)
                    if h == 0:
                        tensor.wait_ge(sem["v"], vb + 4)
                    for tq in range(4):
                        t = 4 * h + tq
                        uidx = c * 8 + h * 4 + tq
                        if uidx >= 4:
                            pidx = uidx - 4
                            pc, prem = pidx // 8, pidx % 8
                            tensor.wait_ge(sem["v"], base.v + VOFF[pc] + vc(pc) - 8 + prem + 1)
                        for k in range(KC):
                            mm = tensor.matmul(ps5[4 + tq][:, :],
                                               s_dup[c % 2][:, k * 128:(k + 1) * 128],
                                               s_x2r[:, k * T + t * CT:k * T + (t + 1) * CT],
                                               start=(k == 0), stop=(k == KC - 1))
                        mm.then_inc(sem["pe"], 1)
        ctx.pe += CCH * 16
        sta.close()
    st5.close(); _open.remove(st5)

    if nphases < 8:
        with nc.Block("fin") as b:
            @b.gpsimd
            def _(gpsimd):
                barrier(gpsimd, ctx)
        for _st in reversed(_open):
            _st.close()
        stp.close()
        return nc

    st6 = ExitStack(); _open.append(st6)
    sb6 = lambda n_, s_, d_: st6.enter_context(nc.sbuf_tensor(n_, s_, d_))
    s_intr = [sb6(f"s_intr{i}", [128, 64 * CT], BF16) for i in range(2)]
    s_tmp6 = sb6("s_tmp6", [128, KC * CT], BF16)
    s_r6 = sb6("s_r6", [128, KC * CT], BF16)
    s_ddl = [sb6(f"s_ddl{i}", [128, 4 * 1024], BF16) for i in range(2)]
    s_mu6 = sb6("s_mu6", [1, CT], F32)
    s_rs6 = sb6("s_rs6", [1, CT], F32)
    s_st6 = sb6("s_st6", [128, 8], F32)
    s_rt = sb6("s_rt", [128, 2048], BF16)
    s_ot = [sb6(f"s_ot{i}", [128, 2048], BF16) for i in range(2)]

    base = Ctx(ctx)
    PE6, V6, S6, LD6 = 138, 55, 35, 67
    with nc.Block("p6") as b:
        sta = ExitStack()
        ps6 = [sta.enter_context(nc.psum_tensor(f"ps6_{i}", [128, CT], F32)) for i in range(8)]
        tp_alias = [ps6[3 + k][:, 0:64].bitcast(BF16) for k in range(4)]

        def pe6(tau, m):
            return base.pe + tau * PE6 + m

        def v6(tau, m):
            return base.v + tau * V6 + m

        def s6(tau, m):
            return base.s + tau * S6 + m

        @b.sync
        def _(sync):
            barrier(sync, base)
            for tau in range(NT):
                ldb = base.ld + tau * LD6
                if tau >= 1:
                    sync.wait_ge(sem["pe"], pe6(tau - 1, 66))
                sync.dma_start(out=s_tmp6[:].rearrange("p (c t) -> p c t", c=KC),
                               in_=x2_h[:, tau * CT:(tau + 1) * CT].rearrange(
                                   "(c p) t -> p c t", p=128)).then_inc(sem["x2t"], 16)
                if tau >= 1:
                    sync.wait_ge(sem["pe"], pe6(tau - 1, 48))
                sync.dma_start(out=s_intr[0][:].rearrange("p (c t) -> p c t", c=64),
                               in_=inter_h[0:8192, tau * CT:(tau + 1) * CT].rearrange(
                                   "(c p) t -> p c t", p=128)).then_inc(sem["inA"], 16)
                if tau >= 1:
                    sync.wait_ge(sem["pe"], pe6(tau - 1, 64))
                sync.dma_start(out=s_intr[1][:].rearrange("p (c t) -> p c t", c=64),
                               in_=inter_h[8192:NM, tau * CT:(tau + 1) * CT].rearrange(
                                   "(c p) t -> p c t", p=128)).then_inc(sem["inB"], 16)
                for grp in range(2):
                    for G in range(32):
                        gg = tau * 64 + grp * 32 + G
                        if gg >= 2:
                            pgg = gg - 2
                            pt, prem = pgg // 64, pgg % 64
                            sync.wait_ge(sem["pe"], pe6(pt, prem + 1))
                        sync.dma_start(
                            out=s_ddl[G % 2][:].rearrange("p (c n) -> p c n", c=4),
                            in_=ddec_h[G * 512:(G + 1) * 512,
                                       grp * 1024:(grp + 1) * 1024].rearrange(
                                "(c p) n -> p c n", p=128)
                        ).then_inc(sem["ddl0" if gg % 2 == 0 else "ddl1"], 16)
        ctx.x2t += NT
        ctx.inA += NT
        ctx.inB += NT
        ctx.ddl0 += NT * 32
        ctx.ddl1 += NT * 32

        @b.tensor
        def _(tensor):
            barrier(tensor, base)
            for tau in range(NT):
                ldb = base.ld + tau * LD6
                for grp in range(2):
                    if tau >= 1 and grp == 0:
                        tensor.wait_ge(sem["v"], v6(tau - 1, V6))
                        tensor.wait_ge(sem["s"], s6(tau - 1, S6))
                    if grp == 1:
                        tensor.wait_ge(sem["v"], v6(tau, 8))
                    tensor.wait_ge(sem["inA" if grp == 0 else "inB"],
                                   16 * (getattr(base, "inA" if grp == 0 else "inB") + tau + 1))
                    for G in range(32):
                        gg = tau * 64 + grp * 32 + G
                        dsm = "ddl0" if gg % 2 == 0 else "ddl1"
                        tensor.wait_ge(sem[dsm], 16 * (getattr(base, dsm) + gg // 2 + 1))
                        for cc in range(4):
                            c = G * 4 + cc
                            intr = s_intr[0] if c < 64 else s_intr[1]
                            for dc in range(8):
                                mm = tensor.matmul(
                                    ps6[dc][:, :],
                                    s_ddl[G % 2][:, cc * 1024 + dc * 128:cc * 1024 + (dc + 1) * 128],
                                    intr[:, (c % 64) * CT:(c % 64 + 1) * CT],
                                    start=(c == 0), stop=(c == CCH - 1))
                        mm.then_inc(sem["pe"], 1)
                tensor.wait_ge(sem["v"], v6(tau, 16))
                for k in range(KC):
                    mm = tensor.matmul(ps6[0][0:1, :], s_onesb[:, 0:1],
                                       s_r6[:, k * CT:(k + 1) * CT],
                                       start=(k == 0), stop=(k == KC - 1))
                mm.then_inc(sem["pe"], 1)
                tensor.wait_ge(sem["s"], s6(tau, 2))
                for k in range(KC):
                    mm = tensor.matmul(ps6[1][0:1, :], s_onesb[:, 0:1],
                                       s_tmp6[:, k * CT:(k + 1) * CT],
                                       start=(k == 0), stop=(k == KC - 1))
                mm.then_inc(sem["pe"], 1)
                tensor.wait_ge(sem["s"], s6(tau, 1))
                for j in range(4):
                    tensor.matmul(ps6[2][:, j:j + 1],
                                  s_mu6[0:1, j * 128:(j + 1) * 128], s_ones1[0:1, 0:1],
                                  start=True, stop=True).then_inc(sem["pe"], 1)
                tensor.wait_ge(sem["s"], s6(tau, 3))
                for j in range(4):
                    tensor.matmul(ps6[2][:, 4 + j:5 + j],
                                  s_rs6[0:1, j * 128:(j + 1) * 128], s_ones1[0:1, 0:1],
                                  start=True, stop=True).then_inc(sem["pe"], 1)
                cnt = 0
                for j in range(4):
                    for dc in range(16):
                        if cnt >= 4:
                            prev = cnt - 4
                            pj, pdc = prev // 16, prev % 16
                            if pdc % 2 == 0:
                                tensor.wait_ge(sem["v"], v6(tau, 19 + pj * 9 + pdc // 2 + 1))
                            else:
                                tensor.wait_ge(sem["s"], s6(tau, 3 + pj * 8 + (pdc - 1) // 2 + 1))
                        tensor.transpose(tp_alias[cnt % 4],
                                         s_r6[:, dc * CT + j * 128:dc * CT + (j + 1) * 128],
                                         s_identb[:, :]).then_inc(sem["pe"], 1)
                        cnt += 1
        ctx.pe += NT * PE6

        @b.vector
        def _(vector):
            barrier(vector, base)
            for tau in range(NT):
                vector.wait_ge(sem["x2t"], 16 * (base.x2t + tau + 1))
                if debug and tau >= 1:
                    vector.wait_ge(sem["dbgst"], 16 * (base.dbgst + 2 * tau))
                for grp in range(2):
                    vector.wait_ge(sem["pe"], pe6(tau, 32 * (grp + 1)))
                    for dc in range(8):
                        vector.tensor_tensor(
                            s_r6[:, (grp * 8 + dc) * CT:(grp * 8 + dc + 1) * CT],
                            ps6[dc][:, :],
                            s_tmp6[:, (grp * 8 + dc) * CT:(grp * 8 + dc + 1) * CT],
                            OP.add).then_inc(sem["v"], 1)
                vector.wait_ge(sem["s"], s6(tau, 1))
                vector.tensor_tensor(s_rs6[0:1, :], s_mu6[0:1, :], s_mu6[0:1, :],
                                     OP.mult).then_inc(sem["v"], 1)
                vector.wait_ge(sem["pe"], pe6(tau, 66))
                vector.scalar_tensor_tensor(s_rs6[0:1, :], ps6[1][0:1, :], 1.0 / 2048.0,
                                            s_rs6[0:1, :], OP.mult, OP.subtract
                                            ).then_inc(sem["v"], 1)
                vector.wait_ge(sem["pe"], pe6(tau, 74))
                vector.tensor_copy(s_st6[:, 0:8], ps6[2][:, 0:8]).then_inc(sem["v"], 1)
                for j in range(4):
                    for dc in range(0, 16, 2):
                        cnt = j * 16 + dc
                        vector.wait_ge(sem["pe"], pe6(tau, 74 + cnt + 1))
                        vector.tensor_copy(s_rt[:, dc * 128:(dc + 1) * 128],
                                           tp_alias[cnt % 4]).then_inc(sem["v"], 1)
                    vector.wait_ge(sem["s"], s6(tau, 3 + (j + 1) * 8))
                    gj = tau * 4 + j
                    if gj >= 2:
                        osm = "ost0" if gj % 2 == 0 else "ost1"
                        vector.wait_ge(sem[osm], 16 * (getattr(base, osm) + (gj - 2) // 2 + 1))
                    vector.tensor_scalar(s_ot[j % 2][:, :], s_rt[:, :],
                                         s_st6[:, j:j + 1], s_st6[:, 4 + j:5 + j],
                                         OP.subtract, OP.mult).then_inc(sem["v"], 1)
        ctx.v += NT * V6

        @b.scalar
        def _(scalar):
            barrier(scalar, base)
            for tau in range(NT):
                scalar.wait_ge(sem["pe"], pe6(tau, 65))
                scalar.activation(s_mu6[0:1, :], ps6[0][0:1, :], AF.Identity,
                                  scale=cst1(C_INV2048)).then_inc(sem["s"], 1)
                scalar.wait_ge(sem["v"], v6(tau, 16))
                scalar.activation(s_tmp6[:], s_r6[:], AF.Square).then_inc(sem["s"], 1)
                scalar.wait_ge(sem["v"], v6(tau, 18))
                scalar.activation(s_rs6[0:1, :], s_rs6[0:1, :], AF.Ln, bias=cst1(C_LNEPS))
                scalar.activation(s_rs6[0:1, :], s_rs6[0:1, :], AF.Exp,
                                  scale=cst1(C_NEGHALF)).then_inc(sem["s"], 1)
                for j in range(4):
                    for dc in range(1, 16, 2):
                        cnt = j * 16 + dc
                        if j >= 1 and dc == 1:
                            scalar.wait_ge(sem["v"], v6(tau, 28 + 9 * (j - 1)))
                        scalar.wait_ge(sem["pe"], pe6(tau, 74 + cnt + 1))
                        scalar.copy(s_rt[:, dc * 128:(dc + 1) * 128],
                                    tp_alias[cnt % 4]).then_inc(sem["s"], 1)
        ctx.s += NT * S6

        @b.gpsimd
        def _(gpsimd):
            barrier(gpsimd, base)
            for tau in range(NT):
                if debug:
                    gpsimd.wait_ge(sem["v"], v6(tau, 16))
                    gpsimd.dma_start(out=dbg_r6[:, tau * 8192:(tau + 1) * 8192],
                                     in_=s_r6[:]).then_inc(sem["dbgst"], 16)
                    gpsimd.wait_ge(sem["v"], v6(tau, 19))
                    gpsimd.dma_start(out=dbg_st6[:, tau * 8:(tau + 1) * 8],
                                     in_=s_st6[:, 0:8]).then_inc(sem["dbgst"], 16)
                for j in range(4):
                    gj = tau * 4 + j
                    gpsimd.wait_ge(sem["v"], v6(tau, 28 + 9 * j))
                    gpsimd.dma_start(out=out_dev[tau * CT + j * 128:tau * CT + (j + 1) * 128, :],
                                     in_=s_ot[j % 2][:, :]
                                     ).then_inc(sem["ost0" if gj % 2 == 0 else "ost1"], 16)
            gpsimd.wait_ge(sem["ost0"], 16 * (base.ost0 + NT * 2))
            gpsimd.wait_ge(sem["ost1"], 16 * (base.ost1 + NT * 2))
            if debug:
                gpsimd.wait_ge(sem["dbgst"], 16 * (base.dbgst + NT * 2))
        ctx.ost0 += NT * 2
        ctx.ost1 += NT * 2
        if debug:
            ctx.dbgst += NT * 2
        sta.close()
    st6.close(); _open.remove(st6)
    stp.close()
    return nc


def prep_shared(inp):
    perm = np.concatenate([np.arange(0, D, 2), np.arange(1, D, 2)])
    perm2 = np.concatenate([np.arange(0, NM, 2), np.arange(1, NM, 2)])
    W_in = np.asarray(inp["L0_attn_W_in"], np.float32)
    Wf = np.concatenate([
        W_in[0:4096],
        W_in[4096:8192],
        np.repeat(W_in[8192:8208], 16, axis=0),
        W_in[8208:8464],
        W_in[8464:8720],
    ], axis=0)
    d = {}
    d["w_inT"] = np.ascontiguousarray(Wf.T[perm])
    d["w_outT"] = np.ascontiguousarray(np.asarray(inp["L0_attn_W_out"], np.float32).T[:, perm])
    d["w_h2i"] = np.ascontiguousarray(np.repeat(np.asarray(inp["L0_attn_W_h2i"], np.float32).T, 16, axis=0))
    d["wnorm"] = np.ascontiguousarray(np.asarray(inp["L0_attn_w_norm"], np.float32)[perm].reshape(KC, 128).T)
    d["dtb"] = np.ascontiguousarray(np.repeat(np.asarray(inp["L0_attn_dt_bias"], np.float32), 16).reshape(2, 128).T)
    d["alog"] = np.ascontiguousarray(np.repeat(np.asarray(inp["L0_attn_A_log"], np.float32), 16).reshape(2, 128).T)
    d["wcv"] = np.ascontiguousarray(
        np.asarray(inp["L0_attn_w_conv"], np.float32).reshape(DI, 4).reshape(ICH, 128, 4)
        .transpose(1, 0, 2).reshape(128, 128))
    d["gate_pT"] = np.ascontiguousarray(np.asarray(inp["L0_ffn_gate_W_packed"], np.int32).T[:, perm2])
    d["up_pT"] = np.ascontiguousarray(np.asarray(inp["L0_ffn_up_W_packed"], np.int32).T[:, perm2])
    d["down_pT"] = np.ascontiguousarray(np.asarray(inp["L0_ffn_down_W_packed"], np.int32).T[:, perm])
    d["gate_scT"] = np.ascontiguousarray(np.asarray(inp["L0_ffn_gate_scales"], np.float32).T[:, perm2])
    d["up_scT"] = np.ascontiguousarray(np.asarray(inp["L0_ffn_up_scales"], np.float32).T[:, perm2])
    d["down_scT"] = np.ascontiguousarray(np.asarray(inp["L0_ffn_down_scales"], np.float32).T[:, perm])
    cstv = np.zeros(12, np.float32)
    cstv[C_ONE] = 1.0
    cstv[C_NEG1] = -1.0
    cstv[C_NEGHALF] = -0.5
    cstv[C_INV2048] = 1.0 / 2048.0
    cstv[C_RMSEPS] = RMS_EPS
    cstv[C_LNEPS] = LN_EPS
    cstv[C_7P5] = 7.5
    cstv[C_N8] = -8.0
    cstv[C_N10] = -10.0
    d["cst_in"] = np.tile(cstv, (128, 1))
    d["ones1"] = np.ones((1, 128), np.float32)
    d["onesb"] = np.ones((128, 1), ml_dtypes.bfloat16)
    ind = np.zeros((128, 8, 128), np.float32)
    for bblk in range(8):
        ind[:, bblk, 16 * bblk:16 * bblk + 16] = 1.0 / 256.0
    d["ind_in"] = ind.reshape(128, 1024).astype(ml_dtypes.bfloat16)
    d["identb_in"] = np.eye(128, dtype=ml_dtypes.bfloat16)
    d["identf_in"] = np.eye(128, dtype=np.float32)
    d["_perm"] = perm
    return d


def prep_core_x(inp, core, perm):
    x = np.asarray(inp["L0_attn_x"], np.float32)[core]
    return np.ascontiguousarray(x.T[perm])


def unpermute_out(res_out, perm):
    inv = np.empty_like(perm)
    inv[perm] = np.arange(D)
    return np.ascontiguousarray(np.asarray(res_out, np.float32)[:, inv])


def kernel(**inputs):
    from concourse.bass_utils import run_bass_kernel_spmd
    shared = prep_shared(inputs)
    perm = shared.pop("_perm")
    nc = build_nc(debug=False)
    in_maps = []
    for core in range(8):
        m = dict(shared)
        m["x_dev"] = prep_core_x(inputs, core, perm)
        in_maps.append(m)
    res = run_bass_kernel_spmd(nc, in_maps, core_ids=list(range(8)))
    out = np.stack([unpermute_out(r["out_dev"], perm) for r in res.results], axis=0)
    return out


# version 06
# speedup vs baseline: 1.2618x; 1.0346x over previous
import copy
import numpy as np
import ml_dtypes
import concourse.bass as bass
import concourse.mybir as mybir
from concourse.alu_op_type import AluOpType as OP
from contextlib import ExitStack

F32 = mybir.dt.float32
BF16 = mybir.dt.bfloat16
I32 = mybir.dt.int32
U8 = mybir.dt.uint8
AF = mybir.ActivationFunctionType

D = 2048
T = 4096
DI = 4096
NM = 16384
KC = D // 128
FCH = 70
ICH = DI // 128
CCH = NM // 128
CT = 512
NCH = T // CT
NT = T // CT
RMS_EPS = 1.1920929e-07
LN_EPS = 1e-5

C_ONE, C_NEG1, C_NEGHALF, C_INV2048, C_RMSEPS, C_LNEPS, C_7P5, C_N8, C_N10 = range(9)


class Ctx:
    DMA_SEMS = None

    def __init__(self, src=None):
        object.__setattr__(self, "c", dict(src.c) if src is not None else {})

    def __getattr__(self, k):
        if k == "c":
            raise AttributeError(k)
        return self.c.get(k, 0)

    def __setattr__(self, k, v):
        self.c[k] = v


OVERSYNC = False
P6LITE = 0

def build_nc(debug=False, nphases=8):
    nc = bass.Bass("TRN2", target_bir_lowering=False)

    x_dev = nc.dram_tensor("x_dev", [D, T], F32, kind="ExternalInput")
    w_inT = nc.dram_tensor("w_inT", [D, FCH * 128], F32, kind="ExternalInput")
    w_outT = nc.dram_tensor("w_outT", [DI, D], F32, kind="ExternalInput")
    w_h2i = nc.dram_tensor("w_h2i", [256, DI], F32, kind="ExternalInput")
    wnorm = nc.dram_tensor("wnorm", [128, KC], F32, kind="ExternalInput")
    dtb = nc.dram_tensor("dtb", [128, 2], F32, kind="ExternalInput")
    alog = nc.dram_tensor("alog", [128, 2], F32, kind="ExternalInput")
    wcv = nc.dram_tensor("wcv", [128, 128], F32, kind="ExternalInput")
    gate_pT = nc.dram_tensor("gate_pT", [1024, NM], I32, kind="ExternalInput")
    up_pT = nc.dram_tensor("up_pT", [1024, NM], I32, kind="ExternalInput")
    down_pT = nc.dram_tensor("down_pT", [8192, D], I32, kind="ExternalInput")
    gate_scT = nc.dram_tensor("gate_scT", [64, NM], F32, kind="ExternalInput")
    up_scT = nc.dram_tensor("up_scT", [64, NM], F32, kind="ExternalInput")
    down_scT = nc.dram_tensor("down_scT", [512, D], F32, kind="ExternalInput")
    cst_in = nc.dram_tensor("cst_in", [128, 12], F32, kind="ExternalInput")
    ones1 = nc.dram_tensor("ones1", [1, 128], F32, kind="ExternalInput")
    onesb = nc.dram_tensor("onesb", [128, 1], BF16, kind="ExternalInput")
    ind_in = nc.dram_tensor("ind_in", [128, 1024], BF16, kind="ExternalInput")
    identb_in = nc.dram_tensor("identb_in", [128, 128], BF16, kind="ExternalInput")
    identf_in = nc.dram_tensor("identf_in", [128, 128], F32, kind="ExternalInput")

    out_dev = nc.dram_tensor("out_dev", [T, D], F32, kind="ExternalOutput")

    spill = "ExternalOutput" if debug else "Internal"
    proj_h = nc.dram_tensor("proj_h", [FCH * 128, T], BF16, kind=spill)
    yg_h = nc.dram_tensor("yg_h", [DI, T], BF16, kind=spill)
    attn_h = nc.dram_tensor("attn_h", [D, T], F32, kind=spill)
    x2_h = nc.dram_tensor("x2_h", [D, T], BF16, kind=spill)
    inter_h = nc.dram_tensor("inter_h", [NM, T], BF16, kind=spill)
    ddec_h = nc.dram_tensor("ddec_h", [NM, D], BF16, kind=spill)
    if debug:
        dbg_dt = nc.dram_tensor("dbg_dt", [128, NCH * 1024], F32, kind="ExternalOutput")
        dbg_dA = nc.dram_tensor("dbg_dA", [128, NCH * 1024], F32, kind="ExternalOutput")
        dbg_U = nc.dram_tensor("dbg_U", [128, NCH * 1024], F32, kind="ExternalOutput")
        dbg_H = nc.dram_tensor("dbg_H", [128, NCH * 1024], F32, kind="ExternalOutput")
        dbg_Yt = nc.dram_tensor("dbg_Yt", [128, NCH * 1024], BF16, kind="ExternalOutput")
        dbg_r6 = nc.dram_tensor("dbg_r6", [128, NT * 8192], BF16, kind="ExternalOutput")
        dbg_st6 = nc.dram_tensor("dbg_st6", [128, NT * 8], F32, kind="ExternalOutput")
    scgr_h = nc.dram_tensor("scgr_h", [1024, NM], BF16, kind="Internal")
    scur_h = nc.dram_tensor("scur_h", [1024, NM], BF16, kind="Internal")
    scdr_h = nc.dram_tensor("scdr_h", [8192, D], BF16, kind="Internal")

    ctx = Ctx()
    _open = []
    stp = ExitStack()
    sbp = lambda n_, s_, d_: stp.enter_context(nc.sbuf_tensor(n_, s_, d_))

    s_cst = sbp("s_cst", [128, 12], F32)
    s_ones1 = sbp("s_ones1", [1, 128], F32)
    s_onesb = sbp("s_onesb", [128, 1], BF16)
    s_identb = sbp("s_identb", [128, 128], BF16)
    s_ind = sbp("s_ind", [128, 1024], BF16)
    s_identf = sbp("s_identf", [128, 128], F32)
    s_wn = sbp("s_wn", [128, KC], F32)
    s_dtb = sbp("s_dtb", [128, 2], F32)
    s_nA = sbp("s_nA", [128, 2], F32)
    s_wcv = sbp("s_wcv", [128, 128], F32)
    s_carry = sbp("s_carry", [128, 2], F32)

    DMA_SEMS = ["ld", "scexp", "x1l", "wc0", "wc1", "pst0", "pst1",
                "dbc0", "dbc1", "xil0", "xil1", "zl", "h2i", "ygst", "dbgst",
                "ygr", "wo0", "wo1", "ast0", "ast1", "p4l", "x2st",
                "p5g", "p5g0", "p5g1", "x2r", "intst", "ddst",
                "x2t", "inA", "inB", "ddl0", "ddl1", "ost0", "ost1"]
    ENG_SEMS = ["pe", "v", "s", "p"]
    Ctx.DMA_SEMS = set(DMA_SEMS)
    sem = {n: stp.enter_context(nc.semaphore("sem_" + n)) for n in DMA_SEMS + ENG_SEMS}

    def cst(i):
        return s_cst[:, i:i + 1]

    def cst1(i):
        return s_cst[0:1, i:i + 1]

    def barrier(eng, c):
        for n, v_ in c.c.items():
            if v_ > 0:
                eng.wait_ge(sem[n], 16 * v_ if n in Ctx.DMA_SEMS else v_)

    with nc.Block("p0") as b:
        @b.sync
        def _(sync):
            for dst, src in [(s_cst, cst_in), (s_ones1, ones1), (s_onesb, onesb),
                             (s_identb, identb_in), (s_identf, identf_in), (s_ind, ind_in),
                             (s_wn, wnorm), (s_dtb, dtb), (s_nA, alog), (s_wcv, wcv)]:
                sync.dma_start(out=dst[:], in_=src[:, :]).then_inc(sem["ld"], 16)
        ctx.ld += 10

        @b.scalar
        def _(scalar):
            scalar.wait_ge(sem["ld"], 16 * ctx.ld)
            scalar.activation(s_nA[:], s_nA[:], AF.Exp).then_inc(sem["s"], 1)
        ctx.s += 1

        @b.vector
        def _(vector):
            vector.wait_ge(sem["s"], ctx.s)
            vector.tensor_single_scalar(s_nA[:], s_nA[:], -1.0, OP.mult).then_inc(sem["v"], 1)
        ctx.v += 1


    if nphases < 2:
        with nc.Block("fin") as b:
            @b.gpsimd
            def _(gpsimd):
                barrier(gpsimd, ctx)
        for _st in reversed(_open):
            _st.close()
        stp.close()
        return nc

    st1 = ExitStack(); _open.append(st1)
    sb1 = lambda n_, s_, d_: st1.enter_context(nc.sbuf_tensor(n_, s_, d_))
    s_xn = sb1("s_xn", [128, KC * T], BF16)
    s_shared = sb1("s_shared", [128, 8192], BF16)
    s_x1b = sb1("s_x1b", [128, KC * CT], BF16)
    s_rstd = sb1("s_rstd", [1, 2 * CT], F32)

    base = Ctx(ctx)
    with nc.Block("p1a") as b:
        sta = ExitStack()
        ps_ss = [sta.enter_context(nc.psum_tensor(f"ps_ss{i}", [1, CT], F32)) for i in range(2)]
        ps_b = [sta.enter_context(nc.psum_tensor(f"ps_b{i}", [128, CT], F32)) for i in range(2)]

        @b.gpsimd
        def _(gpsimd):
            barrier(gpsimd, base)
            for i in range(NCH):
                if i >= 1:
                    gpsimd.wait_ge(sem["v"], base.v + 16 * i)
                gpsimd.dma_start(
                    out=s_x1b[:].rearrange("p (c t) -> p c t", c=KC),
                    in_=x_dev[:, i * CT:(i + 1) * CT].rearrange("(c p) t -> p c t", p=128),
                ).then_inc(sem["x1l"], 16)
        ctx.x1l += NCH

        @b.scalar
        def _(scalar):
            barrier(scalar, base)
            for i in range(NCH):
                scalar.wait_ge(sem["x1l"], 16 * (base.x1l + i + 1))
                if i >= 1:
                    scalar.wait_ge(sem["pe"], base.pe + 2 * i - 1)
                scalar.activation(s_shared[:], s_x1b[:], AF.Square).then_inc(sem["s"], 1)
                scalar.wait_ge(sem["pe"], base.pe + 2 * i + 1)
                sl = s_rstd[0:1, (i % 2) * CT:(i % 2 + 1) * CT]
                scalar.activation(sl, ps_ss[i % 2][0:1, :], AF.Ln,
                                  bias=cst1(C_RMSEPS), scale=cst1(C_INV2048)).then_inc(sem["s"], 1)
                scalar.activation(sl, sl, AF.Exp, scale=cst1(C_NEGHALF)).then_inc(sem["s"], 1)
        ctx.s += 3 * NCH

        @b.tensor
        def _(tensor):
            barrier(tensor, base)
            for i in range(NCH):
                tensor.wait_ge(sem["s"], base.s + 3 * i + 1)
                if i >= 2:
                    tensor.wait_ge(sem["s"], base.s + 3 * (i - 2) + 2)
                for k in range(KC):
                    mm = tensor.matmul(ps_ss[i % 2][0:1, :], s_onesb[:, 0:1],
                                       s_shared[:, k * CT:(k + 1) * CT],
                                       start=(k == 0), stop=(k == KC - 1))
                mm.then_inc(sem["pe"], 1)
                tensor.wait_ge(sem["s"], base.s + 3 * i + 3)
                if i >= 2:
                    tensor.wait_ge(sem["v"], base.v + 16 * (i - 1))
                tensor.matmul(ps_b[i % 2][:, :], s_ones1[0:1, :],
                              s_rstd[0:1, (i % 2) * CT:(i % 2 + 1) * CT],
                              start=True, stop=True).then_inc(sem["pe"], 1)
        ctx.pe += 2 * NCH

        @b.vector
        def _(vector):
            barrier(vector, base)
            for i in range(NCH):
                vector.wait_ge(sem["pe"], base.pe + 2 * i + 2)
                for k in range(KC):
                    vector.scalar_tensor_tensor(
                        s_xn[:, k * T + i * CT:k * T + (i + 1) * CT],
                        s_x1b[:, k * CT:(k + 1) * CT], s_wn[:, k:k + 1],
                        ps_b[i % 2][:, :], OP.mult, OP.mult).then_inc(sem["v"], 1)
        ctx.v += 16 * NCH
        sta.close()

    if nphases < 3:
        with nc.Block("fin") as b:
            @b.gpsimd
            def _(gpsimd):
                barrier(gpsimd, ctx)
        for _st in reversed(_open):
            _st.close()
        stp.close()
        return nc

    s_wcol = [sb1(f"s_wcol{i}", [128, KC * 128], BF16) for i in range(2)]
    base = Ctx(ctx)
    with nc.Block("p1b") as b:
        sta = ExitStack()
        ps1 = [sta.enter_context(nc.psum_tensor(f"ps1_{i}", [128, CT], F32)) for i in range(8)]

        @b.gpsimd
        def _(gpsimd):
            for f in range(FCH):
                if f >= 2:
                    gpsimd.wait_ge(sem["pe"], base.pe + (f - 1) * NT)
                gpsimd.dma_start(
                    out=s_wcol[f % 2][:].rearrange("p (c f) -> p c f", f=128),
                    in_=w_inT[:, f * 128:(f + 1) * 128].rearrange("(c p) f -> p c f", p=128),
                ).then_inc(sem["wc0" if f % 2 == 0 else "wc1"], 16)
        ctx.wc0 += (FCH + 1) // 2
        ctx.wc1 += FCH // 2

        @b.tensor
        def _(tensor):
            barrier(tensor, base)
            for f in range(FCH):
                wsem = "wc0" if f % 2 == 0 else "wc1"
                tensor.wait_ge(sem[wsem], 16 * (getattr(base, wsem) + f // 2 + 1))
                for t in range(NT):
                    if f >= 1:
                        if t % 2 == 0:
                            tensor.wait_ge(sem["v"], base.v + (f - 1) * 4 + t // 2 + 1)
                        else:
                            tensor.wait_ge(sem["s"], base.s + (f - 1) * 4 + (t - 1) // 2 + 1)
                    for k in range(KC):
                        mm = tensor.matmul(ps1[t][:, :],
                                           s_wcol[f % 2][:, k * 128:(k + 1) * 128],
                                           s_xn[:, k * T + t * CT:k * T + (t + 1) * CT],
                                           start=(k == 0), stop=(k == KC - 1))
                    mm.then_inc(sem["pe"], 1)
        ctx.pe += FCH * NT

        @b.vector
        def _(vector):
            barrier(vector, base)
            for f in range(FCH):
                if f >= 2:
                    psem = "pst0" if f % 2 == 0 else "pst1"
                    vector.wait_ge(sem[psem], 16 * (getattr(base, psem) + (f - 2) // 2 + 1))
                for t in range(0, NT, 2):
                    vector.wait_ge(sem["pe"], base.pe + f * NT + t + 1)
                    vector.tensor_copy(
                        s_shared[:, (f % 2) * 4096 + t * CT:(f % 2) * 4096 + (t + 1) * CT],
                        ps1[t][:, :]).then_inc(sem["v"], 1)
        ctx.v += FCH * 4

        @b.scalar
        def _(scalar):
            barrier(scalar, base)
            for f in range(FCH):
                if f >= 2:
                    psem = "pst0" if f % 2 == 0 else "pst1"
                    scalar.wait_ge(sem[psem], 16 * (getattr(base, psem) + (f - 2) // 2 + 1))
                for t in range(1, NT, 2):
                    scalar.wait_ge(sem["pe"], base.pe + f * NT + t + 1)
                    scalar.copy(
                        s_shared[:, (f % 2) * 4096 + t * CT:(f % 2) * 4096 + (t + 1) * CT],
                        ps1[t][:, :]).then_inc(sem["s"], 1)
        ctx.s += FCH * 4

        @b.sync
        def _(sync):
            barrier(sync, base)
            for f in range(FCH):
                sync.wait_ge(sem["v"], base.v + (f + 1) * 4)
                sync.wait_ge(sem["s"], base.s + (f + 1) * 4)
                sync.dma_start(out=proj_h[f * 128:(f + 1) * 128, :],
                               in_=s_shared[:, (f % 2) * 4096:(f % 2) * 4096 + 4096],
                               ).then_inc(sem["pst0" if f % 2 == 0 else "pst1"], 16)
        ctx.pst0 += (FCH + 1) // 2
        ctx.pst1 += FCH // 2
        sta.close()
    st1.close(); _open.remove(st1)

    if nphases < 4:
        with nc.Block("fin") as b:
            @b.gpsimd
            def _(gpsimd):
                barrier(gpsimd, ctx)
        for _st in reversed(_open):
            _st.close()
        stp.close()
        return nc

    st2 = ExitStack(); _open.append(st2)
    sb2 = lambda n_, s_, d_: st2.enter_context(nc.sbuf_tensor(n_, s_, d_))
    s_h2i = sb2("s_h2i", [128, 2 * DI], BF16)
    s_xr = [sb2(f"s_xr{i}", [128, ICH * 520], BF16) for i in range(2)]
    s_zr = sb2("s_zr", [128, ICH * CT], BF16)
    s_dbc = [sb2(f"s_dbc{i}", [128, 3072], BF16) for i in range(2)]
    s_dt = sb2("s_dt", [128, 1024], F32)
    s_dt2 = sb2("s_dt2", [128, 1024], F32)
    s_dt3 = sb2("s_dt3", [128, 1024], F32)
    s_dA = sb2("s_dA", [128, 1024], F32)
    s_du = sb2("s_du", [128, 1024], F32)
    s_U = sb2("s_U", [128, 1024], F32)
    s_H = sb2("s_H", [128, 1024], F32)
    s_Yt = sb2("s_Yt", [128, 1024], BF16)
    s_xc = sb2("s_xc", [128, 2 * CT], F32)
    s_sil = sb2("s_sil", [128, 4 * CT], BF16)
    s_silz = sb2("s_silz", [128, 2 * CT], BF16)
    s_yg = sb2("s_yg", [128, ICH * CT], BF16)

    VP, SP, PP = 72, 66, 64
    V_DTPRE, V_CONV0, V_DT, V_DTA, V_DUU, V_SCAN0, V_CARRY, V_YT, V_YG0 = \
        1, 2, 34, 35, 36, 37, 39, 40, 41
    S_SP, S_SIL0, S_DA, S_SILZ0 = 1, 2, 34, 35

    base = Ctx(ctx)
    with nc.Block("p2") as b:
        sta = ExitStack()
        ps_xs = [sta.enter_context(nc.psum_tensor(f"ps_xs{i}", [128, CT], F32)) for i in range(2)]
        ps_y = [sta.enter_context(nc.psum_tensor(f"ps_y{i}", [128, CT], F32)) for i in range(4)]

        projr = proj_h[:, :].rearrange("(c p) t -> p c t", p=128)

        def vat(i, m):
            return base.v + i * VP + m

        def sat(i, m):
            return base.s + i * SP + m

        def pat(i, m):
            return base.pe + i * PP + m

        @b.gpsimd
        def _(gpsimd):
            barrier(gpsimd, base)
            gpsimd.dma_start(out=s_h2i[:].rearrange("p (c f) -> p c f", f=DI),
                             in_=w_h2i[:, :].rearrange("(c p) f -> p c f", p=128)
                             ).then_inc(sem["h2i"], 16)
            gpsimd.dma_start(
                out=scgr_h[:, :].rearrange("(j r) n -> j r n", r=16),
                in_=gate_scT[:, :].unsqueeze(1).broadcast_to([64, 16, NM])).then_inc(sem["scexp"], 16)
            gpsimd.dma_start(
                out=scur_h[:, :].rearrange("(j r) n -> j r n", r=16),
                in_=up_scT[:, :].unsqueeze(1).broadcast_to([64, 16, NM])).then_inc(sem["scexp"], 16)
            gpsimd.dma_start(
                out=scdr_h[:, :].rearrange("(j r) n -> j r n", r=16),
                in_=down_scT[:, :].unsqueeze(1).broadcast_to([512, 16, D])).then_inc(sem["scexp"], 16)
            for i in range(NCH):
                gpsimd.wait_ge(sem["v"], vat(i, VP))
                gpsimd.dma_start(
                    out=yg_h[:, i * CT:(i + 1) * CT].rearrange("(c p) t -> p c t", p=128),
                    in_=s_yg[:].rearrange("p (c t) -> p c t", c=ICH)).then_inc(sem["ygst"], 16)
                if debug:
                    for dbg, tile in [(dbg_dt, s_dt), (dbg_dA, s_dA), (dbg_U, s_U), (dbg_H, s_H),
                                      (dbg_Yt, s_Yt)]:
                        gpsimd.dma_start(out=dbg[:, i * 1024:(i + 1) * 1024],
                                         in_=tile[:]).then_inc(sem["dbgst"], 16)
        ctx.h2i += 1
        ctx.scexp += 3
        ctx.ygst += NCH
        if debug:
            ctx.dbgst += NCH * 5

        @b.sync
        def _(sync):
            barrier(sync, base)
            for i in range(NCH):
                ldb = base.ld + i * 3
                if i >= 2:
                    sync.wait_ge(sem["v"], vat(i - 2, V_YT))
                sync.dma_start(out=s_dbc[i % 2][:].rearrange("p (r c t) -> p r c t", r=3, c=2),
                               in_=projr[:, 64:70, i * CT:(i + 1) * CT].rearrange(
                                   "p (r c) t -> p r c t", r=3)
                               ).then_inc(sem["dbc0" if i % 2 == 0 else "dbc1"], 16)
                t0 = i * CT
                xrb = s_xr[i % 2][:].rearrange("p (g t) -> p g t", g=ICH)
                if i >= 2:
                    sync.wait_ge(sem["v"], vat(i - 2, V_CONV0 + ICH - 1))
                xsm = "xil0" if i % 2 == 0 else "xil1"
                if t0 == 0:
                    sync.dma_start(out=xrb[:, :, 3:515],
                                   in_=projr[:, 32:64, 0:CT]).then_inc(sem[xsm], 16)
                else:
                    sync.dma_start(out=xrb[:, :, 0:515],
                                   in_=projr[:, 32:64, t0 - 3:t0 + CT]).then_inc(sem[xsm], 16)
                if i >= 1:
                    sync.wait_ge(sem["s"], sat(i - 1, S_SILZ0 + ICH - 1))
                sync.dma_start(out=s_zr[:].rearrange("p (g t) -> p g t", g=ICH),
                               in_=projr[:, 0:32, i * CT:(i + 1) * CT]).then_inc(sem["zl"], 16)
        ctx.dbc0 += (NCH + 1) // 2
        ctx.dbc1 += NCH // 2
        ctx.xil0 += (NCH + 1) // 2
        ctx.xil1 += NCH // 2
        ctx.zl += NCH

        @b.vector
        def _(vector):
            barrier(vector, base)
            for g in range(ICH):
                vector.memset(s_xr[0][:, g * 520:g * 520 + 3], 0.0)
            for i in range(NCH):
                ldb = base.ld + i * 3
                dsm = "dbc0" if i % 2 == 0 else "dbc1"
                vector.wait_ge(sem[dsm], 16 * (getattr(base, dsm) + i // 2 + 1))
                if debug and i > 0:
                    vector.wait_ge(sem["dbgst"], 16 * (base.dbgst + i * 5))
                for h in range(2):
                    op = vector.tensor_scalar(
                        s_dt[:, h * CT:(h + 1) * CT],
                        s_dbc[i % 2][:, h * CT:(h + 1) * CT], s_dtb[:, h:h + 1], None, OP.add)
                op.then_inc(sem["v"], 1)
                xsm = "xil0" if i % 2 == 0 else "xil1"
                vector.wait_ge(sem[xsm], 16 * (getattr(base, xsm) + i // 2 + 1))
                for g in range(ICH):
                    gi = i * ICH + g
                    if gi >= 2:
                        pi, pg = (gi - 2) // ICH, (gi - 2) % ICH
                        vector.wait_ge(sem["s"], sat(pi, S_SIL0 + pg))
                    sl = s_xr[i % 2][:, g * 520:g * 520 + 515]
                    xcs = s_xc[:, (g % 2) * CT:(g % 2 + 1) * CT]
                    w4 = lambda k: s_wcv[:, 4 * g + k:4 * g + k + 1]
                    vector.tensor_scalar(xcs, sl[:, 0:CT], w4(0), None, OP.mult)
                    vector.scalar_tensor_tensor(xcs, sl[:, 1:1 + CT], w4(1), xcs, OP.mult, OP.add)
                    vector.scalar_tensor_tensor(xcs, sl[:, 2:2 + CT], w4(2), xcs, OP.mult, OP.add)
                    vector.scalar_tensor_tensor(xcs, sl[:, 3:3 + CT], w4(3), xcs, OP.mult, OP.add
                                                ).then_inc(sem["v"], 1)
                vector.wait_ge(sem["s"], sat(i, S_SP))
                vector.tensor_tensor(s_dt[:], s_dt2[:], s_dt3[:], OP.add).then_inc(sem["v"], 1)
                for h in range(2):
                    op = vector.tensor_scalar(s_dt2[:, h * CT:(h + 1) * CT],
                                              s_dt[:, h * CT:(h + 1) * CT],
                                              s_nA[:, h:h + 1], None, OP.mult)
                op.then_inc(sem["v"], 1)
                for h in range(2):
                    vector.wait_ge(sem["pe"], pat(i, 16 * (h + 1)))
                    vector.tensor_tensor(s_du[:, h * CT:(h + 1) * CT],
                                         s_dt[:, h * CT:(h + 1) * CT], ps_xs[h][:, :], OP.mult)
                    op = vector.tensor_tensor(s_U[:, h * CT:(h + 1) * CT],
                                              s_du[:, h * CT:(h + 1) * CT],
                                              s_dbc[i % 2][:, 1024 + h * CT:1024 + (h + 1) * CT],
                                              OP.mult)
                op.then_inc(sem["v"], 1)
                vector.wait_ge(sem["s"], sat(i, S_DA))
                for h in range(2):
                    init = 0.0 if i == 0 else s_carry[:, h:h + 1]
                    vector.tensor_tensor_scan(s_H[:, h * CT:(h + 1) * CT],
                                              s_dA[:, h * CT:(h + 1) * CT],
                                              s_U[:, h * CT:(h + 1) * CT],
                                              init, OP.mult, OP.add).then_inc(sem["v"], 1)
                for h in range(2):
                    op = vector.tensor_copy(s_carry[:, h:h + 1],
                                            s_H[:, h * CT + CT - 1:h * CT + CT])
                op.then_inc(sem["v"], 1)
                for h in range(2):
                    op = vector.tensor_tensor(s_Yt[:, h * CT:(h + 1) * CT],
                                              s_H[:, h * CT:(h + 1) * CT],
                                              s_dbc[i % 2][:, 2048 + h * CT:2048 + (h + 1) * CT],
                                              OP.mult)
                op.then_inc(sem["v"], 1)
                for ic in range(ICH):
                    vector.wait_ge(sem["pe"], pat(i, 32 + ic + 1))
                    vector.wait_ge(sem["s"], sat(i, S_SILZ0 + ic) if not OVERSYNC else sat(i, SP))
                    if i > 0 and ic == 0:
                        vector.wait_ge(sem["ygst"], 16 * (base.ygst + i))
                    vector.tensor_tensor(s_yg[:, ic * CT:(ic + 1) * CT], ps_y[ic % 4][:, :],
                                         s_silz[:, (ic % 2) * CT:(ic % 2 + 1) * CT], OP.mult
                                         ).then_inc(sem["v"], 1)
        ctx.v += NCH * VP

        @b.scalar
        def _(scalar):
            barrier(scalar, base)
            for i in range(NCH):
                ldb = base.ld + i * 3
                scalar.wait_ge(sem["v"], vat(i, V_DTPRE))
                scalar.activation(s_dt2[:], s_dt[:], AF.Relu)
                scalar.activation(s_dt3[:], s_dt[:], AF.Abs)
                scalar.activation(s_dt3[:], s_dt3[:], AF.Exp, scale=cst(C_NEG1))
                scalar.activation(s_dt3[:], s_dt3[:], AF.Ln, bias=cst(C_ONE)).then_inc(sem["s"], 1)
                for g in range(ICH):
                    scalar.wait_ge(sem["v"], vat(i, V_CONV0 + g))
                    gi = i * ICH + g
                    if gi >= 4:
                        pi, pg = (gi - 4) // ICH, (gi - 4) % ICH
                        scalar.wait_ge(sem["pe"], pat(pi, pg + 1))
                    scalar.activation(s_sil[:, (g % 4) * CT:(g % 4 + 1) * CT],
                                      s_xc[:, (g % 2) * CT:(g % 2 + 1) * CT],
                                      AF.Silu).then_inc(sem["s"], 1)
                scalar.wait_ge(sem["v"], vat(i, V_DTA))
                if i >= 1:
                    scalar.wait_ge(sem["v"], vat(i - 1, V_SCAN0 + 1))
                scalar.activation(s_dA[:], s_dt2[:], AF.Exp).then_inc(sem["s"], 1)
                scalar.wait_ge(sem["zl"], 16 * (base.zl + i + 1))
                for ic in range(ICH):
                    gi = i * ICH + ic
                    if gi >= 2:
                        pi, pg = (gi - 2) // ICH, (gi - 2) % ICH
                        scalar.wait_ge(sem["v"], vat(pi, V_YG0 + pg))
                    scalar.activation(s_silz[:, (ic % 2) * CT:(ic % 2 + 1) * CT],
                                      s_zr[:, ic * CT:(ic + 1) * CT],
                                      AF.Silu).then_inc(sem["s"], 1)
        ctx.s += NCH * SP

        @b.tensor
        def _(tensor):
            barrier(tensor, base)
            tensor.wait_ge(sem["h2i"], 16 * (base.h2i + 1))
            for i in range(NCH):
                for g in range(ICH):
                    tensor.wait_ge(sem["s"], sat(i, S_SIL0 + g))
                    if g in (0, 16) and i >= 1:
                        tensor.wait_ge(sem["v"], vat(i - 1, V_DUU))
                    mm = tensor.matmul(ps_xs[g // 16][:, :],
                                       s_ind[:, ((g // 2) % 8) * 128:((g // 2) % 8 + 1) * 128],
                                       s_sil[:, (g % 4) * CT:(g % 4 + 1) * CT],
                                       start=(g % 16 == 0), stop=(g % 16 == 15))
                    mm.then_inc(sem["pe"], 1)
                tensor.wait_ge(sem["v"], vat(i, V_YT))
                for ic in range(ICH):
                    gi = i * ICH + ic
                    if gi >= 4:
                        pi, pg = (gi - 4) // ICH, (gi - 4) % ICH
                        tensor.wait_ge(sem["v"], vat(pi, V_YG0 + pg))
                    for h in range(2):
                        mm = tensor.matmul(ps_y[ic % 4][:, :],
                                           s_h2i[:, h * DI + ic * 128:h * DI + (ic + 1) * 128],
                                           s_Yt[:, h * CT:(h + 1) * CT],
                                           start=(h == 0), stop=(h == 1))
                    mm.then_inc(sem["pe"], 1)
        ctx.pe += NCH * PP
        sta.close()
    st2.close(); _open.remove(st2)

    if nphases < 5:
        with nc.Block("fin") as b:
            @b.gpsimd
            def _(gpsimd):
                barrier(gpsimd, ctx)
        for _st in reversed(_open):
            _st.close()
        stp.close()
        return nc

    st3 = ExitStack(); _open.append(st3)
    sb3 = lambda n_, s_, d_: st3.enter_context(nc.sbuf_tensor(n_, s_, d_))
    s_ygr = sb3("s_ygr", [128, ICH * 2048], BF16)
    s_wo = [sb3(f"s_wo{i}", [128, ICH * 128], BF16) for i in range(2)]
    s_ev3 = [sb3(f"s_ev3{i}", [128, 4 * CT], F32) for i in range(2)]
    base = Ctx(ctx)
    HT = 2048
    with nc.Block("p3") as b:
        sta = ExitStack()
        ps3 = [sta.enter_context(nc.psum_tensor(f"ps3_{i}", [128, CT], F32)) for i in range(8)]

        @b.sync
        def _(sync):
            barrier(sync, base)
            for hf in range(2):
                if hf == 1:
                    sync.wait_ge(sem["pe"], base.pe + 16 * 4)
                sync.dma_start(out=s_ygr[:].rearrange("p (c t) -> p c t", c=ICH),
                               in_=yg_h[:, hf * HT:(hf + 1) * HT].rearrange("(c p) t -> p c t", p=128),
                               ).then_inc(sem["ygr"], 16)
                for dc in range(16):
                    j = hf * 16 + dc
                    sync.wait_ge(sem["v"], base.v + j * 2 + 2)
                    sync.wait_ge(sem["s"], base.s + j * 2 + 2)
                    sync.dma_start(out=attn_h[dc * 128:(dc + 1) * 128, hf * HT:(hf + 1) * HT],
                                   in_=s_ev3[j % 2][:, :]
                                   ).then_inc(sem["ast0" if j % 2 == 0 else "ast1"], 16)
        ctx.ygr += 2
        ctx.ast0 += 16
        ctx.ast1 += 16

        @b.gpsimd
        def _(gpsimd):
            barrier(gpsimd, base)
            for hf in range(2):
                for dc in range(16):
                    j = hf * 16 + dc
                    if j >= 2:
                        gpsimd.wait_ge(sem["pe"], base.pe + (j - 1) * 4)
                    gpsimd.dma_start(
                        out=s_wo[j % 2][:].rearrange("p (c f) -> p c f", f=128),
                        in_=w_outT[:, dc * 128:(dc + 1) * 128].rearrange("(c p) f -> p c f", p=128),
                    ).then_inc(sem["wo0" if j % 2 == 0 else "wo1"], 16)
        ctx.wo0 += 16
        ctx.wo1 += 16

        @b.tensor
        def _(tensor):
            barrier(tensor, base)
            for hf in range(2):
                tensor.wait_ge(sem["ygr"], 16 * (base.ygr + hf + 1))
                for dc in range(16):
                    j = hf * 16 + dc
                    wsm = "wo0" if j % 2 == 0 else "wo1"
                    tensor.wait_ge(sem[wsm], 16 * (getattr(base, wsm) + j // 2 + 1))
                    for t in range(4):
                        bank = (j % 2) * 4 + t
                        if j >= 2:
                            if t % 2 == 0:
                                tensor.wait_ge(sem["v"], base.v + (j - 2) * 2 + t // 2 + 1)
                            else:
                                tensor.wait_ge(sem["s"], base.s + (j - 2) * 2 + (t - 1) // 2 + 1)
                        for k in range(ICH):
                            mm = tensor.matmul(ps3[bank][:, :],
                                               s_wo[j % 2][:, k * 128:(k + 1) * 128],
                                               s_ygr[:, k * HT + t * CT:k * HT + (t + 1) * CT],
                                               start=(k == 0), stop=(k == ICH - 1))
                        mm.then_inc(sem["pe"], 1)
        ctx.pe += 32 * 4

        @b.vector
        def _(vector):
            barrier(vector, base)
            for j in range(32):
                if j >= 2:
                    asm_ = "ast0" if j % 2 == 0 else "ast1"
                    vector.wait_ge(sem[asm_], 16 * (getattr(base, asm_) + (j - 2) // 2 + 1))
                for t in range(0, 4, 2):
                    vector.wait_ge(sem["pe"], base.pe + j * 4 + t + 1)
                    vector.tensor_copy(s_ev3[j % 2][:, t * CT:(t + 1) * CT],
                                       ps3[(j % 2) * 4 + t][:, :]).then_inc(sem["v"], 1)
        ctx.v += 64

        @b.scalar
        def _(scalar):
            barrier(scalar, base)
            for j in range(32):
                if j >= 2:
                    asm_ = "ast0" if j % 2 == 0 else "ast1"
                    scalar.wait_ge(sem[asm_], 16 * (getattr(base, asm_) + (j - 2) // 2 + 1))
                for t in range(1, 4, 2):
                    scalar.wait_ge(sem["pe"], base.pe + j * 4 + t + 1)
                    scalar.copy(s_ev3[j % 2][:, t * CT:(t + 1) * CT],
                                ps3[(j % 2) * 4 + t][:, :]).then_inc(sem["s"], 1)
        ctx.s += 64
        sta.close()
    st3.close(); _open.remove(st3)

    if nphases < 6:
        with nc.Block("fin") as b:
            @b.gpsimd
            def _(gpsimd):
                barrier(gpsimd, ctx)
        for _st in reversed(_open):
            _st.close()
        stp.close()
        return nc

    st4 = ExitStack(); _open.append(st4)
    sb4 = lambda n_, s_, d_: st4.enter_context(nc.sbuf_tensor(n_, s_, d_))
    s_ap4 = sb4("s_ap4", [128, KC * CT], F32)
    s_x14 = sb4("s_x14", [128, KC * CT], F32)
    s_r4 = sb4("s_r4", [128, KC * CT], F32)
    s_rb4 = sb4("s_rb4", [128, KC * CT], BF16)
    s_x24 = sb4("s_x24", [128, KC * CT], BF16)
    s_mu4 = sb4("s_mu4", [1, 2 * CT], F32)
    s_rs4 = sb4("s_rs4", [1, 2 * CT], F32)
    base = Ctx(ctx)
    V4, S4, P4N = 5, 4, 4
    with nc.Block("p4") as b:
        sta = ExitStack()
        ps_s1 = [sta.enter_context(nc.psum_tensor(f"ps4a{i}", [1, CT], F32)) for i in range(2)]
        ps_s2 = [sta.enter_context(nc.psum_tensor(f"ps4b{i}", [1, CT], F32)) for i in range(2)]
        ps_mb = [sta.enter_context(nc.psum_tensor(f"ps4c{i}", [128, CT], F32)) for i in range(2)]
        ps_rb = [sta.enter_context(nc.psum_tensor(f"ps4d{i}", [128, CT], F32)) for i in range(2)]

        @b.sync
        def _(sync):
            barrier(sync, base)
            for i in range(NCH):
                if i >= 1:
                    sync.wait_ge(sem["v"], base.v + (i - 1) * V4 + 2)
                sync.dma_start(out=s_ap4[:].rearrange("p (c t) -> p c t", c=KC),
                               in_=attn_h[:, i * CT:(i + 1) * CT].rearrange("(c p) t -> p c t", p=128),
                               ).then_inc(sem["p4l"], 16)
                sync.dma_start(out=s_x14[:].rearrange("p (c t) -> p c t", c=KC),
                               in_=x_dev[:, i * CT:(i + 1) * CT].rearrange("(c p) t -> p c t", p=128),
                               ).then_inc(sem["p4l"], 16)
        ctx.p4l += 2 * NCH

        @b.vector
        def _(vector):
            barrier(vector, base)
            for i in range(NCH):
                vector.wait_ge(sem["p4l"], 16 * (base.p4l + 2 * i + 2))
                vector.tensor_tensor(s_r4[:], s_x14[:], s_ap4[:], OP.add).then_inc(sem["v"], 1)
                if i >= 1:
                    vector.wait_ge(sem["pe"], base.pe + (i - 1) * P4N + 2)
                vector.tensor_copy(s_rb4[:], s_r4[:]).then_inc(sem["v"], 1)
                mu = s_mu4[0:1, (i % 2) * CT:(i % 2 + 1) * CT]
                rs = s_rs4[0:1, (i % 2) * CT:(i % 2 + 1) * CT]
                vector.wait_ge(sem["s"], base.s + i * S4 + 2)
                vector.tensor_tensor(rs, mu, mu, OP.mult).then_inc(sem["v"], 1)
                vector.wait_ge(sem["pe"], base.pe + i * P4N + 2)
                vector.scalar_tensor_tensor(rs, ps_s2[i % 2][0:1, :], 1.0 / 2048.0, rs,
                                            OP.mult, OP.subtract).then_inc(sem["v"], 1)
                vector.wait_ge(sem["pe"], base.pe + i * P4N + 4)
                if i >= 1:
                    vector.wait_ge(sem["x2st"], 16 * (base.x2st + i))
                for k in range(KC):
                    vector.tensor_tensor(s_r4[:, k * CT:(k + 1) * CT], s_r4[:, k * CT:(k + 1) * CT],
                                         ps_mb[i % 2][:, :], OP.subtract)
                    op = vector.tensor_tensor(s_x24[:, k * CT:(k + 1) * CT],
                                              s_r4[:, k * CT:(k + 1) * CT],
                                              ps_rb[i % 2][:, :], OP.mult)
                op.then_inc(sem["v"], 1)
        ctx.v += NCH * V4

        @b.scalar
        def _(scalar):
            barrier(scalar, base)
            for i in range(NCH):
                scalar.wait_ge(sem["v"], base.v + i * V4 + 2)
                scalar.wait_ge(sem["pe"], base.pe + i * P4N + 1)
                scalar.activation(s_rb4[:], s_rb4[:], AF.Square).then_inc(sem["s"], 1)
                mu = s_mu4[0:1, (i % 2) * CT:(i % 2 + 1) * CT]
                rs = s_rs4[0:1, (i % 2) * CT:(i % 2 + 1) * CT]
                scalar.activation(mu, ps_s1[i % 2][0:1, :], AF.Identity,
                                  scale=cst1(C_INV2048)).then_inc(sem["s"], 1)
                scalar.wait_ge(sem["v"], base.v + i * V4 + 4)
                scalar.activation(rs, rs, AF.Ln, bias=cst1(C_LNEPS)).then_inc(sem["s"], 1)
                scalar.activation(rs, rs, AF.Exp, scale=cst1(C_NEGHALF)).then_inc(sem["s"], 1)
        ctx.s += NCH * S4

        @b.tensor
        def _(tensor):
            barrier(tensor, base)
            for i in range(NCH):
                tensor.wait_ge(sem["v"], base.v + i * V4 + 2)
                if i >= 2:
                    tensor.wait_ge(sem["s"], base.s + (i - 2) * S4 + 2)
                for k in range(KC):
                    mm = tensor.matmul(ps_s1[i % 2][0:1, :], s_onesb[:, 0:1],
                                       s_rb4[:, k * CT:(k + 1) * CT],
                                       start=(k == 0), stop=(k == KC - 1))
                mm.then_inc(sem["pe"], 1)
                tensor.wait_ge(sem["s"], base.s + i * S4 + 1)
                if i >= 2:
                    tensor.wait_ge(sem["v"], base.v + (i - 2) * V4 + 4)
                for k in range(KC):
                    mm = tensor.matmul(ps_s2[i % 2][0:1, :], s_onesb[:, 0:1],
                                       s_rb4[:, k * CT:(k + 1) * CT],
                                       start=(k == 0), stop=(k == KC - 1))
                mm.then_inc(sem["pe"], 1)
                tensor.wait_ge(sem["s"], base.s + i * S4 + 2)
                if i >= 2:
                    tensor.wait_ge(sem["v"], base.v + (i - 2) * V4 + 5)
                tensor.matmul(ps_mb[i % 2][:, :], s_ones1[0:1, :],
                              s_mu4[0:1, (i % 2) * CT:(i % 2 + 1) * CT],
                              start=True, stop=True).then_inc(sem["pe"], 1)
                tensor.wait_ge(sem["s"], base.s + i * S4 + 4)
                if i >= 2:
                    tensor.wait_ge(sem["v"], base.v + (i - 2) * V4 + 5)
                tensor.matmul(ps_rb[i % 2][:, :], s_ones1[0:1, :],
                              s_rs4[0:1, (i % 2) * CT:(i % 2 + 1) * CT],
                              start=True, stop=True).then_inc(sem["pe"], 1)
        ctx.pe += NCH * P4N

        @b.gpsimd
        def _(gpsimd):
            barrier(gpsimd, base)
            for i in range(NCH):
                gpsimd.wait_ge(sem["v"], base.v + (i + 1) * V4)
                gpsimd.dma_start(out=x2_h[:, i * CT:(i + 1) * CT].rearrange("(c p) t -> p c t", p=128),
                                 in_=s_x24[:].rearrange("p (c t) -> p c t", c=KC)).then_inc(sem["x2st"], 16)
        ctx.x2st += NCH
        sta.close()
    st4.close(); _open.remove(st4)

    if nphases < 7:
        with nc.Block("fin") as b:
            @b.gpsimd
            def _(gpsimd):
                barrier(gpsimd, ctx)
        for _st in reversed(_open):
            _st.close()
        stp.close()
        return nc

    st5 = ExitStack(); _open.append(st5)
    sb5 = lambda n_, s_, d_: st5.enter_context(nc.sbuf_tensor(n_, s_, d_))
    s_x2r = sb5("s_x2r", [128, KC * T], BF16)
    s_pkg = sb5("s_pkg", [128, 1024], U8)
    s_pku = sb5("s_pku", [128, 1024], U8)
    s_pkd = sb5("s_pkd", [128, 2048], U8)
    s_scg = sb5("s_scg", [128, 1024], BF16)
    s_scu = sb5("s_scu", [128, 1024], BF16)
    s_scd = sb5("s_scd", [128, 2048], BF16)
    s_dg = [sb5(f"s_dg{i}", [128, 2048], BF16) for i in range(2)]
    s_dup = [sb5(f"s_dup{i}", [128, 2048], BF16) for i in range(2)]
    s_dd = sb5("s_dd", [128, 4096], BF16)
    s_gel = sb5("s_gel", [128, 4 * CT], BF16)
    s_int = sb5("s_int", [128, T], BF16)
    s_j7 = sb5("s_j7", [128, 1024], BF16)
    s_sg5 = sb5("s_sg5", [128, 1024], BF16)
    s_t1 = sb5("s_t1", [128, 1024], BF16)
    s_t2 = sb5("s_t2", [128, 1024], BF16)
    s_i1 = sb5("s_i1", [128, 1024], U8)

    def it5(c):
        return (4 if c == 0 else 0) + (4 if c < CCH - 1 else 0) + (4 if c < 64 else 0) + 8

    VOFF2 = [0]
    for c in range(CCH):
        VOFF2.append(VOFF2[-1] + it5(c))

    def dpos(c):
        return VOFF2[c] + (4 if c == 0 else 0)

    def decg_done(c):
        return VOFF2[0] + 2 if c == 0 else dpos(c - 1) + 2

    def decu_done(c):
        return VOFF2[0] + 4 if c == 0 else dpos(c - 1) + 4

    def ddpos(c):
        return VOFF2[c] + (4 if c == 0 else 0) + (4 if c < CCH - 1 else 0)

    def ipos(c):
        return VOFF2[c] + it5(c) - 8

    def gcnt(c):
        return 4 + (2 if c < 64 else 0)

    GOFF = [0]
    for c in range(CCH):
        GOFF.append(GOFF[-1] + gcnt(c))

    def stcnt(c):
        return 1 + (2 if c < 64 else 0)

    STOFF = [0]
    for c in range(CCH):
        STOFF.append(STOFF[-1] + stcnt(c))

    def decode_half(vector, pk_ap, shift, sc_ap, out_ap, width):
        j = s_i1[:, 0:width]
        if shift:
            vector.tensor_single_scalar(j, pk_ap, shift, OP.logical_shift_right)
            vector.tensor_single_scalar(j, j, 7, OP.bitwise_and)
        else:
            vector.tensor_single_scalar(j, pk_ap, 7, OP.bitwise_and)
        jb = s_j7[:, 0:width]
        vector.tensor_copy(jb, j)
        vector.tensor_single_scalar(j, pk_ap, shift + 3, OP.logical_shift_right)
        vector.tensor_single_scalar(j, j, 1, OP.bitwise_and)
        vector.tensor_scalar(s_sg5[:, 0:width], j, -2.0, 1.0, OP.mult, OP.add)
        vector.tensor_scalar_max(s_t1[:, 0:width], jb, 4.0)
        vector.tensor_scalar_max(s_t2[:, 0:width], jb, 6.0)
        vector.scalar_tensor_tensor(s_t1[:, 0:width], s_t1[:, 0:width], 0.5,
                                    s_t2[:, 0:width], OP.mult, OP.add)
        vector.scalar_tensor_tensor(s_t1[:, 0:width], jb, 0.5, s_t1[:, 0:width], OP.mult, OP.add)
        vector.scalar_tensor_tensor(s_t1[:, 0:width], s_t1[:, 0:width], -8.0,
                                    s_sg5[:, 0:width], OP.add, OP.mult)
        return vector.tensor_tensor(out_ap, s_t1[:, 0:width], sc_ap, OP.mult)

    base = Ctx(ctx)
    with nc.Block("p5") as b:
        sta = ExitStack()
        ps5 = [sta.enter_context(nc.psum_tensor(f"ps5_{i}", [128, CT], F32)) for i in range(8)]

        @b.sync
        def _(sync):
            barrier(sync, base)
            sync.dma_start(out=s_x2r[:].rearrange("p (c t) -> p c t", c=KC),
                           in_=x2_h[:, :].rearrange("(c p) t -> p c t", p=128)).then_inc(sem["x2r"], 16)
            for c in range(CCH):
                if c < 64:
                    sync.wait_ge(sem["v"], base.v + ddpos(c) + 4)
                    sync.dma_start(out=ddec_h[c * 128:(c + 1) * 128, :],
                                   in_=s_dd[:, 0:2048]).then_inc(sem["ddst"], 16)
                    sync.dma_start(out=ddec_h[(64 + c) * 128:(64 + c + 1) * 128, :],
                                   in_=s_dd[:, 2048:4096]).then_inc(sem["ddst"], 16)
                sync.wait_ge(sem["v"], base.v + VOFF2[c] + it5(c))
                sync.dma_start(out=inter_h[c * 128:(c + 1) * 128, :],
                               in_=s_int[:, :]).then_inc(sem["intst"], 16)
        ctx.x2r += 1
        ctx.intst += CCH
        ctx.ddst += 128

        @b.gpsimd
        def _(gpsimd):
            barrier(gpsimd, base)
            for c in range(CCH):
                if c == 1:
                    gpsimd.wait_ge(sem["v"], base.v + VOFF2[0] + 4)
                elif c >= 2:
                    gpsimd.wait_ge(sem["v"], base.v + dpos(c - 2) + 4)
                gpsimd.dma_start(out=s_pkg[:],
                                 in_=gate_pT[:, c * 128:(c + 1) * 128].rearrange(
                                     "(a p) n -> p a n", p=128)).then_inc(sem["p5g"], 16)
                gpsimd.dma_start(out=s_pku[:],
                                 in_=up_pT[:, c * 128:(c + 1) * 128].rearrange(
                                     "(a p) n -> p a n", p=128)).then_inc(sem["p5g"], 16)
                gpsimd.dma_start(
                    out=s_scg[:],
                    in_=scgr_h[:, c * 128:(c + 1) * 128].rearrange(
                        "(a p) n -> p a n", p=128)).then_inc(sem["p5g"], 16)
                gpsimd.dma_start(
                    out=s_scu[:],
                    in_=scur_h[:, c * 128:(c + 1) * 128].rearrange(
                        "(a p) n -> p a n", p=128)).then_inc(sem["p5g"], 16)
                if c < 64:
                    if c >= 1:
                        gpsimd.wait_ge(sem["v"], base.v + ddpos(c - 1) + 4)
                    gpsimd.dma_start(out=s_pkd[:],
                                     in_=down_pT[c * 128:(c + 1) * 128, :]).then_inc(sem["p5g"], 16)
                    gpsimd.dma_start(out=s_scd[:],
                                     in_=scdr_h[c * 128:(c + 1) * 128, :]).then_inc(sem["p5g"], 16)
        ctx.p5g += GOFF[CCH]

        @b.vector
        def _(vector):
            barrier(vector, base)

            def dec_gu(cc):
                vector.wait_ge(sem["p5g"], 16 * (base.p5g + GOFF[cc] + 4))
                if cc >= 2:
                    vector.wait_ge(sem["pe"], base.pe + (cc - 1) * 16)
                decode_half(vector, s_pkg[:, 0:1024], 0, s_scg[:, 0:1024],
                            s_dg[cc % 2][:, 0:1024], 1024).then_inc(sem["v"], 1)
                decode_half(vector, s_pkg[:, 0:1024], 4, s_scg[:, 0:1024],
                            s_dg[cc % 2][:, 1024:2048], 1024).then_inc(sem["v"], 1)
                decode_half(vector, s_pku[:, 0:1024], 0, s_scu[:, 0:1024],
                            s_dup[cc % 2][:, 0:1024], 1024).then_inc(sem["v"], 1)
                decode_half(vector, s_pku[:, 0:1024], 4, s_scu[:, 0:1024],
                            s_dup[cc % 2][:, 1024:2048], 1024).then_inc(sem["v"], 1)

            for c in range(CCH):
                if c == 0:
                    dec_gu(0)
                if c < CCH - 1:
                    dec_gu(c + 1)
                if c < 64:
                    vector.wait_ge(sem["p5g"], 16 * (base.p5g + GOFF[c] + 6))
                    if c >= 1:
                        vector.wait_ge(sem["ddst"], 16 * (base.ddst + 2 * c))
                    for half in range(2):
                        w0 = half * 1024
                        decode_half(vector, s_pkd[:, w0:w0 + 1024], 0, s_scd[:, w0:w0 + 1024],
                                    s_dd[:, w0:w0 + 1024], 1024).then_inc(sem["v"], 1)
                        decode_half(vector, s_pkd[:, w0:w0 + 1024], 4, s_scd[:, w0:w0 + 1024],
                                    s_dd[:, 2048 + w0:2048 + w0 + 1024], 1024).then_inc(sem["v"], 1)
                for t in range(NT):
                    h, tq = t // 4, t % 4
                    vector.wait_ge(sem["pe"], base.pe + c * 16 + h * 8 + 4 + tq + 1)
                    vector.wait_ge(sem["s"], base.s + c * 8 + h * 4 + tq + 1)
                    if c >= 1 and t == 0:
                        vector.wait_ge(sem["intst"], 16 * (base.intst + c))
                    vector.tensor_tensor(s_int[:, t * CT:(t + 1) * CT], ps5[4 + tq][:, :],
                                         s_gel[:, tq * CT:(tq + 1) * CT], OP.mult).then_inc(sem["v"], 1)
        ctx.v += VOFF2[CCH]

        @b.scalar
        def _(scalar):
            barrier(scalar, base)
            for c in range(CCH):
                for t in range(NT):
                    h, tq = t // 4, t % 4
                    scalar.wait_ge(sem["pe"], base.pe + c * 16 + h * 8 + tq + 1)
                    if c * 8 + h * 4 + tq >= 4:
                        pidx = c * 8 + h * 4 + tq - 4
                        pc, prem = pidx // 8, pidx % 8
                        scalar.wait_ge(sem["v"], base.v + ipos(pc) + prem + 1)
                    scalar.activation(s_gel[:, tq * CT:(tq + 1) * CT], ps5[tq][:, :],
                                      AF.Gelu_apprx_tanh).then_inc(sem["s"], 1)
        ctx.s += CCH * 8

        @b.tensor
        def _(tensor):
            barrier(tensor, base)
            tensor.wait_ge(sem["x2r"], 16 * (base.x2r + 1))
            for c in range(CCH):
                tensor.wait_ge(sem["v"], base.v + decg_done(c))
                for h in range(2):
                    for tq in range(4):
                        t = 4 * h + tq
                        gidx = c * 8 + h * 4 + tq
                        if gidx >= 4:
                            pidx = gidx - 4
                            pc, prem = pidx // 8, pidx % 8
                            tensor.wait_ge(sem["s"], base.s + pc * 8 + prem + 1)
                        for k in range(KC):
                            mm = tensor.matmul(ps5[tq][:, :], s_dg[c % 2][:, k * 128:(k + 1) * 128],
                                               s_x2r[:, k * T + t * CT:k * T + (t + 1) * CT],
                                               start=(k == 0), stop=(k == KC - 1))
                        mm.then_inc(sem["pe"], 1)
                    if h == 0:
                        tensor.wait_ge(sem["v"], base.v + decu_done(c))
                    for tq in range(4):
                        t = 4 * h + tq
                        uidx = c * 8 + h * 4 + tq
                        if uidx >= 4:
                            pidx = uidx - 4
                            pc, prem = pidx // 8, pidx % 8
                            tensor.wait_ge(sem["v"], base.v + ipos(pc) + prem + 1)
                        for k in range(KC):
                            mm = tensor.matmul(ps5[4 + tq][:, :],
                                               s_dup[c % 2][:, k * 128:(k + 1) * 128],
                                               s_x2r[:, k * T + t * CT:k * T + (t + 1) * CT],
                                               start=(k == 0), stop=(k == KC - 1))
                        mm.then_inc(sem["pe"], 1)
        ctx.pe += CCH * 16
        sta.close()
    st5.close(); _open.remove(st5)

    if nphases < 8:
        with nc.Block("fin") as b:
            @b.gpsimd
            def _(gpsimd):
                barrier(gpsimd, ctx)
        for _st in reversed(_open):
            _st.close()
        stp.close()
        return nc

    st6 = ExitStack(); _open.append(st6)
    sb6 = lambda n_, s_, d_: st6.enter_context(nc.sbuf_tensor(n_, s_, d_))
    s_intr = [sb6(f"s_intr{i}", [128, 64 * CT], BF16) for i in range(2)]
    s_tmp6 = sb6("s_tmp6", [128, KC * CT], BF16)
    s_r6 = sb6("s_r6", [128, KC * CT], BF16)
    s_ddl = [sb6(f"s_ddl{i}", [128, 4 * 1024], BF16) for i in range(2)]
    s_mu6 = sb6("s_mu6", [1, CT], F32)
    s_rs6 = sb6("s_rs6", [1, CT], F32)
    s_st6 = sb6("s_st6", [128, 8], F32)
    s_rt = sb6("s_rt", [128, 2048], BF16)
    s_ot = [sb6(f"s_ot{i}", [128, 2048], BF16) for i in range(2)]

    base = Ctx(ctx)
    PE6, V6, S6, LD6 = 138, 55, 35, 67
    with nc.Block("p6") as b:
        sta = ExitStack()
        ps6 = [sta.enter_context(nc.psum_tensor(f"ps6_{i}", [128, CT], F32)) for i in range(8)]
        tp_alias = [ps6[3 + k][:, 0:64].bitcast(BF16) for k in range(4)]

        def pe6(tau, m):
            return base.pe + tau * PE6 + m

        def v6(tau, m):
            return base.v + tau * V6 + m

        def s6(tau, m):
            return base.s + tau * S6 + m

        @b.sync
        def _(sync):
            barrier(sync, base)
            for tau in range(NT):
                ldb = base.ld + tau * LD6
                if tau >= 1:
                    sync.wait_ge(sem["pe"], pe6(tau - 1, 66))
                sync.dma_start(out=s_tmp6[:].rearrange("p (c t) -> p c t", c=KC),
                               in_=x2_h[:, tau * CT:(tau + 1) * CT].rearrange(
                                   "(c p) t -> p c t", p=128)).then_inc(sem["x2t"], 16)
                if tau >= 1:
                    sync.wait_ge(sem["pe"], pe6(tau - 1, 48))
                sync.dma_start(out=s_intr[0][:].rearrange("p (c t) -> p c t", c=64),
                               in_=inter_h[0:8192, tau * CT:(tau + 1) * CT].rearrange(
                                   "(c p) t -> p c t", p=128)).then_inc(sem["inA"], 16)
                if tau >= 1:
                    sync.wait_ge(sem["pe"], pe6(tau - 1, 64))
                sync.dma_start(out=s_intr[1][:].rearrange("p (c t) -> p c t", c=64),
                               in_=inter_h[8192:NM, tau * CT:(tau + 1) * CT].rearrange(
                                   "(c p) t -> p c t", p=128)).then_inc(sem["inB"], 16)
                for grp in range(2):
                    for G in range(32):
                        gg = tau * 64 + grp * 32 + G
                        if gg >= 2:
                            pgg = gg - 2
                            pt, prem = pgg // 64, pgg % 64
                            sync.wait_ge(sem["pe"], pe6(pt, prem + 1))
                        sync.dma_start(
                            out=s_ddl[G % 2][:].rearrange("p (c n) -> p c n", c=4),
                            in_=ddec_h[G * 512:(G + 1) * 512,
                                       grp * 1024:(grp + 1) * 1024].rearrange(
                                "(c p) n -> p c n", p=128)
                        ).then_inc(sem["ddl0" if gg % 2 == 0 else "ddl1"], 16)
        ctx.x2t += NT
        ctx.inA += NT
        ctx.inB += NT
        ctx.ddl0 += NT * 32
        ctx.ddl1 += NT * 32

        @b.tensor
        def _(tensor):
            barrier(tensor, base)
            for tau in range(NT):
                ldb = base.ld + tau * LD6
                for grp in range(2):
                    if tau >= 1 and grp == 0:
                        tensor.wait_ge(sem["v"], v6(tau - 1, V6))
                        tensor.wait_ge(sem["s"], s6(tau - 1, S6))
                    if grp == 1:
                        tensor.wait_ge(sem["v"], v6(tau, 8))
                    tensor.wait_ge(sem["inA" if grp == 0 else "inB"],
                                   16 * (getattr(base, "inA" if grp == 0 else "inB") + tau + 1))
                    for G in range(32):
                        gg = tau * 64 + grp * 32 + G
                        dsm = "ddl0" if gg % 2 == 0 else "ddl1"
                        tensor.wait_ge(sem[dsm], 16 * (getattr(base, dsm) + gg // 2 + 1))
                        for cc in range(4):
                            c = G * 4 + cc
                            intr = s_intr[0] if c < 64 else s_intr[1]
                            for dc in range(8):
                                mm = tensor.matmul(
                                    ps6[dc][:, :],
                                    s_ddl[G % 2][:, cc * 1024 + dc * 128:cc * 1024 + (dc + 1) * 128],
                                    intr[:, (c % 64) * CT:(c % 64 + 1) * CT],
                                    start=(c == 0), stop=(c == CCH - 1))
                        mm.then_inc(sem["pe"], 1)
                tensor.wait_ge(sem["v"], v6(tau, 16))
                for k in range(KC):
                    mm = tensor.matmul(ps6[0][0:1, :], s_onesb[:, 0:1],
                                       s_r6[:, k * CT:(k + 1) * CT],
                                       start=(k == 0), stop=(k == KC - 1))
                mm.then_inc(sem["pe"], 1)
                tensor.wait_ge(sem["s"], s6(tau, 2))
                for k in range(KC):
                    mm = tensor.matmul(ps6[1][0:1, :], s_onesb[:, 0:1],
                                       s_tmp6[:, k * CT:(k + 1) * CT],
                                       start=(k == 0), stop=(k == KC - 1))
                mm.then_inc(sem["pe"], 1)
                tensor.wait_ge(sem["s"], s6(tau, 1))
                for j in range(4):
                    tensor.matmul(ps6[2][:, j:j + 1],
                                  s_mu6[0:1, j * 128:(j + 1) * 128], s_ones1[0:1, 0:1],
                                  start=True, stop=True).then_inc(sem["pe"], 1)
                tensor.wait_ge(sem["s"], s6(tau, 3))
                for j in range(4):
                    tensor.matmul(ps6[2][:, 4 + j:5 + j],
                                  s_rs6[0:1, j * 128:(j + 1) * 128], s_ones1[0:1, 0:1],
                                  start=True, stop=True).then_inc(sem["pe"], 1)
                cnt = 0
                for j in range(4):
                    for dc in range(16):
                        if cnt >= 4:
                            prev = cnt - 4
                            pj, pdc = prev // 16, prev % 16
                            if pdc % 2 == 0:
                                tensor.wait_ge(sem["v"], v6(tau, 19 + pj * 9 + pdc // 2 + 1))
                            else:
                                tensor.wait_ge(sem["s"], s6(tau, 3 + pj * 8 + (pdc - 1) // 2 + 1))
                        tensor.transpose(tp_alias[cnt % 4],
                                         s_r6[:, dc * CT + j * 128:dc * CT + (j + 1) * 128],
                                         s_identb[:, :]).then_inc(sem["pe"], 1)
                        cnt += 1
        ctx.pe += NT * PE6

        @b.vector
        def _(vector):
            barrier(vector, base)
            for tau in range(NT):
                vector.wait_ge(sem["x2t"], 16 * (base.x2t + tau + 1))
                if debug and tau >= 1:
                    vector.wait_ge(sem["dbgst"], 16 * (base.dbgst + 2 * tau))
                for grp in range(2):
                    vector.wait_ge(sem["pe"], pe6(tau, 32 * (grp + 1)))
                    for dc in range(8):
                        vector.tensor_tensor(
                            s_r6[:, (grp * 8 + dc) * CT:(grp * 8 + dc + 1) * CT],
                            ps6[dc][:, :],
                            s_tmp6[:, (grp * 8 + dc) * CT:(grp * 8 + dc + 1) * CT],
                            OP.add).then_inc(sem["v"], 1)
                vector.wait_ge(sem["s"], s6(tau, 1))
                vector.tensor_tensor(s_rs6[0:1, :], s_mu6[0:1, :], s_mu6[0:1, :],
                                     OP.mult).then_inc(sem["v"], 1)
                vector.wait_ge(sem["pe"], pe6(tau, 66))
                vector.scalar_tensor_tensor(s_rs6[0:1, :], ps6[1][0:1, :], 1.0 / 2048.0,
                                            s_rs6[0:1, :], OP.mult, OP.subtract
                                            ).then_inc(sem["v"], 1)
                vector.wait_ge(sem["pe"], pe6(tau, 74))
                vector.tensor_copy(s_st6[:, 0:8], ps6[2][:, 0:8]).then_inc(sem["v"], 1)
                for j in range(4):
                    for dc in range(0, 16, 2):
                        cnt = j * 16 + dc
                        vector.wait_ge(sem["pe"], pe6(tau, 74 + cnt + 1))
                        vector.tensor_copy(s_rt[:, dc * 128:(dc + 1) * 128],
                                           tp_alias[cnt % 4]).then_inc(sem["v"], 1)
                    vector.wait_ge(sem["s"], s6(tau, 3 + (j + 1) * 8))
                    gj = tau * 4 + j
                    if gj >= 2:
                        osm = "ost0" if gj % 2 == 0 else "ost1"
                        vector.wait_ge(sem[osm], 16 * (getattr(base, osm) + (gj - 2) // 2 + 1))
                    vector.tensor_scalar(s_ot[j % 2][:, :], s_rt[:, :],
                                         s_st6[:, j:j + 1], s_st6[:, 4 + j:5 + j],
                                         OP.subtract, OP.mult).then_inc(sem["v"], 1)
        ctx.v += NT * V6

        @b.scalar
        def _(scalar):
            barrier(scalar, base)
            for tau in range(NT):
                scalar.wait_ge(sem["pe"], pe6(tau, 65))
                scalar.activation(s_mu6[0:1, :], ps6[0][0:1, :], AF.Identity,
                                  scale=cst1(C_INV2048)).then_inc(sem["s"], 1)
                scalar.wait_ge(sem["v"], v6(tau, 16))
                scalar.activation(s_tmp6[:], s_r6[:], AF.Square).then_inc(sem["s"], 1)
                scalar.wait_ge(sem["v"], v6(tau, 18))
                scalar.activation(s_rs6[0:1, :], s_rs6[0:1, :], AF.Ln, bias=cst1(C_LNEPS))
                scalar.activation(s_rs6[0:1, :], s_rs6[0:1, :], AF.Exp,
                                  scale=cst1(C_NEGHALF)).then_inc(sem["s"], 1)
                for j in range(4):
                    for dc in range(1, 16, 2):
                        cnt = j * 16 + dc
                        if j >= 1 and dc == 1:
                            scalar.wait_ge(sem["v"], v6(tau, 28 + 9 * (j - 1)))
                        scalar.wait_ge(sem["pe"], pe6(tau, 74 + cnt + 1))
                        scalar.copy(s_rt[:, dc * 128:(dc + 1) * 128],
                                    tp_alias[cnt % 4]).then_inc(sem["s"], 1)
        ctx.s += NT * S6

        @b.gpsimd
        def _(gpsimd):
            barrier(gpsimd, base)
            for tau in range(NT):
                if debug:
                    gpsimd.wait_ge(sem["v"], v6(tau, 16))
                    gpsimd.dma_start(out=dbg_r6[:, tau * 8192:(tau + 1) * 8192],
                                     in_=s_r6[:]).then_inc(sem["dbgst"], 16)
                    gpsimd.wait_ge(sem["v"], v6(tau, 19))
                    gpsimd.dma_start(out=dbg_st6[:, tau * 8:(tau + 1) * 8],
                                     in_=s_st6[:, 0:8]).then_inc(sem["dbgst"], 16)
                for j in range(4):
                    gj = tau * 4 + j
                    gpsimd.wait_ge(sem["v"], v6(tau, 28 + 9 * j))
                    gpsimd.dma_start(out=out_dev[tau * CT + j * 128:tau * CT + (j + 1) * 128, :],
                                     in_=s_ot[j % 2][:, :]
                                     ).then_inc(sem["ost0" if gj % 2 == 0 else "ost1"], 16)
            gpsimd.wait_ge(sem["ost0"], 16 * (base.ost0 + NT * 2))
            gpsimd.wait_ge(sem["ost1"], 16 * (base.ost1 + NT * 2))
            if debug:
                gpsimd.wait_ge(sem["dbgst"], 16 * (base.dbgst + NT * 2))
        ctx.ost0 += NT * 2
        ctx.ost1 += NT * 2
        if debug:
            ctx.dbgst += NT * 2
        sta.close()
    st6.close(); _open.remove(st6)
    stp.close()
    return nc


def prep_shared(inp):
    perm = np.concatenate([np.arange(0, D, 2), np.arange(1, D, 2)])
    perm2 = np.concatenate([np.arange(0, NM, 2), np.arange(1, NM, 2)])
    W_in = np.asarray(inp["L0_attn_W_in"], np.float32)
    Wf = np.concatenate([
        W_in[0:4096],
        W_in[4096:8192],
        np.repeat(W_in[8192:8208], 16, axis=0),
        W_in[8208:8464],
        W_in[8464:8720],
    ], axis=0)
    d = {}
    d["w_inT"] = np.ascontiguousarray(Wf.T[perm])
    d["w_outT"] = np.ascontiguousarray(np.asarray(inp["L0_attn_W_out"], np.float32).T[:, perm])
    d["w_h2i"] = np.ascontiguousarray(np.repeat(np.asarray(inp["L0_attn_W_h2i"], np.float32).T, 16, axis=0))
    d["wnorm"] = np.ascontiguousarray(np.asarray(inp["L0_attn_w_norm"], np.float32)[perm].reshape(KC, 128).T)
    d["dtb"] = np.ascontiguousarray(np.repeat(np.asarray(inp["L0_attn_dt_bias"], np.float32), 16).reshape(2, 128).T)
    d["alog"] = np.ascontiguousarray(np.repeat(np.asarray(inp["L0_attn_A_log"], np.float32), 16).reshape(2, 128).T)
    d["wcv"] = np.ascontiguousarray(
        np.asarray(inp["L0_attn_w_conv"], np.float32).reshape(DI, 4).reshape(ICH, 128, 4)
        .transpose(1, 0, 2).reshape(128, 128))
    d["gate_pT"] = np.ascontiguousarray(np.asarray(inp["L0_ffn_gate_W_packed"], np.int32).T[:, perm2])
    d["up_pT"] = np.ascontiguousarray(np.asarray(inp["L0_ffn_up_W_packed"], np.int32).T[:, perm2])
    d["down_pT"] = np.ascontiguousarray(np.asarray(inp["L0_ffn_down_W_packed"], np.int32).T[:, perm])
    d["gate_scT"] = np.ascontiguousarray(np.asarray(inp["L0_ffn_gate_scales"], np.float32).T[:, perm2])
    d["up_scT"] = np.ascontiguousarray(np.asarray(inp["L0_ffn_up_scales"], np.float32).T[:, perm2])
    d["down_scT"] = np.ascontiguousarray(np.asarray(inp["L0_ffn_down_scales"], np.float32).T[:, perm])
    cstv = np.zeros(12, np.float32)
    cstv[C_ONE] = 1.0
    cstv[C_NEG1] = -1.0
    cstv[C_NEGHALF] = -0.5
    cstv[C_INV2048] = 1.0 / 2048.0
    cstv[C_RMSEPS] = RMS_EPS
    cstv[C_LNEPS] = LN_EPS
    cstv[C_7P5] = 7.5
    cstv[C_N8] = -8.0
    cstv[C_N10] = -10.0
    d["cst_in"] = np.tile(cstv, (128, 1))
    d["ones1"] = np.ones((1, 128), np.float32)
    d["onesb"] = np.ones((128, 1), ml_dtypes.bfloat16)
    ind = np.zeros((128, 8, 128), np.float32)
    for bblk in range(8):
        ind[:, bblk, 16 * bblk:16 * bblk + 16] = 1.0 / 256.0
    d["ind_in"] = ind.reshape(128, 1024).astype(ml_dtypes.bfloat16)
    d["identb_in"] = np.eye(128, dtype=ml_dtypes.bfloat16)
    d["identf_in"] = np.eye(128, dtype=np.float32)
    d["_perm"] = perm
    return d


def prep_core_x(inp, core, perm):
    x = np.asarray(inp["L0_attn_x"], np.float32)[core]
    return np.ascontiguousarray(x.T[perm])


def unpermute_out(res_out, perm):
    inv = np.empty_like(perm)
    inv[perm] = np.arange(D)
    return np.ascontiguousarray(np.asarray(res_out, np.float32)[:, inv])


def kernel(**inputs):
    from concourse.bass_utils import run_bass_kernel_spmd
    shared = prep_shared(inputs)
    perm = shared.pop("_perm")
    nc = build_nc(debug=False)
    in_maps = []
    for core in range(8):
        m = dict(shared)
        m["x_dev"] = prep_core_x(inputs, core, perm)
        in_maps.append(m)
    res = run_bass_kernel_spmd(nc, in_maps, core_ids=list(range(8)))
    out = np.stack([unpermute_out(r["out_dev"], perm) for r in res.results], axis=0)
    return out


# version 07 (final)
# speedup vs baseline: 1.2993x; 1.0297x over previous
import copy
import numpy as np
import ml_dtypes
import concourse.bass as bass
import concourse.mybir as mybir
from concourse.alu_op_type import AluOpType as OP
from contextlib import ExitStack

F32 = mybir.dt.float32
BF16 = mybir.dt.bfloat16
I32 = mybir.dt.int32
U8 = mybir.dt.uint8
AF = mybir.ActivationFunctionType

D = 2048
T = 4096
DI = 4096
NM = 16384
KC = D // 128
FCH = 70
ICH = DI // 128
CCH = NM // 128
CT = 512
NCH = T // CT
NT = T // CT
RMS_EPS = 1.1920929e-07
LN_EPS = 1e-5

C_ONE, C_NEG1, C_NEGHALF, C_INV2048, C_RMSEPS, C_LNEPS, C_7P5, C_N8, C_N10 = range(9)


class Ctx:
    DMA_SEMS = None

    def __init__(self, src=None):
        object.__setattr__(self, "c", dict(src.c) if src is not None else {})

    def __getattr__(self, k):
        if k == "c":
            raise AttributeError(k)
        return self.c.get(k, 0)

    def __setattr__(self, k, v):
        self.c[k] = v


OVERSYNC = False
P6LITE = 0

def build_nc(debug=False, nphases=8):
    nc = bass.Bass("TRN2", target_bir_lowering=False)

    x_dev = nc.dram_tensor("x_dev", [D, T], F32, kind="ExternalInput")
    w_inT = nc.dram_tensor("w_inT", [D, FCH * 128], F32, kind="ExternalInput")
    w_outT = nc.dram_tensor("w_outT", [DI, D], F32, kind="ExternalInput")
    w_h2i = nc.dram_tensor("w_h2i", [256, DI], F32, kind="ExternalInput")
    wnorm = nc.dram_tensor("wnorm", [128, KC], F32, kind="ExternalInput")
    dtb = nc.dram_tensor("dtb", [128, 2], F32, kind="ExternalInput")
    alog = nc.dram_tensor("alog", [128, 2], F32, kind="ExternalInput")
    wcv = nc.dram_tensor("wcv", [128, 128], F32, kind="ExternalInput")
    gate_pT = nc.dram_tensor("gate_pT", [1024, NM], I32, kind="ExternalInput")
    up_pT = nc.dram_tensor("up_pT", [1024, NM], I32, kind="ExternalInput")
    down_pT = nc.dram_tensor("down_pT", [8192, D], I32, kind="ExternalInput")
    gate_scT = nc.dram_tensor("gate_scT", [64, NM], F32, kind="ExternalInput")
    up_scT = nc.dram_tensor("up_scT", [64, NM], F32, kind="ExternalInput")
    down_scT = nc.dram_tensor("down_scT", [512, D], F32, kind="ExternalInput")
    cst_in = nc.dram_tensor("cst_in", [128, 12], F32, kind="ExternalInput")
    ones1 = nc.dram_tensor("ones1", [1, 128], F32, kind="ExternalInput")
    onesb = nc.dram_tensor("onesb", [128, 1], BF16, kind="ExternalInput")
    ind_in = nc.dram_tensor("ind_in", [128, 1024], BF16, kind="ExternalInput")
    identb_in = nc.dram_tensor("identb_in", [128, 128], BF16, kind="ExternalInput")
    identf_in = nc.dram_tensor("identf_in", [128, 128], F32, kind="ExternalInput")

    out_dev = nc.dram_tensor("out_dev", [T, D], F32, kind="ExternalOutput")

    spill = "ExternalOutput" if debug else "Internal"
    proj_h = nc.dram_tensor("proj_h", [FCH * 128, T], BF16, kind=spill)
    yg_h = nc.dram_tensor("yg_h", [DI, T], BF16, kind=spill)
    attn_h = nc.dram_tensor("attn_h", [D, T], F32, kind=spill)
    x2_h = nc.dram_tensor("x2_h", [D, T], BF16, kind=spill)
    inter_h = nc.dram_tensor("inter_h", [NM, T], BF16, kind=spill)
    ddec_h = nc.dram_tensor("ddec_h", [NM, D], BF16, kind=spill)
    if debug:
        dbg_dt = nc.dram_tensor("dbg_dt", [128, NCH * 1024], F32, kind="ExternalOutput")
        dbg_dA = nc.dram_tensor("dbg_dA", [128, NCH * 1024], F32, kind="ExternalOutput")
        dbg_U = nc.dram_tensor("dbg_U", [128, NCH * 1024], F32, kind="ExternalOutput")
        dbg_H = nc.dram_tensor("dbg_H", [128, NCH * 1024], F32, kind="ExternalOutput")
        dbg_Yt = nc.dram_tensor("dbg_Yt", [128, NCH * 1024], BF16, kind="ExternalOutput")
        dbg_r6 = nc.dram_tensor("dbg_r6", [128, NT * 8192], BF16, kind="ExternalOutput")
        dbg_st6 = nc.dram_tensor("dbg_st6", [128, NT * 8], F32, kind="ExternalOutput")
    scgr_h = nc.dram_tensor("scgr_h", [1024, NM], BF16, kind="Internal")
    scur_h = nc.dram_tensor("scur_h", [1024, NM], BF16, kind="Internal")
    scdr_h = nc.dram_tensor("scdr_h", [8192, D], BF16, kind="Internal")

    ctx = Ctx()
    _open = []
    stp = ExitStack()
    sbp = lambda n_, s_, d_: stp.enter_context(nc.sbuf_tensor(n_, s_, d_))

    s_cst = sbp("s_cst", [128, 12], F32)
    s_ones1 = sbp("s_ones1", [1, 128], F32)
    s_onesb = sbp("s_onesb", [128, 1], BF16)
    s_identb = sbp("s_identb", [128, 128], BF16)
    s_ind = sbp("s_ind", [128, 1024], BF16)
    s_identf = sbp("s_identf", [128, 128], F32)
    s_wn = sbp("s_wn", [128, KC], F32)
    s_dtb = sbp("s_dtb", [128, 2], F32)
    s_nA = sbp("s_nA", [128, 2], F32)
    s_wcv = sbp("s_wcv", [128, 128], F32)
    s_carry = sbp("s_carry", [128, 2], F32)

    DMA_SEMS = ["ld", "scexp", "x1l", "wc0", "wc1", "pst0", "pst1",
                "dbc0", "dbc1", "xil0", "xil1", "zl", "h2i", "ygst", "dbgst",
                "ygr", "wo0", "wo1", "ast0", "ast1", "p4l", "x2st",
                "p5g", "p5g0", "p5g1", "x2r", "intst", "ddst",
                "x2t", "inA", "inB", "ddl0", "ddl1", "ost0", "ost1"]
    ENG_SEMS = ["pe", "v", "s", "p"]
    Ctx.DMA_SEMS = set(DMA_SEMS)
    sem = {n: stp.enter_context(nc.semaphore("sem_" + n)) for n in DMA_SEMS + ENG_SEMS}

    def cst(i):
        return s_cst[:, i:i + 1]

    def cst1(i):
        return s_cst[0:1, i:i + 1]

    def barrier(eng, c):
        for n, v_ in c.c.items():
            if v_ > 0:
                eng.wait_ge(sem[n], 16 * v_ if n in Ctx.DMA_SEMS else v_)

    with nc.Block("p0") as b:
        @b.sync
        def _(sync):
            for dst, src in [(s_cst, cst_in), (s_ones1, ones1), (s_onesb, onesb),
                             (s_identb, identb_in), (s_identf, identf_in), (s_ind, ind_in),
                             (s_wn, wnorm), (s_dtb, dtb), (s_nA, alog), (s_wcv, wcv)]:
                sync.dma_start(out=dst[:], in_=src[:, :]).then_inc(sem["ld"], 16)
        ctx.ld += 10

        @b.scalar
        def _(scalar):
            scalar.wait_ge(sem["ld"], 16 * ctx.ld)
            scalar.activation(s_nA[:], s_nA[:], AF.Exp).then_inc(sem["s"], 1)
        ctx.s += 1

        @b.vector
        def _(vector):
            vector.wait_ge(sem["s"], ctx.s)
            vector.tensor_single_scalar(s_nA[:], s_nA[:], -1.0, OP.mult).then_inc(sem["v"], 1)
        ctx.v += 1


    if nphases < 2:
        with nc.Block("fin") as b:
            @b.gpsimd
            def _(gpsimd):
                barrier(gpsimd, ctx)
        for _st in reversed(_open):
            _st.close()
        stp.close()
        return nc

    st1 = ExitStack(); _open.append(st1)
    sb1 = lambda n_, s_, d_: st1.enter_context(nc.sbuf_tensor(n_, s_, d_))
    s_xn = sb1("s_xn", [128, KC * T], BF16)
    s_shared = sb1("s_shared", [128, 8192], BF16)
    s_x1b = sb1("s_x1b", [128, KC * CT], BF16)
    s_rstd = sb1("s_rstd", [1, 2 * CT], F32)

    base = Ctx(ctx)
    with nc.Block("p1a") as b:
        sta = ExitStack()
        ps_ss = [sta.enter_context(nc.psum_tensor(f"ps_ss{i}", [1, CT], F32)) for i in range(2)]
        ps_b = [sta.enter_context(nc.psum_tensor(f"ps_b{i}", [128, CT], F32)) for i in range(2)]

        @b.gpsimd
        def _(gpsimd):
            barrier(gpsimd, base)
            for i in range(NCH):
                if i >= 1:
                    gpsimd.wait_ge(sem["v"], base.v + 16 * i)
                gpsimd.dma_start(
                    out=s_x1b[:].rearrange("p (c t) -> p c t", c=KC),
                    in_=x_dev[:, i * CT:(i + 1) * CT].rearrange("(c p) t -> p c t", p=128),
                ).then_inc(sem["x1l"], 16)
        ctx.x1l += NCH

        @b.scalar
        def _(scalar):
            barrier(scalar, base)
            for i in range(NCH):
                scalar.wait_ge(sem["x1l"], 16 * (base.x1l + i + 1))
                if i >= 1:
                    scalar.wait_ge(sem["pe"], base.pe + 2 * i - 1)
                scalar.activation(s_shared[:], s_x1b[:], AF.Square).then_inc(sem["s"], 1)
                scalar.wait_ge(sem["pe"], base.pe + 2 * i + 1)
                sl = s_rstd[0:1, (i % 2) * CT:(i % 2 + 1) * CT]
                scalar.activation(sl, ps_ss[i % 2][0:1, :], AF.Ln,
                                  bias=cst1(C_RMSEPS), scale=cst1(C_INV2048)).then_inc(sem["s"], 1)
                scalar.activation(sl, sl, AF.Exp, scale=cst1(C_NEGHALF)).then_inc(sem["s"], 1)
        ctx.s += 3 * NCH

        @b.tensor
        def _(tensor):
            barrier(tensor, base)
            for i in range(NCH):
                tensor.wait_ge(sem["s"], base.s + 3 * i + 1)
                if i >= 2:
                    tensor.wait_ge(sem["s"], base.s + 3 * (i - 2) + 2)
                for k in range(KC):
                    mm = tensor.matmul(ps_ss[i % 2][0:1, :], s_onesb[:, 0:1],
                                       s_shared[:, k * CT:(k + 1) * CT],
                                       start=(k == 0), stop=(k == KC - 1))
                mm.then_inc(sem["pe"], 1)
                tensor.wait_ge(sem["s"], base.s + 3 * i + 3)
                if i >= 2:
                    tensor.wait_ge(sem["v"], base.v + 16 * (i - 1))
                tensor.matmul(ps_b[i % 2][:, :], s_ones1[0:1, :],
                              s_rstd[0:1, (i % 2) * CT:(i % 2 + 1) * CT],
                              start=True, stop=True).then_inc(sem["pe"], 1)
        ctx.pe += 2 * NCH

        @b.vector
        def _(vector):
            barrier(vector, base)
            for i in range(NCH):
                vector.wait_ge(sem["pe"], base.pe + 2 * i + 2)
                for k in range(KC):
                    vector.scalar_tensor_tensor(
                        s_xn[:, k * T + i * CT:k * T + (i + 1) * CT],
                        s_x1b[:, k * CT:(k + 1) * CT], s_wn[:, k:k + 1],
                        ps_b[i % 2][:, :], OP.mult, OP.mult).then_inc(sem["v"], 1)
        ctx.v += 16 * NCH
        sta.close()

    if nphases < 3:
        with nc.Block("fin") as b:
            @b.gpsimd
            def _(gpsimd):
                barrier(gpsimd, ctx)
        for _st in reversed(_open):
            _st.close()
        stp.close()
        return nc

    s_wcol = [sb1(f"s_wcol{i}", [128, KC * 128], BF16) for i in range(2)]
    base = Ctx(ctx)
    with nc.Block("p1b") as b:
        sta = ExitStack()
        ps1 = [sta.enter_context(nc.psum_tensor(f"ps1_{i}", [128, CT], F32)) for i in range(8)]

        @b.gpsimd
        def _(gpsimd):
            for f in range(FCH):
                if f >= 2:
                    gpsimd.wait_ge(sem["pe"], base.pe + (f - 1) * NT)
                gpsimd.dma_start(
                    out=s_wcol[f % 2][:].rearrange("p (c f) -> p c f", f=128),
                    in_=w_inT[:, f * 128:(f + 1) * 128].rearrange("(c p) f -> p c f", p=128),
                ).then_inc(sem["wc0" if f % 2 == 0 else "wc1"], 16)
        ctx.wc0 += (FCH + 1) // 2
        ctx.wc1 += FCH // 2

        @b.tensor
        def _(tensor):
            barrier(tensor, base)
            for f in range(FCH):
                wsem = "wc0" if f % 2 == 0 else "wc1"
                tensor.wait_ge(sem[wsem], 16 * (getattr(base, wsem) + f // 2 + 1))
                for t in range(NT):
                    if f >= 1:
                        if t % 2 == 0:
                            tensor.wait_ge(sem["v"], base.v + (f - 1) * 4 + t // 2 + 1)
                        else:
                            tensor.wait_ge(sem["s"], base.s + (f - 1) * 4 + (t - 1) // 2 + 1)
                    for k in range(KC):
                        mm = tensor.matmul(ps1[t][:, :],
                                           s_wcol[f % 2][:, k * 128:(k + 1) * 128],
                                           s_xn[:, k * T + t * CT:k * T + (t + 1) * CT],
                                           start=(k == 0), stop=(k == KC - 1))
                    mm.then_inc(sem["pe"], 1)
        ctx.pe += FCH * NT

        @b.vector
        def _(vector):
            barrier(vector, base)
            for f in range(FCH):
                if f >= 2:
                    psem = "pst0" if f % 2 == 0 else "pst1"
                    vector.wait_ge(sem[psem], 16 * (getattr(base, psem) + (f - 2) // 2 + 1))
                for t in range(0, NT, 2):
                    vector.wait_ge(sem["pe"], base.pe + f * NT + t + 1)
                    vector.tensor_copy(
                        s_shared[:, (f % 2) * 4096 + t * CT:(f % 2) * 4096 + (t + 1) * CT],
                        ps1[t][:, :]).then_inc(sem["v"], 1)
        ctx.v += FCH * 4

        @b.scalar
        def _(scalar):
            barrier(scalar, base)
            for f in range(FCH):
                if f >= 2:
                    psem = "pst0" if f % 2 == 0 else "pst1"
                    scalar.wait_ge(sem[psem], 16 * (getattr(base, psem) + (f - 2) // 2 + 1))
                for t in range(1, NT, 2):
                    scalar.wait_ge(sem["pe"], base.pe + f * NT + t + 1)
                    scalar.copy(
                        s_shared[:, (f % 2) * 4096 + t * CT:(f % 2) * 4096 + (t + 1) * CT],
                        ps1[t][:, :]).then_inc(sem["s"], 1)
        ctx.s += FCH * 4

        @b.sync
        def _(sync):
            barrier(sync, base)
            for f in range(FCH):
                sync.wait_ge(sem["v"], base.v + (f + 1) * 4)
                sync.wait_ge(sem["s"], base.s + (f + 1) * 4)
                sync.dma_start(out=proj_h[f * 128:(f + 1) * 128, :],
                               in_=s_shared[:, (f % 2) * 4096:(f % 2) * 4096 + 4096],
                               ).then_inc(sem["pst0" if f % 2 == 0 else "pst1"], 16)
        ctx.pst0 += (FCH + 1) // 2
        ctx.pst1 += FCH // 2
        sta.close()
    st1.close(); _open.remove(st1)

    if nphases < 4:
        with nc.Block("fin") as b:
            @b.gpsimd
            def _(gpsimd):
                barrier(gpsimd, ctx)
        for _st in reversed(_open):
            _st.close()
        stp.close()
        return nc

    st2 = ExitStack(); _open.append(st2)
    sb2 = lambda n_, s_, d_: st2.enter_context(nc.sbuf_tensor(n_, s_, d_))
    s_h2i = sb2("s_h2i", [128, 2 * DI], BF16)
    s_xr = [sb2(f"s_xr{i}", [128, ICH * 520], BF16) for i in range(2)]
    s_zr = sb2("s_zr", [128, ICH * CT], BF16)
    s_dbc = [sb2(f"s_dbc{i}", [128, 3072], BF16) for i in range(2)]
    s_dt = sb2("s_dt", [128, 1024], F32)
    s_dt2 = sb2("s_dt2", [128, 1024], F32)
    s_dt3 = sb2("s_dt3", [128, 1024], F32)
    s_dA = sb2("s_dA", [128, 1024], F32)
    s_du = sb2("s_du", [128, 1024], F32)
    s_U = sb2("s_U", [128, 1024], F32)
    s_H = sb2("s_H", [128, 1024], F32)
    s_Yt = sb2("s_Yt", [128, 1024], BF16)
    s_xc = sb2("s_xc", [128, 2 * CT], F32)
    s_sil = sb2("s_sil", [128, 4 * CT], BF16)
    s_silz = sb2("s_silz", [128, 2 * CT], BF16)
    s_yg = sb2("s_yg", [128, ICH * CT], BF16)

    VP, SP, PP = 72, 66, 64
    V_DTPRE, V_CONV0, V_DT, V_DTA, V_DUU, V_SCAN0, V_CARRY, V_YT, V_YG0 = \
        1, 2, 34, 35, 36, 37, 39, 40, 41
    S_SP, S_SIL0, S_DA, S_SILZ0 = 1, 2, 34, 35

    base = Ctx(ctx)
    with nc.Block("p2") as b:
        sta = ExitStack()
        ps_xs = [sta.enter_context(nc.psum_tensor(f"ps_xs{i}", [128, CT], F32)) for i in range(2)]
        ps_y = [sta.enter_context(nc.psum_tensor(f"ps_y{i}", [128, CT], F32)) for i in range(4)]

        projr = proj_h[:, :].rearrange("(c p) t -> p c t", p=128)

        def vat(i, m):
            return base.v + i * VP + m

        def sat(i, m):
            return base.s + i * SP + m

        def pat(i, m):
            return base.pe + i * PP + m

        @b.gpsimd
        def _(gpsimd):
            barrier(gpsimd, base)
            gpsimd.dma_start(out=s_h2i[:].rearrange("p (c f) -> p c f", f=DI),
                             in_=w_h2i[:, :].rearrange("(c p) f -> p c f", p=128)
                             ).then_inc(sem["h2i"], 16)
            gpsimd.dma_start(
                out=scgr_h[:, :].rearrange("(j r) n -> j r n", r=16),
                in_=gate_scT[:, :].unsqueeze(1).broadcast_to([64, 16, NM])).then_inc(sem["scexp"], 16)
            gpsimd.dma_start(
                out=scur_h[:, :].rearrange("(j r) n -> j r n", r=16),
                in_=up_scT[:, :].unsqueeze(1).broadcast_to([64, 16, NM])).then_inc(sem["scexp"], 16)
            gpsimd.dma_start(
                out=scdr_h[:, :].rearrange("(j r) n -> j r n", r=16),
                in_=down_scT[:, :].unsqueeze(1).broadcast_to([512, 16, D])).then_inc(sem["scexp"], 16)
            for i in range(NCH):
                gpsimd.wait_ge(sem["v"], vat(i, VP))
                gpsimd.dma_start(
                    out=yg_h[:, i * CT:(i + 1) * CT].rearrange("(c p) t -> p c t", p=128),
                    in_=s_yg[:].rearrange("p (c t) -> p c t", c=ICH)).then_inc(sem["ygst"], 16)
                if debug:
                    for dbg, tile in [(dbg_dt, s_dt), (dbg_dA, s_dA), (dbg_U, s_U), (dbg_H, s_H),
                                      (dbg_Yt, s_Yt)]:
                        gpsimd.dma_start(out=dbg[:, i * 1024:(i + 1) * 1024],
                                         in_=tile[:]).then_inc(sem["dbgst"], 16)
        ctx.h2i += 1
        ctx.scexp += 3
        ctx.ygst += NCH
        if debug:
            ctx.dbgst += NCH * 5

        @b.sync
        def _(sync):
            barrier(sync, base)
            for i in range(NCH):
                ldb = base.ld + i * 3
                if i >= 2:
                    sync.wait_ge(sem["v"], vat(i - 2, V_YT))
                sync.dma_start(out=s_dbc[i % 2][:].rearrange("p (r c t) -> p r c t", r=3, c=2),
                               in_=projr[:, 64:70, i * CT:(i + 1) * CT].rearrange(
                                   "p (r c) t -> p r c t", r=3)
                               ).then_inc(sem["dbc0" if i % 2 == 0 else "dbc1"], 16)
                t0 = i * CT
                xrb = s_xr[i % 2][:].rearrange("p (g t) -> p g t", g=ICH)
                if i >= 2:
                    sync.wait_ge(sem["v"], vat(i - 2, V_CONV0 + ICH - 1))
                xsm = "xil0" if i % 2 == 0 else "xil1"
                if t0 == 0:
                    sync.dma_start(out=xrb[:, :, 3:515],
                                   in_=projr[:, 32:64, 0:CT]).then_inc(sem[xsm], 16)
                else:
                    sync.dma_start(out=xrb[:, :, 0:515],
                                   in_=projr[:, 32:64, t0 - 3:t0 + CT]).then_inc(sem[xsm], 16)
                if i >= 1:
                    sync.wait_ge(sem["s"], sat(i - 1, S_SILZ0 + ICH - 1))
                sync.dma_start(out=s_zr[:].rearrange("p (g t) -> p g t", g=ICH),
                               in_=projr[:, 0:32, i * CT:(i + 1) * CT]).then_inc(sem["zl"], 16)
        ctx.dbc0 += (NCH + 1) // 2
        ctx.dbc1 += NCH // 2
        ctx.xil0 += (NCH + 1) // 2
        ctx.xil1 += NCH // 2
        ctx.zl += NCH

        @b.vector
        def _(vector):
            barrier(vector, base)
            for g in range(ICH):
                vector.memset(s_xr[0][:, g * 520:g * 520 + 3], 0.0)
            for i in range(NCH):
                ldb = base.ld + i * 3
                dsm = "dbc0" if i % 2 == 0 else "dbc1"
                vector.wait_ge(sem[dsm], 16 * (getattr(base, dsm) + i // 2 + 1))
                if debug and i > 0:
                    vector.wait_ge(sem["dbgst"], 16 * (base.dbgst + i * 5))
                for h in range(2):
                    op = vector.tensor_scalar(
                        s_dt[:, h * CT:(h + 1) * CT],
                        s_dbc[i % 2][:, h * CT:(h + 1) * CT], s_dtb[:, h:h + 1], None, OP.add)
                op.then_inc(sem["v"], 1)
                xsm = "xil0" if i % 2 == 0 else "xil1"
                vector.wait_ge(sem[xsm], 16 * (getattr(base, xsm) + i // 2 + 1))
                for g in range(ICH):
                    gi = i * ICH + g
                    if gi >= 2:
                        pi, pg = (gi - 2) // ICH, (gi - 2) % ICH
                        vector.wait_ge(sem["s"], sat(pi, S_SIL0 + pg))
                    sl = s_xr[i % 2][:, g * 520:g * 520 + 515]
                    xcs = s_xc[:, (g % 2) * CT:(g % 2 + 1) * CT]
                    w4 = lambda k: s_wcv[:, 4 * g + k:4 * g + k + 1]
                    vector.tensor_scalar(xcs, sl[:, 0:CT], w4(0), None, OP.mult)
                    vector.scalar_tensor_tensor(xcs, sl[:, 1:1 + CT], w4(1), xcs, OP.mult, OP.add)
                    vector.scalar_tensor_tensor(xcs, sl[:, 2:2 + CT], w4(2), xcs, OP.mult, OP.add)
                    vector.scalar_tensor_tensor(xcs, sl[:, 3:3 + CT], w4(3), xcs, OP.mult, OP.add
                                                ).then_inc(sem["v"], 1)
                vector.wait_ge(sem["s"], sat(i, S_SP))
                vector.tensor_tensor(s_dt[:], s_dt2[:], s_dt3[:], OP.add).then_inc(sem["v"], 1)
                for h in range(2):
                    op = vector.tensor_scalar(s_dt2[:, h * CT:(h + 1) * CT],
                                              s_dt[:, h * CT:(h + 1) * CT],
                                              s_nA[:, h:h + 1], None, OP.mult)
                op.then_inc(sem["v"], 1)
                for h in range(2):
                    vector.wait_ge(sem["pe"], pat(i, 16 * (h + 1)))
                    vector.tensor_tensor(s_du[:, h * CT:(h + 1) * CT],
                                         s_dt[:, h * CT:(h + 1) * CT], ps_xs[h][:, :], OP.mult)
                    op = vector.tensor_tensor(s_U[:, h * CT:(h + 1) * CT],
                                              s_du[:, h * CT:(h + 1) * CT],
                                              s_dbc[i % 2][:, 1024 + h * CT:1024 + (h + 1) * CT],
                                              OP.mult)
                op.then_inc(sem["v"], 1)
                vector.wait_ge(sem["s"], sat(i, S_DA))
                for h in range(2):
                    init = 0.0 if i == 0 else s_carry[:, h:h + 1]
                    vector.tensor_tensor_scan(s_H[:, h * CT:(h + 1) * CT],
                                              s_dA[:, h * CT:(h + 1) * CT],
                                              s_U[:, h * CT:(h + 1) * CT],
                                              init, OP.mult, OP.add).then_inc(sem["v"], 1)
                for h in range(2):
                    op = vector.tensor_copy(s_carry[:, h:h + 1],
                                            s_H[:, h * CT + CT - 1:h * CT + CT])
                op.then_inc(sem["v"], 1)
                for h in range(2):
                    op = vector.tensor_tensor(s_Yt[:, h * CT:(h + 1) * CT],
                                              s_H[:, h * CT:(h + 1) * CT],
                                              s_dbc[i % 2][:, 2048 + h * CT:2048 + (h + 1) * CT],
                                              OP.mult)
                op.then_inc(sem["v"], 1)
                for ic in range(ICH):
                    vector.wait_ge(sem["pe"], pat(i, 32 + ic + 1))
                    vector.wait_ge(sem["s"], sat(i, S_SILZ0 + ic) if not OVERSYNC else sat(i, SP))
                    if i > 0 and ic == 0:
                        vector.wait_ge(sem["ygst"], 16 * (base.ygst + i))
                    vector.tensor_tensor(s_yg[:, ic * CT:(ic + 1) * CT], ps_y[ic % 4][:, :],
                                         s_silz[:, (ic % 2) * CT:(ic % 2 + 1) * CT], OP.mult
                                         ).then_inc(sem["v"], 1)
        ctx.v += NCH * VP

        @b.scalar
        def _(scalar):
            barrier(scalar, base)
            for i in range(NCH):
                ldb = base.ld + i * 3
                scalar.wait_ge(sem["v"], vat(i, V_DTPRE))
                scalar.activation(s_dt2[:], s_dt[:], AF.Relu)
                scalar.activation(s_dt3[:], s_dt[:], AF.Abs)
                scalar.activation(s_dt3[:], s_dt3[:], AF.Exp, scale=cst(C_NEG1))
                scalar.activation(s_dt3[:], s_dt3[:], AF.Ln, bias=cst(C_ONE)).then_inc(sem["s"], 1)
                for g in range(ICH):
                    scalar.wait_ge(sem["v"], vat(i, V_CONV0 + g))
                    gi = i * ICH + g
                    if gi >= 4:
                        pi, pg = (gi - 4) // ICH, (gi - 4) % ICH
                        scalar.wait_ge(sem["pe"], pat(pi, pg + 1))
                    scalar.activation(s_sil[:, (g % 4) * CT:(g % 4 + 1) * CT],
                                      s_xc[:, (g % 2) * CT:(g % 2 + 1) * CT],
                                      AF.Silu).then_inc(sem["s"], 1)
                scalar.wait_ge(sem["v"], vat(i, V_DTA))
                if i >= 1:
                    scalar.wait_ge(sem["v"], vat(i - 1, V_SCAN0 + 1))
                scalar.activation(s_dA[:], s_dt2[:], AF.Exp).then_inc(sem["s"], 1)
                scalar.wait_ge(sem["zl"], 16 * (base.zl + i + 1))
                for ic in range(ICH):
                    gi = i * ICH + ic
                    if gi >= 2:
                        pi, pg = (gi - 2) // ICH, (gi - 2) % ICH
                        scalar.wait_ge(sem["v"], vat(pi, V_YG0 + pg))
                    scalar.activation(s_silz[:, (ic % 2) * CT:(ic % 2 + 1) * CT],
                                      s_zr[:, ic * CT:(ic + 1) * CT],
                                      AF.Silu).then_inc(sem["s"], 1)
        ctx.s += NCH * SP

        @b.tensor
        def _(tensor):
            barrier(tensor, base)
            tensor.wait_ge(sem["h2i"], 16 * (base.h2i + 1))
            for i in range(NCH):
                for g in range(ICH):
                    tensor.wait_ge(sem["s"], sat(i, S_SIL0 + g))
                    if g in (0, 16) and i >= 1:
                        tensor.wait_ge(sem["v"], vat(i - 1, V_DUU))
                    mm = tensor.matmul(ps_xs[g // 16][:, :],
                                       s_ind[:, ((g // 2) % 8) * 128:((g // 2) % 8 + 1) * 128],
                                       s_sil[:, (g % 4) * CT:(g % 4 + 1) * CT],
                                       start=(g % 16 == 0), stop=(g % 16 == 15))
                    mm.then_inc(sem["pe"], 1)
                tensor.wait_ge(sem["v"], vat(i, V_YT))
                for ic in range(ICH):
                    gi = i * ICH + ic
                    if gi >= 4:
                        pi, pg = (gi - 4) // ICH, (gi - 4) % ICH
                        tensor.wait_ge(sem["v"], vat(pi, V_YG0 + pg))
                    for h in range(2):
                        mm = tensor.matmul(ps_y[ic % 4][:, :],
                                           s_h2i[:, h * DI + ic * 128:h * DI + (ic + 1) * 128],
                                           s_Yt[:, h * CT:(h + 1) * CT],
                                           start=(h == 0), stop=(h == 1))
                    mm.then_inc(sem["pe"], 1)
        ctx.pe += NCH * PP
        sta.close()
    st2.close(); _open.remove(st2)

    if nphases < 5:
        with nc.Block("fin") as b:
            @b.gpsimd
            def _(gpsimd):
                barrier(gpsimd, ctx)
        for _st in reversed(_open):
            _st.close()
        stp.close()
        return nc

    st3 = ExitStack(); _open.append(st3)
    sb3 = lambda n_, s_, d_: st3.enter_context(nc.sbuf_tensor(n_, s_, d_))
    s_ygr = sb3("s_ygr", [128, ICH * 2048], BF16)
    s_wo = [sb3(f"s_wo{i}", [128, ICH * 128], BF16) for i in range(2)]
    s_ev3 = [sb3(f"s_ev3{i}", [128, 4 * CT], F32) for i in range(2)]
    base = Ctx(ctx)
    HT = 2048
    with nc.Block("p3") as b:
        sta = ExitStack()
        ps3 = [sta.enter_context(nc.psum_tensor(f"ps3_{i}", [128, CT], F32)) for i in range(8)]

        @b.sync
        def _(sync):
            barrier(sync, base)
            for hf in range(2):
                if hf == 1:
                    sync.wait_ge(sem["pe"], base.pe + 16 * 4)
                sync.dma_start(out=s_ygr[:].rearrange("p (c t) -> p c t", c=ICH),
                               in_=yg_h[:, hf * HT:(hf + 1) * HT].rearrange("(c p) t -> p c t", p=128),
                               ).then_inc(sem["ygr"], 16)
                for dc in range(16):
                    j = hf * 16 + dc
                    sync.wait_ge(sem["v"], base.v + j * 2 + 2)
                    sync.wait_ge(sem["s"], base.s + j * 2 + 2)
                    sync.dma_start(out=attn_h[dc * 128:(dc + 1) * 128, hf * HT:(hf + 1) * HT],
                                   in_=s_ev3[j % 2][:, :]
                                   ).then_inc(sem["ast0" if j % 2 == 0 else "ast1"], 16)
        ctx.ygr += 2
        ctx.ast0 += 16
        ctx.ast1 += 16

        @b.gpsimd
        def _(gpsimd):
            barrier(gpsimd, base)
            for hf in range(2):
                for dc in range(16):
                    j = hf * 16 + dc
                    if j >= 2:
                        gpsimd.wait_ge(sem["pe"], base.pe + (j - 1) * 4)
                    gpsimd.dma_start(
                        out=s_wo[j % 2][:].rearrange("p (c f) -> p c f", f=128),
                        in_=w_outT[:, dc * 128:(dc + 1) * 128].rearrange("(c p) f -> p c f", p=128),
                    ).then_inc(sem["wo0" if j % 2 == 0 else "wo1"], 16)
        ctx.wo0 += 16
        ctx.wo1 += 16

        @b.tensor
        def _(tensor):
            barrier(tensor, base)
            for hf in range(2):
                tensor.wait_ge(sem["ygr"], 16 * (base.ygr + hf + 1))
                for dc in range(16):
                    j = hf * 16 + dc
                    wsm = "wo0" if j % 2 == 0 else "wo1"
                    tensor.wait_ge(sem[wsm], 16 * (getattr(base, wsm) + j // 2 + 1))
                    for t in range(4):
                        bank = (j % 2) * 4 + t
                        if j >= 2:
                            if t % 2 == 0:
                                tensor.wait_ge(sem["v"], base.v + (j - 2) * 2 + t // 2 + 1)
                            else:
                                tensor.wait_ge(sem["s"], base.s + (j - 2) * 2 + (t - 1) // 2 + 1)
                        for k in range(ICH):
                            mm = tensor.matmul(ps3[bank][:, :],
                                               s_wo[j % 2][:, k * 128:(k + 1) * 128],
                                               s_ygr[:, k * HT + t * CT:k * HT + (t + 1) * CT],
                                               start=(k == 0), stop=(k == ICH - 1))
                        mm.then_inc(sem["pe"], 1)
        ctx.pe += 32 * 4

        @b.vector
        def _(vector):
            barrier(vector, base)
            for j in range(32):
                if j >= 2:
                    asm_ = "ast0" if j % 2 == 0 else "ast1"
                    vector.wait_ge(sem[asm_], 16 * (getattr(base, asm_) + (j - 2) // 2 + 1))
                for t in range(0, 4, 2):
                    vector.wait_ge(sem["pe"], base.pe + j * 4 + t + 1)
                    vector.tensor_copy(s_ev3[j % 2][:, t * CT:(t + 1) * CT],
                                       ps3[(j % 2) * 4 + t][:, :]).then_inc(sem["v"], 1)
        ctx.v += 64

        @b.scalar
        def _(scalar):
            barrier(scalar, base)
            for j in range(32):
                if j >= 2:
                    asm_ = "ast0" if j % 2 == 0 else "ast1"
                    scalar.wait_ge(sem[asm_], 16 * (getattr(base, asm_) + (j - 2) // 2 + 1))
                for t in range(1, 4, 2):
                    scalar.wait_ge(sem["pe"], base.pe + j * 4 + t + 1)
                    scalar.copy(s_ev3[j % 2][:, t * CT:(t + 1) * CT],
                                ps3[(j % 2) * 4 + t][:, :]).then_inc(sem["s"], 1)
        ctx.s += 64
        sta.close()
    st3.close(); _open.remove(st3)

    if nphases < 6:
        with nc.Block("fin") as b:
            @b.gpsimd
            def _(gpsimd):
                barrier(gpsimd, ctx)
        for _st in reversed(_open):
            _st.close()
        stp.close()
        return nc

    st4 = ExitStack(); _open.append(st4)
    sb4 = lambda n_, s_, d_: st4.enter_context(nc.sbuf_tensor(n_, s_, d_))
    s_ap4 = sb4("s_ap4", [128, KC * CT], F32)
    s_x14 = sb4("s_x14", [128, KC * CT], F32)
    s_r4 = sb4("s_r4", [128, KC * CT], F32)
    s_rb4 = sb4("s_rb4", [128, KC * CT], BF16)
    s_x24 = sb4("s_x24", [128, KC * CT], BF16)
    s_mu4 = sb4("s_mu4", [1, 2 * CT], F32)
    s_rs4 = sb4("s_rs4", [1, 2 * CT], F32)
    base = Ctx(ctx)
    V4, S4, P4N = 5, 4, 4
    with nc.Block("p4") as b:
        sta = ExitStack()
        ps_s1 = [sta.enter_context(nc.psum_tensor(f"ps4a{i}", [1, CT], F32)) for i in range(2)]
        ps_s2 = [sta.enter_context(nc.psum_tensor(f"ps4b{i}", [1, CT], F32)) for i in range(2)]
        ps_mb = [sta.enter_context(nc.psum_tensor(f"ps4c{i}", [128, CT], F32)) for i in range(2)]
        ps_rb = [sta.enter_context(nc.psum_tensor(f"ps4d{i}", [128, CT], F32)) for i in range(2)]

        @b.sync
        def _(sync):
            barrier(sync, base)
            for i in range(NCH):
                if i >= 1:
                    sync.wait_ge(sem["v"], base.v + (i - 1) * V4 + 2)
                sync.dma_start(out=s_ap4[:].rearrange("p (c t) -> p c t", c=KC),
                               in_=attn_h[:, i * CT:(i + 1) * CT].rearrange("(c p) t -> p c t", p=128),
                               ).then_inc(sem["p4l"], 16)
                sync.dma_start(out=s_x14[:].rearrange("p (c t) -> p c t", c=KC),
                               in_=x_dev[:, i * CT:(i + 1) * CT].rearrange("(c p) t -> p c t", p=128),
                               ).then_inc(sem["p4l"], 16)
        ctx.p4l += 2 * NCH

        @b.vector
        def _(vector):
            barrier(vector, base)
            for i in range(NCH):
                vector.wait_ge(sem["p4l"], 16 * (base.p4l + 2 * i + 2))
                vector.tensor_tensor(s_r4[:], s_x14[:], s_ap4[:], OP.add).then_inc(sem["v"], 1)
                if i >= 1:
                    vector.wait_ge(sem["pe"], base.pe + (i - 1) * P4N + 2)
                vector.tensor_copy(s_rb4[:], s_r4[:]).then_inc(sem["v"], 1)
                mu = s_mu4[0:1, (i % 2) * CT:(i % 2 + 1) * CT]
                rs = s_rs4[0:1, (i % 2) * CT:(i % 2 + 1) * CT]
                vector.wait_ge(sem["s"], base.s + i * S4 + 2)
                vector.tensor_tensor(rs, mu, mu, OP.mult).then_inc(sem["v"], 1)
                vector.wait_ge(sem["pe"], base.pe + i * P4N + 2)
                vector.scalar_tensor_tensor(rs, ps_s2[i % 2][0:1, :], 1.0 / 2048.0, rs,
                                            OP.mult, OP.subtract).then_inc(sem["v"], 1)
                vector.wait_ge(sem["pe"], base.pe + i * P4N + 4)
                if i >= 1:
                    vector.wait_ge(sem["x2st"], 16 * (base.x2st + i))
                for k in range(KC):
                    vector.tensor_tensor(s_r4[:, k * CT:(k + 1) * CT], s_r4[:, k * CT:(k + 1) * CT],
                                         ps_mb[i % 2][:, :], OP.subtract)
                    op = vector.tensor_tensor(s_x24[:, k * CT:(k + 1) * CT],
                                              s_r4[:, k * CT:(k + 1) * CT],
                                              ps_rb[i % 2][:, :], OP.mult)
                op.then_inc(sem["v"], 1)
        ctx.v += NCH * V4

        @b.scalar
        def _(scalar):
            barrier(scalar, base)
            for i in range(NCH):
                scalar.wait_ge(sem["v"], base.v + i * V4 + 2)
                scalar.wait_ge(sem["pe"], base.pe + i * P4N + 1)
                scalar.activation(s_rb4[:], s_rb4[:], AF.Square).then_inc(sem["s"], 1)
                mu = s_mu4[0:1, (i % 2) * CT:(i % 2 + 1) * CT]
                rs = s_rs4[0:1, (i % 2) * CT:(i % 2 + 1) * CT]
                scalar.activation(mu, ps_s1[i % 2][0:1, :], AF.Identity,
                                  scale=cst1(C_INV2048)).then_inc(sem["s"], 1)
                scalar.wait_ge(sem["v"], base.v + i * V4 + 4)
                scalar.activation(rs, rs, AF.Ln, bias=cst1(C_LNEPS)).then_inc(sem["s"], 1)
                scalar.activation(rs, rs, AF.Exp, scale=cst1(C_NEGHALF)).then_inc(sem["s"], 1)
        ctx.s += NCH * S4

        @b.tensor
        def _(tensor):
            barrier(tensor, base)
            for i in range(NCH):
                tensor.wait_ge(sem["v"], base.v + i * V4 + 2)
                if i >= 2:
                    tensor.wait_ge(sem["s"], base.s + (i - 2) * S4 + 2)
                for k in range(KC):
                    mm = tensor.matmul(ps_s1[i % 2][0:1, :], s_onesb[:, 0:1],
                                       s_rb4[:, k * CT:(k + 1) * CT],
                                       start=(k == 0), stop=(k == KC - 1))
                mm.then_inc(sem["pe"], 1)
                tensor.wait_ge(sem["s"], base.s + i * S4 + 1)
                if i >= 2:
                    tensor.wait_ge(sem["v"], base.v + (i - 2) * V4 + 4)
                for k in range(KC):
                    mm = tensor.matmul(ps_s2[i % 2][0:1, :], s_onesb[:, 0:1],
                                       s_rb4[:, k * CT:(k + 1) * CT],
                                       start=(k == 0), stop=(k == KC - 1))
                mm.then_inc(sem["pe"], 1)
                tensor.wait_ge(sem["s"], base.s + i * S4 + 2)
                if i >= 2:
                    tensor.wait_ge(sem["v"], base.v + (i - 2) * V4 + 5)
                tensor.matmul(ps_mb[i % 2][:, :], s_ones1[0:1, :],
                              s_mu4[0:1, (i % 2) * CT:(i % 2 + 1) * CT],
                              start=True, stop=True).then_inc(sem["pe"], 1)
                tensor.wait_ge(sem["s"], base.s + i * S4 + 4)
                if i >= 2:
                    tensor.wait_ge(sem["v"], base.v + (i - 2) * V4 + 5)
                tensor.matmul(ps_rb[i % 2][:, :], s_ones1[0:1, :],
                              s_rs4[0:1, (i % 2) * CT:(i % 2 + 1) * CT],
                              start=True, stop=True).then_inc(sem["pe"], 1)
        ctx.pe += NCH * P4N

        @b.gpsimd
        def _(gpsimd):
            barrier(gpsimd, base)
            for i in range(NCH):
                gpsimd.wait_ge(sem["v"], base.v + (i + 1) * V4)
                gpsimd.dma_start(out=x2_h[:, i * CT:(i + 1) * CT].rearrange("(c p) t -> p c t", p=128),
                                 in_=s_x24[:].rearrange("p (c t) -> p c t", c=KC)).then_inc(sem["x2st"], 16)
        ctx.x2st += NCH
        sta.close()
    st4.close(); _open.remove(st4)

    if nphases < 7:
        with nc.Block("fin") as b:
            @b.gpsimd
            def _(gpsimd):
                barrier(gpsimd, ctx)
        for _st in reversed(_open):
            _st.close()
        stp.close()
        return nc

    st5 = ExitStack(); _open.append(st5)
    sb5 = lambda n_, s_, d_: st5.enter_context(nc.sbuf_tensor(n_, s_, d_))
    s_x2r = sb5("s_x2r", [128, KC * T], BF16)
    s_pkg = sb5("s_pkg", [128, 1024], U8)
    s_pku = sb5("s_pku", [128, 1024], U8)
    s_pkd = sb5("s_pkd", [128, 2048], U8)
    s_scg = sb5("s_scg", [128, 1024], BF16)
    s_scu = sb5("s_scu", [128, 1024], BF16)
    s_scd = sb5("s_scd", [128, 2048], BF16)
    s_dg = [sb5(f"s_dg{i}", [128, 2048], BF16) for i in range(2)]
    s_dup = [sb5(f"s_dup{i}", [128, 2048], BF16) for i in range(2)]
    s_dd = sb5("s_dd", [128, 4096], BF16)
    s_gel = sb5("s_gel", [128, 4 * CT], BF16)
    s_int = sb5("s_int", [128, T], BF16)
    s_j7 = sb5("s_j7", [128, 1024], BF16)
    s_sg5 = sb5("s_sg5", [128, 1024], BF16)
    s_t1 = sb5("s_t1", [128, 1024], BF16)
    s_t2 = sb5("s_t2", [128, 1024], BF16)
    s_i1 = sb5("s_i1", [128, 1024], U8)

    def it5(c):
        return (4 if c == 0 else 0) + (4 if c < CCH - 1 else 0) + 2 + 8

    VOFF2 = [0]
    for c in range(CCH):
        VOFF2.append(VOFF2[-1] + it5(c))

    def dpos(c):
        return VOFF2[c] + (4 if c == 0 else 0)

    def decg_done(c):
        return VOFF2[0] + 2 if c == 0 else dpos(c - 1) + 2

    def decu_done(c):
        return VOFF2[0] + 4 if c == 0 else dpos(c - 1) + 4

    def ddpos(c):
        return VOFF2[c] + (4 if c == 0 else 0) + (4 if c < CCH - 1 else 0)

    def ipos(c):
        return VOFF2[c] + it5(c) - 8

    def gcnt(c):
        return 4 + (2 if c % 2 == 0 else 0)

    GOFF = [0]
    for c in range(CCH):
        GOFF.append(GOFF[-1] + gcnt(c))

    def stcnt(c):
        return 1 + (2 if c < 64 else 0)

    STOFF = [0]
    for c in range(CCH):
        STOFF.append(STOFF[-1] + stcnt(c))

    def decode_half(vector, pk_ap, shift, sc_ap, out_ap, width):
        j = s_i1[:, 0:width]
        if shift:
            vector.tensor_single_scalar(j, pk_ap, shift, OP.logical_shift_right)
            vector.tensor_single_scalar(j, j, 7, OP.bitwise_and)
        else:
            vector.tensor_single_scalar(j, pk_ap, 7, OP.bitwise_and)
        jb = s_j7[:, 0:width]
        vector.tensor_copy(jb, j)
        vector.tensor_single_scalar(j, pk_ap, shift + 3, OP.logical_shift_right)
        vector.tensor_single_scalar(j, j, 1, OP.bitwise_and)
        vector.tensor_scalar(s_sg5[:, 0:width], j, -2.0, 1.0, OP.mult, OP.add)
        vector.tensor_scalar_max(s_t1[:, 0:width], jb, 4.0)
        vector.tensor_scalar_max(s_t2[:, 0:width], jb, 6.0)
        vector.scalar_tensor_tensor(s_t1[:, 0:width], s_t1[:, 0:width], 0.5,
                                    s_t2[:, 0:width], OP.mult, OP.add)
        vector.scalar_tensor_tensor(s_t1[:, 0:width], jb, 0.5, s_t1[:, 0:width], OP.mult, OP.add)
        vector.scalar_tensor_tensor(s_t1[:, 0:width], s_t1[:, 0:width], -8.0,
                                    s_sg5[:, 0:width], OP.add, OP.mult)
        return vector.tensor_tensor(out_ap, s_t1[:, 0:width], sc_ap, OP.mult)

    base = Ctx(ctx)
    with nc.Block("p5") as b:
        sta = ExitStack()
        ps5 = [sta.enter_context(nc.psum_tensor(f"ps5_{i}", [128, CT], F32)) for i in range(8)]

        @b.sync
        def _(sync):
            barrier(sync, base)
            sync.dma_start(out=s_x2r[:].rearrange("p (c t) -> p c t", c=KC),
                           in_=x2_h[:, :].rearrange("(c p) t -> p c t", p=128)).then_inc(sem["x2r"], 16)
            for c in range(CCH):
                if c % 2 == 1:
                    j = c // 2
                    sync.wait_ge(sem["v"], base.v + ddpos(c) + 2)
                    sync.dma_start(out=ddec_h[j * 128:(j + 1) * 128, :],
                                   in_=s_dd[:, 0:2048]).then_inc(sem["ddst"], 16)
                    sync.dma_start(out=ddec_h[(64 + j) * 128:(64 + j + 1) * 128, :],
                                   in_=s_dd[:, 2048:4096]).then_inc(sem["ddst"], 16)
                sync.wait_ge(sem["v"], base.v + VOFF2[c] + it5(c))
                sync.dma_start(out=inter_h[c * 128:(c + 1) * 128, :],
                               in_=s_int[:, :]).then_inc(sem["intst"], 16)
        ctx.x2r += 1
        ctx.intst += CCH
        ctx.ddst += 128

        @b.gpsimd
        def _(gpsimd):
            barrier(gpsimd, base)
            for c in range(CCH):
                if c == 1:
                    gpsimd.wait_ge(sem["v"], base.v + VOFF2[0] + 4)
                elif c >= 2:
                    gpsimd.wait_ge(sem["v"], base.v + dpos(c - 2) + 4)
                gpsimd.dma_start(out=s_pkg[:],
                                 in_=gate_pT[:, c * 128:(c + 1) * 128].rearrange(
                                     "(a p) n -> p a n", p=128)).then_inc(sem["p5g"], 16)
                gpsimd.dma_start(out=s_pku[:],
                                 in_=up_pT[:, c * 128:(c + 1) * 128].rearrange(
                                     "(a p) n -> p a n", p=128)).then_inc(sem["p5g"], 16)
                gpsimd.dma_start(
                    out=s_scg[:],
                    in_=scgr_h[:, c * 128:(c + 1) * 128].rearrange(
                        "(a p) n -> p a n", p=128)).then_inc(sem["p5g"], 16)
                gpsimd.dma_start(
                    out=s_scu[:],
                    in_=scur_h[:, c * 128:(c + 1) * 128].rearrange(
                        "(a p) n -> p a n", p=128)).then_inc(sem["p5g"], 16)
                if c % 2 == 0:
                    j = c // 2
                    if j >= 1:
                        gpsimd.wait_ge(sem["v"], base.v + ddpos(c - 1) + 2)
                    gpsimd.dma_start(out=s_pkd[:],
                                     in_=down_pT[j * 128:(j + 1) * 128, :]).then_inc(sem["p5g"], 16)
                    gpsimd.dma_start(out=s_scd[:],
                                     in_=scdr_h[j * 128:(j + 1) * 128, :]).then_inc(sem["p5g"], 16)
        ctx.p5g += GOFF[CCH]

        @b.vector
        def _(vector):
            barrier(vector, base)

            def dec_gu(cc):
                vector.wait_ge(sem["p5g"], 16 * (base.p5g + GOFF[cc] + 4))
                if cc >= 2:
                    vector.wait_ge(sem["pe"], base.pe + (cc - 1) * 16)
                decode_half(vector, s_pkg[:, 0:1024], 0, s_scg[:, 0:1024],
                            s_dg[cc % 2][:, 0:1024], 1024).then_inc(sem["v"], 1)
                decode_half(vector, s_pkg[:, 0:1024], 4, s_scg[:, 0:1024],
                            s_dg[cc % 2][:, 1024:2048], 1024).then_inc(sem["v"], 1)
                decode_half(vector, s_pku[:, 0:1024], 0, s_scu[:, 0:1024],
                            s_dup[cc % 2][:, 0:1024], 1024).then_inc(sem["v"], 1)
                decode_half(vector, s_pku[:, 0:1024], 4, s_scu[:, 0:1024],
                            s_dup[cc % 2][:, 1024:2048], 1024).then_inc(sem["v"], 1)

            for c in range(CCH):
                if c == 0:
                    dec_gu(0)
                if c < CCH - 1:
                    dec_gu(c + 1)
                vector.wait_ge(sem["p5g"], 16 * (base.p5g + GOFF[c - (c % 2)] + 6))
                if c % 2 == 0 and c >= 2:
                    vector.wait_ge(sem["ddst"], 16 * (base.ddst + 2 * (c // 2)))
                w0 = (c % 2) * 1024
                decode_half(vector, s_pkd[:, w0:w0 + 1024], 0, s_scd[:, w0:w0 + 1024],
                            s_dd[:, w0:w0 + 1024], 1024).then_inc(sem["v"], 1)
                decode_half(vector, s_pkd[:, w0:w0 + 1024], 4, s_scd[:, w0:w0 + 1024],
                            s_dd[:, 2048 + w0:2048 + w0 + 1024], 1024).then_inc(sem["v"], 1)
                for t in range(NT):
                    h, tq = t // 4, t % 4
                    vector.wait_ge(sem["pe"], base.pe + c * 16 + h * 8 + 4 + tq + 1)
                    vector.wait_ge(sem["s"], base.s + c * 8 + h * 4 + tq + 1)
                    if c >= 1 and t == 0:
                        vector.wait_ge(sem["intst"], 16 * (base.intst + c))
                    vector.tensor_tensor(s_int[:, t * CT:(t + 1) * CT], ps5[4 + tq][:, :],
                                         s_gel[:, tq * CT:(tq + 1) * CT], OP.mult).then_inc(sem["v"], 1)
        ctx.v += VOFF2[CCH]

        @b.scalar
        def _(scalar):
            barrier(scalar, base)
            for c in range(CCH):
                for t in range(NT):
                    h, tq = t // 4, t % 4
                    scalar.wait_ge(sem["pe"], base.pe + c * 16 + h * 8 + tq + 1)
                    if c * 8 + h * 4 + tq >= 4:
                        pidx = c * 8 + h * 4 + tq - 4
                        pc, prem = pidx // 8, pidx % 8
                        scalar.wait_ge(sem["v"], base.v + ipos(pc) + prem + 1)
                    scalar.activation(s_gel[:, tq * CT:(tq + 1) * CT], ps5[tq][:, :],
                                      AF.Gelu_apprx_tanh).then_inc(sem["s"], 1)
        ctx.s += CCH * 8

        @b.tensor
        def _(tensor):
            barrier(tensor, base)
            tensor.wait_ge(sem["x2r"], 16 * (base.x2r + 1))
            for c in range(CCH):
                tensor.wait_ge(sem["v"], base.v + decg_done(c))
                for h in range(2):
                    for tq in range(4):
                        t = 4 * h + tq
                        gidx = c * 8 + h * 4 + tq
                        if gidx >= 4:
                            pidx = gidx - 4
                            pc, prem = pidx // 8, pidx % 8
                            tensor.wait_ge(sem["s"], base.s + pc * 8 + prem + 1)
                        for k in range(KC):
                            mm = tensor.matmul(ps5[tq][:, :], s_dg[c % 2][:, k * 128:(k + 1) * 128],
                                               s_x2r[:, k * T + t * CT:k * T + (t + 1) * CT],
                                               start=(k == 0), stop=(k == KC - 1))
                        mm.then_inc(sem["pe"], 1)
                    if h == 0:
                        tensor.wait_ge(sem["v"], base.v + decu_done(c))
                    for tq in range(4):
                        t = 4 * h + tq
                        uidx = c * 8 + h * 4 + tq
                        if uidx >= 4:
                            pidx = uidx - 4
                            pc, prem = pidx // 8, pidx % 8
                            tensor.wait_ge(sem["v"], base.v + ipos(pc) + prem + 1)
                        for k in range(KC):
                            mm = tensor.matmul(ps5[4 + tq][:, :],
                                               s_dup[c % 2][:, k * 128:(k + 1) * 128],
                                               s_x2r[:, k * T + t * CT:k * T + (t + 1) * CT],
                                               start=(k == 0), stop=(k == KC - 1))
                        mm.then_inc(sem["pe"], 1)
        ctx.pe += CCH * 16
        sta.close()
    st5.close(); _open.remove(st5)

    if nphases < 8:
        with nc.Block("fin") as b:
            @b.gpsimd
            def _(gpsimd):
                barrier(gpsimd, ctx)
        for _st in reversed(_open):
            _st.close()
        stp.close()
        return nc

    st6 = ExitStack(); _open.append(st6)
    sb6 = lambda n_, s_, d_: st6.enter_context(nc.sbuf_tensor(n_, s_, d_))
    s_intr = [sb6(f"s_intr{i}", [128, 64 * CT], BF16) for i in range(2)]
    s_tmp6 = sb6("s_tmp6", [128, KC * CT], BF16)
    s_r6 = sb6("s_r6", [128, KC * CT], BF16)
    s_ddl = [sb6(f"s_ddl{i}", [128, 4 * 1024], BF16) for i in range(2)]
    s_mu6 = sb6("s_mu6", [1, CT], F32)
    s_rs6 = sb6("s_rs6", [1, CT], F32)
    s_st6 = sb6("s_st6", [128, 8], F32)
    s_rt = sb6("s_rt", [128, 2048], BF16)
    s_ot = [sb6(f"s_ot{i}", [128, 2048], BF16) for i in range(2)]

    base = Ctx(ctx)
    PE6, V6, S6, LD6 = 138, 55, 35, 67
    with nc.Block("p6") as b:
        sta = ExitStack()
        ps6 = [sta.enter_context(nc.psum_tensor(f"ps6_{i}", [128, CT], F32)) for i in range(8)]
        tp_alias = [ps6[3 + k][:, 0:64].bitcast(BF16) for k in range(4)]

        def pe6(tau, m):
            return base.pe + tau * PE6 + m

        def v6(tau, m):
            return base.v + tau * V6 + m

        def s6(tau, m):
            return base.s + tau * S6 + m

        @b.sync
        def _(sync):
            barrier(sync, base)
            for tau in range(NT):
                ldb = base.ld + tau * LD6
                if tau >= 1:
                    sync.wait_ge(sem["pe"], pe6(tau - 1, 66))
                sync.dma_start(out=s_tmp6[:].rearrange("p (c t) -> p c t", c=KC),
                               in_=x2_h[:, tau * CT:(tau + 1) * CT].rearrange(
                                   "(c p) t -> p c t", p=128)).then_inc(sem["x2t"], 16)
                if tau >= 1:
                    sync.wait_ge(sem["pe"], pe6(tau - 1, 48))
                sync.dma_start(out=s_intr[0][:].rearrange("p (c t) -> p c t", c=64),
                               in_=inter_h[0:8192, tau * CT:(tau + 1) * CT].rearrange(
                                   "(c p) t -> p c t", p=128)).then_inc(sem["inA"], 16)
                if tau >= 1:
                    sync.wait_ge(sem["pe"], pe6(tau - 1, 64))
                sync.dma_start(out=s_intr[1][:].rearrange("p (c t) -> p c t", c=64),
                               in_=inter_h[8192:NM, tau * CT:(tau + 1) * CT].rearrange(
                                   "(c p) t -> p c t", p=128)).then_inc(sem["inB"], 16)
                for grp in range(2):
                    for G in range(32):
                        gg = tau * 64 + grp * 32 + G
                        if gg >= 2:
                            pgg = gg - 2
                            pt, prem = pgg // 64, pgg % 64
                            sync.wait_ge(sem["pe"], pe6(pt, prem + 1))
                        sync.dma_start(
                            out=s_ddl[G % 2][:].rearrange("p (c n) -> p c n", c=4),
                            in_=ddec_h[G * 512:(G + 1) * 512,
                                       grp * 1024:(grp + 1) * 1024].rearrange(
                                "(c p) n -> p c n", p=128)
                        ).then_inc(sem["ddl0" if gg % 2 == 0 else "ddl1"], 16)
        ctx.x2t += NT
        ctx.inA += NT
        ctx.inB += NT
        ctx.ddl0 += NT * 32
        ctx.ddl1 += NT * 32

        @b.tensor
        def _(tensor):
            barrier(tensor, base)
            for tau in range(NT):
                ldb = base.ld + tau * LD6
                for grp in range(2):
                    if tau >= 1 and grp == 0:
                        tensor.wait_ge(sem["v"], v6(tau - 1, V6))
                        tensor.wait_ge(sem["s"], s6(tau - 1, S6))
                    if grp == 1:
                        tensor.wait_ge(sem["v"], v6(tau, 8))
                    tensor.wait_ge(sem["inA" if grp == 0 else "inB"],
                                   16 * (getattr(base, "inA" if grp == 0 else "inB") + tau + 1))
                    for G in range(32):
                        gg = tau * 64 + grp * 32 + G
                        dsm = "ddl0" if gg % 2 == 0 else "ddl1"
                        tensor.wait_ge(sem[dsm], 16 * (getattr(base, dsm) + gg // 2 + 1))
                        for cc in range(4):
                            c = G * 4 + cc
                            intr = s_intr[0] if c < 64 else s_intr[1]
                            for dc in range(8):
                                mm = tensor.matmul(
                                    ps6[dc][:, :],
                                    s_ddl[G % 2][:, cc * 1024 + dc * 128:cc * 1024 + (dc + 1) * 128],
                                    intr[:, (c % 64) * CT:(c % 64 + 1) * CT],
                                    start=(c == 0), stop=(c == CCH - 1))
                        mm.then_inc(sem["pe"], 1)
                tensor.wait_ge(sem["v"], v6(tau, 16))
                for k in range(KC):
                    mm = tensor.matmul(ps6[0][0:1, :], s_onesb[:, 0:1],
                                       s_r6[:, k * CT:(k + 1) * CT],
                                       start=(k == 0), stop=(k == KC - 1))
                mm.then_inc(sem["pe"], 1)
                tensor.wait_ge(sem["s"], s6(tau, 2))
                for k in range(KC):
                    mm = tensor.matmul(ps6[1][0:1, :], s_onesb[:, 0:1],
                                       s_tmp6[:, k * CT:(k + 1) * CT],
                                       start=(k == 0), stop=(k == KC - 1))
                mm.then_inc(sem["pe"], 1)
                tensor.wait_ge(sem["s"], s6(tau, 1))
                for j in range(4):
                    tensor.matmul(ps6[2][:, j:j + 1],
                                  s_mu6[0:1, j * 128:(j + 1) * 128], s_ones1[0:1, 0:1],
                                  start=True, stop=True).then_inc(sem["pe"], 1)
                tensor.wait_ge(sem["s"], s6(tau, 3))
                for j in range(4):
                    tensor.matmul(ps6[2][:, 4 + j:5 + j],
                                  s_rs6[0:1, j * 128:(j + 1) * 128], s_ones1[0:1, 0:1],
                                  start=True, stop=True).then_inc(sem["pe"], 1)
                cnt = 0
                for j in range(4):
                    for dc in range(16):
                        if cnt >= 4:
                            prev = cnt - 4
                            pj, pdc = prev // 16, prev % 16
                            if pdc % 2 == 0:
                                tensor.wait_ge(sem["v"], v6(tau, 19 + pj * 9 + pdc // 2 + 1))
                            else:
                                tensor.wait_ge(sem["s"], s6(tau, 3 + pj * 8 + (pdc - 1) // 2 + 1))
                        tensor.transpose(tp_alias[cnt % 4],
                                         s_r6[:, dc * CT + j * 128:dc * CT + (j + 1) * 128],
                                         s_identb[:, :]).then_inc(sem["pe"], 1)
                        cnt += 1
        ctx.pe += NT * PE6

        @b.vector
        def _(vector):
            barrier(vector, base)
            for tau in range(NT):
                vector.wait_ge(sem["x2t"], 16 * (base.x2t + tau + 1))
                if debug and tau >= 1:
                    vector.wait_ge(sem["dbgst"], 16 * (base.dbgst + 2 * tau))
                for grp in range(2):
                    vector.wait_ge(sem["pe"], pe6(tau, 32 * (grp + 1)))
                    for dc in range(8):
                        vector.tensor_tensor(
                            s_r6[:, (grp * 8 + dc) * CT:(grp * 8 + dc + 1) * CT],
                            ps6[dc][:, :],
                            s_tmp6[:, (grp * 8 + dc) * CT:(grp * 8 + dc + 1) * CT],
                            OP.add).then_inc(sem["v"], 1)
                vector.wait_ge(sem["s"], s6(tau, 1))
                vector.tensor_tensor(s_rs6[0:1, :], s_mu6[0:1, :], s_mu6[0:1, :],
                                     OP.mult).then_inc(sem["v"], 1)
                vector.wait_ge(sem["pe"], pe6(tau, 66))
                vector.scalar_tensor_tensor(s_rs6[0:1, :], ps6[1][0:1, :], 1.0 / 2048.0,
                                            s_rs6[0:1, :], OP.mult, OP.subtract
                                            ).then_inc(sem["v"], 1)
                vector.wait_ge(sem["pe"], pe6(tau, 74))
                vector.tensor_copy(s_st6[:, 0:8], ps6[2][:, 0:8]).then_inc(sem["v"], 1)
                for j in range(4):
                    for dc in range(0, 16, 2):
                        cnt = j * 16 + dc
                        vector.wait_ge(sem["pe"], pe6(tau, 74 + cnt + 1))
                        vector.tensor_copy(s_rt[:, dc * 128:(dc + 1) * 128],
                                           tp_alias[cnt % 4]).then_inc(sem["v"], 1)
                    vector.wait_ge(sem["s"], s6(tau, 3 + (j + 1) * 8))
                    gj = tau * 4 + j
                    if gj >= 2:
                        osm = "ost0" if gj % 2 == 0 else "ost1"
                        vector.wait_ge(sem[osm], 16 * (getattr(base, osm) + (gj - 2) // 2 + 1))
                    vector.tensor_scalar(s_ot[j % 2][:, :], s_rt[:, :],
                                         s_st6[:, j:j + 1], s_st6[:, 4 + j:5 + j],
                                         OP.subtract, OP.mult).then_inc(sem["v"], 1)
        ctx.v += NT * V6

        @b.scalar
        def _(scalar):
            barrier(scalar, base)
            for tau in range(NT):
                scalar.wait_ge(sem["pe"], pe6(tau, 65))
                scalar.activation(s_mu6[0:1, :], ps6[0][0:1, :], AF.Identity,
                                  scale=cst1(C_INV2048)).then_inc(sem["s"], 1)
                scalar.wait_ge(sem["v"], v6(tau, 16))
                scalar.activation(s_tmp6[:], s_r6[:], AF.Square).then_inc(sem["s"], 1)
                scalar.wait_ge(sem["v"], v6(tau, 18))
                scalar.activation(s_rs6[0:1, :], s_rs6[0:1, :], AF.Ln, bias=cst1(C_LNEPS))
                scalar.activation(s_rs6[0:1, :], s_rs6[0:1, :], AF.Exp,
                                  scale=cst1(C_NEGHALF)).then_inc(sem["s"], 1)
                for j in range(4):
                    for dc in range(1, 16, 2):
                        cnt = j * 16 + dc
                        if j >= 1 and dc == 1:
                            scalar.wait_ge(sem["v"], v6(tau, 28 + 9 * (j - 1)))
                        scalar.wait_ge(sem["pe"], pe6(tau, 74 + cnt + 1))
                        scalar.copy(s_rt[:, dc * 128:(dc + 1) * 128],
                                    tp_alias[cnt % 4]).then_inc(sem["s"], 1)
        ctx.s += NT * S6

        @b.gpsimd
        def _(gpsimd):
            barrier(gpsimd, base)
            for tau in range(NT):
                if debug:
                    gpsimd.wait_ge(sem["v"], v6(tau, 16))
                    gpsimd.dma_start(out=dbg_r6[:, tau * 8192:(tau + 1) * 8192],
                                     in_=s_r6[:]).then_inc(sem["dbgst"], 16)
                    gpsimd.wait_ge(sem["v"], v6(tau, 19))
                    gpsimd.dma_start(out=dbg_st6[:, tau * 8:(tau + 1) * 8],
                                     in_=s_st6[:, 0:8]).then_inc(sem["dbgst"], 16)
                for j in range(4):
                    gj = tau * 4 + j
                    gpsimd.wait_ge(sem["v"], v6(tau, 28 + 9 * j))
                    gpsimd.dma_start(out=out_dev[tau * CT + j * 128:tau * CT + (j + 1) * 128, :],
                                     in_=s_ot[j % 2][:, :]
                                     ).then_inc(sem["ost0" if gj % 2 == 0 else "ost1"], 16)
            gpsimd.wait_ge(sem["ost0"], 16 * (base.ost0 + NT * 2))
            gpsimd.wait_ge(sem["ost1"], 16 * (base.ost1 + NT * 2))
            if debug:
                gpsimd.wait_ge(sem["dbgst"], 16 * (base.dbgst + NT * 2))
        ctx.ost0 += NT * 2
        ctx.ost1 += NT * 2
        if debug:
            ctx.dbgst += NT * 2
        sta.close()
    st6.close(); _open.remove(st6)
    stp.close()
    return nc


def prep_shared(inp):
    perm = np.concatenate([np.arange(0, D, 2), np.arange(1, D, 2)])
    perm2 = np.concatenate([np.arange(0, NM, 2), np.arange(1, NM, 2)])
    W_in = np.asarray(inp["L0_attn_W_in"], np.float32)
    Wf = np.concatenate([
        W_in[0:4096],
        W_in[4096:8192],
        np.repeat(W_in[8192:8208], 16, axis=0),
        W_in[8208:8464],
        W_in[8464:8720],
    ], axis=0)
    d = {}
    d["w_inT"] = np.ascontiguousarray(Wf.T[perm])
    d["w_outT"] = np.ascontiguousarray(np.asarray(inp["L0_attn_W_out"], np.float32).T[:, perm])
    d["w_h2i"] = np.ascontiguousarray(np.repeat(np.asarray(inp["L0_attn_W_h2i"], np.float32).T, 16, axis=0))
    d["wnorm"] = np.ascontiguousarray(np.asarray(inp["L0_attn_w_norm"], np.float32)[perm].reshape(KC, 128).T)
    d["dtb"] = np.ascontiguousarray(np.repeat(np.asarray(inp["L0_attn_dt_bias"], np.float32), 16).reshape(2, 128).T)
    d["alog"] = np.ascontiguousarray(np.repeat(np.asarray(inp["L0_attn_A_log"], np.float32), 16).reshape(2, 128).T)
    d["wcv"] = np.ascontiguousarray(
        np.asarray(inp["L0_attn_w_conv"], np.float32).reshape(DI, 4).reshape(ICH, 128, 4)
        .transpose(1, 0, 2).reshape(128, 128))
    d["gate_pT"] = np.ascontiguousarray(np.asarray(inp["L0_ffn_gate_W_packed"], np.int32).T[:, perm2])
    d["up_pT"] = np.ascontiguousarray(np.asarray(inp["L0_ffn_up_W_packed"], np.int32).T[:, perm2])
    d["down_pT"] = np.ascontiguousarray(np.asarray(inp["L0_ffn_down_W_packed"], np.int32).T[:, perm])
    d["gate_scT"] = np.ascontiguousarray(np.asarray(inp["L0_ffn_gate_scales"], np.float32).T[:, perm2])
    d["up_scT"] = np.ascontiguousarray(np.asarray(inp["L0_ffn_up_scales"], np.float32).T[:, perm2])
    d["down_scT"] = np.ascontiguousarray(np.asarray(inp["L0_ffn_down_scales"], np.float32).T[:, perm])
    cstv = np.zeros(12, np.float32)
    cstv[C_ONE] = 1.0
    cstv[C_NEG1] = -1.0
    cstv[C_NEGHALF] = -0.5
    cstv[C_INV2048] = 1.0 / 2048.0
    cstv[C_RMSEPS] = RMS_EPS
    cstv[C_LNEPS] = LN_EPS
    cstv[C_7P5] = 7.5
    cstv[C_N8] = -8.0
    cstv[C_N10] = -10.0
    d["cst_in"] = np.tile(cstv, (128, 1))
    d["ones1"] = np.ones((1, 128), np.float32)
    d["onesb"] = np.ones((128, 1), ml_dtypes.bfloat16)
    ind = np.zeros((128, 8, 128), np.float32)
    for bblk in range(8):
        ind[:, bblk, 16 * bblk:16 * bblk + 16] = 1.0 / 256.0
    d["ind_in"] = ind.reshape(128, 1024).astype(ml_dtypes.bfloat16)
    d["identb_in"] = np.eye(128, dtype=ml_dtypes.bfloat16)
    d["identf_in"] = np.eye(128, dtype=np.float32)
    d["_perm"] = perm
    return d


def prep_core_x(inp, core, perm):
    x = np.asarray(inp["L0_attn_x"], np.float32)[core]
    return np.ascontiguousarray(x.T[perm])


def unpermute_out(res_out, perm):
    inv = np.empty_like(perm)
    inv[perm] = np.arange(D)
    return np.ascontiguousarray(np.asarray(res_out, np.float32)[:, inv])


def kernel(**inputs):
    from concourse.bass_utils import run_bass_kernel_spmd
    shared = prep_shared(inputs)
    perm = shared.pop("_perm")
    nc = build_nc(debug=False)
    in_maps = []
    for core in range(8):
        m = dict(shared)
        m["x_dev"] = prep_core_x(inputs, core, perm)
        in_maps.append(m)
    res = run_bass_kernel_spmd(nc, in_maps, core_ids=list(range(8)))
    out = np.stack([unpermute_out(r["out_dev"], perm) for r in res.results], axis=0)
    return out


# version 09
# speedup vs baseline: 1.3187x; 1.0150x over previous
import copy
import numpy as np
import ml_dtypes
import concourse.bass as bass
import concourse.mybir as mybir
from concourse.alu_op_type import AluOpType as OP
from contextlib import ExitStack

F32 = mybir.dt.float32
BF16 = mybir.dt.bfloat16
I32 = mybir.dt.int32
U8 = mybir.dt.uint8
AF = mybir.ActivationFunctionType

D = 2048
T = 4096
DI = 4096
NM = 16384
KC = D // 128
FCH = 70
ICH = DI // 128
CCH = NM // 128
CT = 512
NCH = T // CT
NT = T // CT
RMS_EPS = 1.1920929e-07
LN_EPS = 1e-5

C_ONE, C_NEG1, C_NEGHALF, C_INV2048, C_RMSEPS, C_LNEPS, C_7P5, C_N8, C_N10 = range(9)


class Ctx:
    DMA_SEMS = None

    def __init__(self, src=None):
        object.__setattr__(self, "c", dict(src.c) if src is not None else {})

    def __getattr__(self, k):
        if k == "c":
            raise AttributeError(k)
        return self.c.get(k, 0)

    def __setattr__(self, k, v):
        self.c[k] = v


OVERSYNC = False
P6LITE = 0

def build_nc(debug=False, nphases=8):
    nc = bass.Bass("TRN2", target_bir_lowering=False)

    x_dev = nc.dram_tensor("x_dev", [D, T], F32, kind="ExternalInput")
    w_inT = nc.dram_tensor("w_inT", [D, FCH * 128], F32, kind="ExternalInput")
    w_outT = nc.dram_tensor("w_outT", [DI, D], F32, kind="ExternalInput")
    w_h2i = nc.dram_tensor("w_h2i", [256, DI], F32, kind="ExternalInput")
    wnorm = nc.dram_tensor("wnorm", [128, KC], F32, kind="ExternalInput")
    dtb = nc.dram_tensor("dtb", [128, 2], F32, kind="ExternalInput")
    alog = nc.dram_tensor("alog", [128, 2], F32, kind="ExternalInput")
    wcv = nc.dram_tensor("wcv", [128, 128], F32, kind="ExternalInput")
    gate_pT = nc.dram_tensor("gate_pT", [1024, NM], I32, kind="ExternalInput")
    up_pT = nc.dram_tensor("up_pT", [1024, NM], I32, kind="ExternalInput")
    down_pT = nc.dram_tensor("down_pT", [8192, D], I32, kind="ExternalInput")
    gate_scT = nc.dram_tensor("gate_scT", [64, NM], F32, kind="ExternalInput")
    up_scT = nc.dram_tensor("up_scT", [64, NM], F32, kind="ExternalInput")
    down_scT = nc.dram_tensor("down_scT", [512, D], F32, kind="ExternalInput")
    cst_in = nc.dram_tensor("cst_in", [128, 12], F32, kind="ExternalInput")
    ones1 = nc.dram_tensor("ones1", [1, 128], F32, kind="ExternalInput")
    onesb = nc.dram_tensor("onesb", [128, 1], BF16, kind="ExternalInput")
    ind_in = nc.dram_tensor("ind_in", [128, 1024], BF16, kind="ExternalInput")
    identb_in = nc.dram_tensor("identb_in", [128, 128], BF16, kind="ExternalInput")
    identf_in = nc.dram_tensor("identf_in", [128, 128], F32, kind="ExternalInput")

    out_dev = nc.dram_tensor("out_dev", [T, D], F32, kind="ExternalOutput")

    spill = "ExternalOutput" if debug else "Internal"
    proj_h = nc.dram_tensor("proj_h", [FCH * 128, T], BF16, kind=spill)
    yg_h = nc.dram_tensor("yg_h", [DI, T], BF16, kind=spill)
    attn_h = nc.dram_tensor("attn_h", [D, T], F32, kind=spill)
    x2_h = nc.dram_tensor("x2_h", [D, T], BF16, kind=spill)
    inter_h = nc.dram_tensor("inter_h", [NM, T], BF16, kind=spill)
    ddec_h = nc.dram_tensor("ddec_h", [NM, D], BF16, kind=spill)
    if debug:
        dbg_dt = nc.dram_tensor("dbg_dt", [128, NCH * 1024], F32, kind="ExternalOutput")
        dbg_dA = nc.dram_tensor("dbg_dA", [128, NCH * 1024], F32, kind="ExternalOutput")
        dbg_U = nc.dram_tensor("dbg_U", [128, NCH * 1024], F32, kind="ExternalOutput")
        dbg_H = nc.dram_tensor("dbg_H", [128, NCH * 1024], F32, kind="ExternalOutput")
        dbg_Yt = nc.dram_tensor("dbg_Yt", [128, NCH * 1024], BF16, kind="ExternalOutput")
        dbg_r6 = nc.dram_tensor("dbg_r6", [128, NT * 8192], BF16, kind="ExternalOutput")
        dbg_st6 = nc.dram_tensor("dbg_st6", [128, NT * 8], F32, kind="ExternalOutput")
    scgr_h = nc.dram_tensor("scgr_h", [1024, NM], BF16, kind="Internal")
    scur_h = nc.dram_tensor("scur_h", [1024, NM], BF16, kind="Internal")
    scdr_h = nc.dram_tensor("scdr_h", [8192, D], BF16, kind="Internal")

    ctx = Ctx()
    _open = []
    stp = ExitStack()
    sbp = lambda n_, s_, d_: stp.enter_context(nc.sbuf_tensor(n_, s_, d_))

    s_cst = sbp("s_cst", [128, 12], F32)
    s_ones1 = sbp("s_ones1", [1, 128], F32)
    s_onesb = sbp("s_onesb", [128, 1], BF16)
    s_identb = sbp("s_identb", [128, 128], BF16)
    s_ind = sbp("s_ind", [128, 1024], BF16)
    s_identf = sbp("s_identf", [128, 128], F32)
    s_wn = sbp("s_wn", [128, KC], F32)
    s_dtb = sbp("s_dtb", [128, 2], F32)
    s_nA = sbp("s_nA", [128, 2], F32)
    s_wcv = sbp("s_wcv", [128, 128], F32)
    s_carry = sbp("s_carry", [128, 2], F32)

    DMA_SEMS = ["ld", "scexp", "x1l", "wc0", "wc1", "pst0", "pst1",
                "dbc0", "dbc1", "xil0", "xil1", "zl", "h2i", "ygst", "dbgst",
                "ygr", "wo0", "wo1", "ast0", "ast1", "p4l", "x2st",
                "p5g", "p5g0", "p5g1", "x2r", "intst", "ddst",
                "x2t", "inA", "inB", "ddl0", "ddl1", "ost0", "ost1"]
    ENG_SEMS = ["pe", "v", "s", "p"]
    Ctx.DMA_SEMS = set(DMA_SEMS)
    sem = {n: stp.enter_context(nc.semaphore("sem_" + n)) for n in DMA_SEMS + ENG_SEMS}

    def cst(i):
        return s_cst[:, i:i + 1]

    def cst1(i):
        return s_cst[0:1, i:i + 1]

    def barrier(eng, c):
        for n, v_ in c.c.items():
            if v_ > 0:
                eng.wait_ge(sem[n], 16 * v_ if n in Ctx.DMA_SEMS else v_)

    with nc.Block("p0") as b:
        @b.sync
        def _(sync):
            for dst, src in [(s_cst, cst_in), (s_ones1, ones1), (s_onesb, onesb),
                             (s_identb, identb_in), (s_identf, identf_in), (s_ind, ind_in),
                             (s_wn, wnorm), (s_dtb, dtb), (s_nA, alog), (s_wcv, wcv)]:
                sync.dma_start(out=dst[:], in_=src[:, :]).then_inc(sem["ld"], 16)
        ctx.ld += 10

        @b.scalar
        def _(scalar):
            scalar.wait_ge(sem["ld"], 16 * ctx.ld)
            scalar.activation(s_nA[:], s_nA[:], AF.Exp).then_inc(sem["s"], 1)
        ctx.s += 1

        @b.vector
        def _(vector):
            vector.wait_ge(sem["s"], ctx.s)
            vector.tensor_single_scalar(s_nA[:], s_nA[:], -1.0, OP.mult).then_inc(sem["v"], 1)
        ctx.v += 1


    if nphases < 2:
        with nc.Block("fin") as b:
            @b.gpsimd
            def _(gpsimd):
                barrier(gpsimd, ctx)
        for _st in reversed(_open):
            _st.close()
        stp.close()
        return nc

    st1 = ExitStack(); _open.append(st1)
    sb1 = lambda n_, s_, d_: st1.enter_context(nc.sbuf_tensor(n_, s_, d_))
    s_xn = sb1("s_xn", [128, KC * T], BF16)
    s_shared = sb1("s_shared", [128, 8192], BF16)
    s_x1b = sb1("s_x1b", [128, KC * CT], BF16)
    s_rstd = sb1("s_rstd", [1, 2 * CT], F32)

    base = Ctx(ctx)
    with nc.Block("p1a") as b:
        sta = ExitStack()
        ps_ss = [sta.enter_context(nc.psum_tensor(f"ps_ss{i}", [1, CT], F32)) for i in range(2)]
        ps_b = [sta.enter_context(nc.psum_tensor(f"ps_b{i}", [128, CT], F32)) for i in range(2)]

        @b.gpsimd
        def _(gpsimd):
            barrier(gpsimd, base)
            for i in range(NCH):
                if i >= 1:
                    gpsimd.wait_ge(sem["v"], base.v + 16 * i)
                gpsimd.dma_start(
                    out=s_x1b[:].rearrange("p (c t) -> p c t", c=KC),
                    in_=x_dev[:, i * CT:(i + 1) * CT].rearrange("(c p) t -> p c t", p=128),
                ).then_inc(sem["x1l"], 16)
            gpsimd.dma_start(
                out=scgr_h[:, :].rearrange("(j r) n -> j r n", r=16),
                in_=gate_scT[:, :].unsqueeze(1).broadcast_to([64, 16, NM])).then_inc(sem["scexp"], 16)
            gpsimd.dma_start(
                out=scur_h[:, :].rearrange("(j r) n -> j r n", r=16),
                in_=up_scT[:, :].unsqueeze(1).broadcast_to([64, 16, NM])).then_inc(sem["scexp"], 16)
            gpsimd.dma_start(
                out=scdr_h[:, :].rearrange("(j r) n -> j r n", r=16),
                in_=down_scT[:, :].unsqueeze(1).broadcast_to([512, 16, D])).then_inc(sem["scexp"], 16)
        ctx.x1l += NCH
        ctx.scexp += 3

        @b.scalar
        def _(scalar):
            barrier(scalar, base)
            for i in range(NCH):
                scalar.wait_ge(sem["x1l"], 16 * (base.x1l + i + 1))
                if i >= 1:
                    scalar.wait_ge(sem["pe"], base.pe + 2 * i - 1)
                scalar.activation(s_shared[:], s_x1b[:], AF.Square).then_inc(sem["s"], 1)
                scalar.wait_ge(sem["pe"], base.pe + 2 * i + 1)
                sl = s_rstd[0:1, (i % 2) * CT:(i % 2 + 1) * CT]
                scalar.activation(sl, ps_ss[i % 2][0:1, :], AF.Ln,
                                  bias=cst1(C_RMSEPS), scale=cst1(C_INV2048)).then_inc(sem["s"], 1)
                scalar.activation(sl, sl, AF.Exp, scale=cst1(C_NEGHALF)).then_inc(sem["s"], 1)
        ctx.s += 3 * NCH

        @b.tensor
        def _(tensor):
            barrier(tensor, base)
            for i in range(NCH):
                tensor.wait_ge(sem["s"], base.s + 3 * i + 1)
                if i >= 2:
                    tensor.wait_ge(sem["s"], base.s + 3 * (i - 2) + 2)
                for k in range(KC):
                    mm = tensor.matmul(ps_ss[i % 2][0:1, :], s_onesb[:, 0:1],
                                       s_shared[:, k * CT:(k + 1) * CT],
                                       start=(k == 0), stop=(k == KC - 1))
                mm.then_inc(sem["pe"], 1)
                tensor.wait_ge(sem["s"], base.s + 3 * i + 3)
                if i >= 2:
                    tensor.wait_ge(sem["v"], base.v + 16 * (i - 1))
                tensor.matmul(ps_b[i % 2][:, :], s_ones1[0:1, :],
                              s_rstd[0:1, (i % 2) * CT:(i % 2 + 1) * CT],
                              start=True, stop=True).then_inc(sem["pe"], 1)
        ctx.pe += 2 * NCH

        @b.vector
        def _(vector):
            barrier(vector, base)
            for i in range(NCH):
                vector.wait_ge(sem["pe"], base.pe + 2 * i + 2)
                for k in range(KC):
                    vector.scalar_tensor_tensor(
                        s_xn[:, k * T + i * CT:k * T + (i + 1) * CT],
                        s_x1b[:, k * CT:(k + 1) * CT], s_wn[:, k:k + 1],
                        ps_b[i % 2][:, :], OP.mult, OP.mult).then_inc(sem["v"], 1)
        ctx.v += 16 * NCH
        sta.close()

    if nphases < 3:
        with nc.Block("fin") as b:
            @b.gpsimd
            def _(gpsimd):
                barrier(gpsimd, ctx)
        for _st in reversed(_open):
            _st.close()
        stp.close()
        return nc

    s_wcol = [sb1(f"s_wcol{i}", [128, KC * 128], BF16) for i in range(2)]
    base = Ctx(ctx)
    base.scexp = 0
    with nc.Block("p1b") as b:
        sta = ExitStack()
        ps1 = [sta.enter_context(nc.psum_tensor(f"ps1_{i}", [128, CT], F32)) for i in range(8)]

        @b.gpsimd
        def _(gpsimd):
            for f in range(FCH):
                if f >= 2:
                    gpsimd.wait_ge(sem["pe"], base.pe + (f - 1) * NT)
                gpsimd.dma_start(
                    out=s_wcol[f % 2][:].rearrange("p (c f) -> p c f", f=128),
                    in_=w_inT[:, f * 128:(f + 1) * 128].rearrange("(c p) f -> p c f", p=128),
                ).then_inc(sem["wc0" if f % 2 == 0 else "wc1"], 16)
        ctx.wc0 += (FCH + 1) // 2
        ctx.wc1 += FCH // 2

        @b.tensor
        def _(tensor):
            barrier(tensor, base)
            for f in range(FCH):
                wsem = "wc0" if f % 2 == 0 else "wc1"
                tensor.wait_ge(sem[wsem], 16 * (getattr(base, wsem) + f // 2 + 1))
                for t in range(NT):
                    if f >= 1:
                        if t % 2 == 0:
                            tensor.wait_ge(sem["v"], base.v + (f - 1) * 4 + t // 2 + 1)
                        else:
                            tensor.wait_ge(sem["s"], base.s + (f - 1) * 4 + (t - 1) // 2 + 1)
                    for k in range(KC):
                        mm = tensor.matmul(ps1[t][:, :],
                                           s_wcol[f % 2][:, k * 128:(k + 1) * 128],
                                           s_xn[:, k * T + t * CT:k * T + (t + 1) * CT],
                                           start=(k == 0), stop=(k == KC - 1))
                    mm.then_inc(sem["pe"], 1)
        ctx.pe += FCH * NT

        @b.vector
        def _(vector):
            barrier(vector, base)
            for f in range(FCH):
                if f >= 2:
                    psem = "pst0" if f % 2 == 0 else "pst1"
                    vector.wait_ge(sem[psem], 16 * (getattr(base, psem) + (f - 2) // 2 + 1))
                for t in range(0, NT, 2):
                    vector.wait_ge(sem["pe"], base.pe + f * NT + t + 1)
                    vector.tensor_copy(
                        s_shared[:, (f % 2) * 4096 + t * CT:(f % 2) * 4096 + (t + 1) * CT],
                        ps1[t][:, :]).then_inc(sem["v"], 1)
        ctx.v += FCH * 4

        @b.scalar
        def _(scalar):
            barrier(scalar, base)
            for f in range(FCH):
                if f >= 2:
                    psem = "pst0" if f % 2 == 0 else "pst1"
                    scalar.wait_ge(sem[psem], 16 * (getattr(base, psem) + (f - 2) // 2 + 1))
                for t in range(1, NT, 2):
                    scalar.wait_ge(sem["pe"], base.pe + f * NT + t + 1)
                    scalar.copy(
                        s_shared[:, (f % 2) * 4096 + t * CT:(f % 2) * 4096 + (t + 1) * CT],
                        ps1[t][:, :]).then_inc(sem["s"], 1)
        ctx.s += FCH * 4

        @b.sync
        def _(sync):
            barrier(sync, base)
            for f in range(FCH):
                sync.wait_ge(sem["v"], base.v + (f + 1) * 4)
                sync.wait_ge(sem["s"], base.s + (f + 1) * 4)
                sync.dma_start(out=proj_h[f * 128:(f + 1) * 128, :],
                               in_=s_shared[:, (f % 2) * 4096:(f % 2) * 4096 + 4096],
                               ).then_inc(sem["pst0" if f % 2 == 0 else "pst1"], 16)
        ctx.pst0 += (FCH + 1) // 2
        ctx.pst1 += FCH // 2
        sta.close()
    st1.close(); _open.remove(st1)

    if nphases < 4:
        with nc.Block("fin") as b:
            @b.gpsimd
            def _(gpsimd):
                barrier(gpsimd, ctx)
        for _st in reversed(_open):
            _st.close()
        stp.close()
        return nc

    st2 = ExitStack(); _open.append(st2)
    sb2 = lambda n_, s_, d_: st2.enter_context(nc.sbuf_tensor(n_, s_, d_))
    s_h2i = sb2("s_h2i", [128, 2 * DI], BF16)
    s_xr = [sb2(f"s_xr{i}", [128, ICH * 520], BF16) for i in range(2)]
    s_zr = sb2("s_zr", [128, 16 * CT], BF16)
    s_dbc = [sb2(f"s_dbc{i}", [128, 3072], BF16) for i in range(2)]
    s_dt = sb2("s_dt", [128, 1024], F32)
    s_dt2 = sb2("s_dt2", [128, 1024], F32)
    s_dt3 = sb2("s_dt3", [128, 1024], F32)
    s_dA = sb2("s_dA", [128, 1024], F32)
    s_du = s_dt3
    s_U = sb2("s_U", [128, 1024], F32)
    s_H = sb2("s_H", [128, 1024], F32)
    s_Yt = sb2("s_Yt", [128, 1024], BF16)
    s_xc = sb2("s_xc", [128, 2 * CT], F32)
    s_sil = sb2("s_sil", [128, 4 * CT], BF16)
    s_silz = sb2("s_silz", [128, 2 * CT], BF16)
    s_yg = sb2("s_yg", [128, ICH * CT], BF16)
    s_wo3 = [sb2(f"s_wo3_{i}", [128, ICH * 128], BF16) for i in range(2)]
    s_evf = sb2("s_evf", [128, 4 * CT], F32)

    VP, SP, PP = 80, 74, 80
    V_DTPRE, V_CONV0, V_DT, V_DTA, V_DUU, V_SCAN0, V_CARRY, V_YT, V_YG0 = \
        1, 2, 34, 35, 36, 37, 39, 40, 41
    S_SP, S_SIL0, S_DA, S_SILZ0 = 1, 2, 34, 35

    base = Ctx(ctx)
    with nc.Block("p2") as b:
        sta = ExitStack()
        ps_xs = [sta.enter_context(nc.psum_tensor(f"ps_xs{i}", [128, CT], F32)) for i in range(2)]
        ps_y = [sta.enter_context(nc.psum_tensor(f"ps_y{i}", [128, CT], F32)) for i in range(4)]
        ps3f = [sta.enter_context(nc.psum_tensor(f"ps3f{i}", [128, CT], F32)) for i in range(2)]

        projr = proj_h[:, :].rearrange("(c p) t -> p c t", p=128)

        def vat(i, m):
            return base.v + i * VP + m

        def sat(i, m):
            return base.s + i * SP + m

        def pat(i, m):
            return base.pe + i * PP + m

        @b.gpsimd
        def _(gpsimd):
            barrier(gpsimd, base)
            gpsimd.dma_start(out=s_h2i[:].rearrange("p (c f) -> p c f", f=DI),
                             in_=w_h2i[:, :].rearrange("(c p) f -> p c f", p=128)
                             ).then_inc(sem["h2i"], 16)
            for i in range(NCH):
                for dc in range(16):
                    g = i * 16 + dc
                    if g >= 2:
                        pi, pdc = (g - 2) // 16, (g - 2) % 16
                        gpsimd.wait_ge(sem["pe"], pat(pi, 64 + pdc + 1))
                    gpsimd.dma_start(
                        out=s_wo3[dc % 2][:].rearrange("p (c f) -> p c f", f=128),
                        in_=w_outT[:, dc * 128:(dc + 1) * 128].rearrange("(c p) f -> p c f", p=128),
                    ).then_inc(sem["wo0" if dc % 2 == 0 else "wo1"], 16)
        ctx.h2i += 1
        ctx.wo0 += NCH * 8
        ctx.wo1 += NCH * 8

        def attn_store(sync, j):
            for dc in range(16):
                if dc % 2 == 0:
                    sync.wait_ge(sem["v"], vat(j, 73 + dc // 2))
                else:
                    sync.wait_ge(sem["s"], sat(j, 67 + (dc - 1) // 2))
                sync.dma_start(out=attn_h[dc * 128:(dc + 1) * 128, j * CT:(j + 1) * CT],
                               in_=s_evf[:, (dc % 4) * CT:(dc % 4 + 1) * CT]
                               ).then_inc(sem["ast0" if dc % 2 == 0 else "ast1"], 16)

        @b.sync
        def _(sync):
            barrier(sync, base)
            for i in range(NCH):
                if i >= 1:
                    attn_store(sync, i - 1)
                ldb = base.ld + i * 3
                if i >= 2:
                    sync.wait_ge(sem["v"], vat(i - 2, V_YT))
                sync.dma_start(out=s_dbc[i % 2][:].rearrange("p (r c t) -> p r c t", r=3, c=2),
                               in_=projr[:, 64:70, i * CT:(i + 1) * CT].rearrange(
                                   "p (r c) t -> p r c t", r=3)
                               ).then_inc(sem["dbc0" if i % 2 == 0 else "dbc1"], 16)
                t0 = i * CT
                xrb = s_xr[i % 2][:].rearrange("p (g t) -> p g t", g=ICH)
                if i >= 2:
                    sync.wait_ge(sem["v"], vat(i - 2, V_CONV0 + ICH - 1))
                xsm = "xil0" if i % 2 == 0 else "xil1"
                if t0 == 0:
                    sync.dma_start(out=xrb[:, :, 3:515],
                                   in_=projr[:, 32:64, 0:CT]).then_inc(sem[xsm], 16)
                else:
                    sync.dma_start(out=xrb[:, :, 0:515],
                                   in_=projr[:, 32:64, t0 - 3:t0 + CT]).then_inc(sem[xsm], 16)
                if i >= 1:
                    sync.wait_ge(sem["s"], sat(i - 1, S_SILZ0 + ICH - 1))
                sync.dma_start(out=s_zr[:].rearrange("p (g t) -> p g t", g=16),
                               in_=projr[:, 0:16, i * CT:(i + 1) * CT]).then_inc(sem["zl"], 16)
                sync.wait_ge(sem["s"], sat(i, S_SILZ0 + 15))
                sync.dma_start(out=s_zr[:].rearrange("p (g t) -> p g t", g=16),
                               in_=projr[:, 16:32, i * CT:(i + 1) * CT]).then_inc(sem["zl"], 16)
            attn_store(sync, NCH - 1)
        ctx.dbc0 += (NCH + 1) // 2
        ctx.dbc1 += NCH // 2
        ctx.xil0 += (NCH + 1) // 2
        ctx.xil1 += NCH // 2
        ctx.zl += 2 * NCH
        ctx.ast0 += NCH * 8
        ctx.ast1 += NCH * 8

        @b.vector
        def _(vector):
            barrier(vector, base)
            for g in range(ICH):
                vector.memset(s_xr[0][:, g * 520:g * 520 + 3], 0.0)
            for i in range(NCH):
                ldb = base.ld + i * 3
                dsm = "dbc0" if i % 2 == 0 else "dbc1"
                vector.wait_ge(sem[dsm], 16 * (getattr(base, dsm) + i // 2 + 1))
                if debug and i > 0:
                    vector.wait_ge(sem["dbgst"], 16 * (base.dbgst + i * 5))
                for h in range(2):
                    op = vector.tensor_scalar(
                        s_dt[:, h * CT:(h + 1) * CT],
                        s_dbc[i % 2][:, h * CT:(h + 1) * CT], s_dtb[:, h:h + 1], None, OP.add)
                op.then_inc(sem["v"], 1)
                xsm = "xil0" if i % 2 == 0 else "xil1"
                vector.wait_ge(sem[xsm], 16 * (getattr(base, xsm) + i // 2 + 1))
                for g in range(ICH):
                    gi = i * ICH + g
                    if gi >= 2:
                        pi, pg = (gi - 2) // ICH, (gi - 2) % ICH
                        vector.wait_ge(sem["s"], sat(pi, S_SIL0 + pg))
                    sl = s_xr[i % 2][:, g * 520:g * 520 + 515]
                    xcs = s_xc[:, (g % 2) * CT:(g % 2 + 1) * CT]
                    w4 = lambda k: s_wcv[:, 4 * g + k:4 * g + k + 1]
                    vector.tensor_scalar(xcs, sl[:, 0:CT], w4(0), None, OP.mult)
                    vector.scalar_tensor_tensor(xcs, sl[:, 1:1 + CT], w4(1), xcs, OP.mult, OP.add)
                    vector.scalar_tensor_tensor(xcs, sl[:, 2:2 + CT], w4(2), xcs, OP.mult, OP.add)
                    vector.scalar_tensor_tensor(xcs, sl[:, 3:3 + CT], w4(3), xcs, OP.mult, OP.add
                                                ).then_inc(sem["v"], 1)
                vector.wait_ge(sem["s"], sat(i, S_SP))
                vector.tensor_tensor(s_dt[:], s_dt2[:], s_dt3[:], OP.add).then_inc(sem["v"], 1)
                for h in range(2):
                    op = vector.tensor_scalar(s_dt2[:, h * CT:(h + 1) * CT],
                                              s_dt[:, h * CT:(h + 1) * CT],
                                              s_nA[:, h:h + 1], None, OP.mult)
                op.then_inc(sem["v"], 1)
                for h in range(2):
                    vector.wait_ge(sem["pe"], pat(i, 16 * (h + 1)))
                    vector.tensor_tensor(s_du[:, h * CT:(h + 1) * CT],
                                         s_dt[:, h * CT:(h + 1) * CT], ps_xs[h][:, :], OP.mult)
                    op = vector.tensor_tensor(s_U[:, h * CT:(h + 1) * CT],
                                              s_du[:, h * CT:(h + 1) * CT],
                                              s_dbc[i % 2][:, 1024 + h * CT:1024 + (h + 1) * CT],
                                              OP.mult)
                op.then_inc(sem["v"], 1)
                vector.wait_ge(sem["s"], sat(i, S_DA))
                for h in range(2):
                    init = 0.0 if i == 0 else s_carry[:, h:h + 1]
                    vector.tensor_tensor_scan(s_H[:, h * CT:(h + 1) * CT],
                                              s_dA[:, h * CT:(h + 1) * CT],
                                              s_U[:, h * CT:(h + 1) * CT],
                                              init, OP.mult, OP.add).then_inc(sem["v"], 1)
                for h in range(2):
                    op = vector.tensor_copy(s_carry[:, h:h + 1],
                                            s_H[:, h * CT + CT - 1:h * CT + CT])
                op.then_inc(sem["v"], 1)
                for h in range(2):
                    op = vector.tensor_tensor(s_Yt[:, h * CT:(h + 1) * CT],
                                              s_H[:, h * CT:(h + 1) * CT],
                                              s_dbc[i % 2][:, 2048 + h * CT:2048 + (h + 1) * CT],
                                              OP.mult)
                op.then_inc(sem["v"], 1)
                for ic in range(ICH):
                    vector.wait_ge(sem["pe"], pat(i, 32 + ic + 1))
                    vector.wait_ge(sem["s"], sat(i, S_SILZ0 + ic) if not OVERSYNC else sat(i, SP))
                    if i > 0 and ic == 0:
                        vector.wait_ge(sem["pe"], pat(i - 1, 80))
                    vector.tensor_tensor(s_yg[:, ic * CT:(ic + 1) * CT], ps_y[ic % 4][:, :],
                                         s_silz[:, (ic % 2) * CT:(ic % 2 + 1) * CT], OP.mult
                                         ).then_inc(sem["v"], 1)
                for dc in range(0, 16, 2):
                    vector.wait_ge(sem["pe"], pat(i, 64 + dc + 1))
                    gidx = i * 8 + dc // 2
                    if gidx >= 2:
                        vector.wait_ge(sem["ast0"], 16 * (base.ast0 + gidx - 1))
                    vector.tensor_copy(s_evf[:, (dc % 4) * CT:(dc % 4 + 1) * CT],
                                       ps3f[dc % 2][:, :]).then_inc(sem["v"], 1)
        ctx.v += NCH * VP

        @b.scalar
        def _(scalar):
            barrier(scalar, base)
            for i in range(NCH):
                ldb = base.ld + i * 3
                scalar.wait_ge(sem["v"], vat(i, V_DTPRE))
                scalar.activation(s_dt2[:], s_dt[:], AF.Relu)
                scalar.activation(s_dt3[:], s_dt[:], AF.Abs)
                scalar.activation(s_dt3[:], s_dt3[:], AF.Exp, scale=cst(C_NEG1))
                scalar.activation(s_dt3[:], s_dt3[:], AF.Ln, bias=cst(C_ONE)).then_inc(sem["s"], 1)
                for g in range(ICH):
                    scalar.wait_ge(sem["v"], vat(i, V_CONV0 + g))
                    gi = i * ICH + g
                    if gi >= 4:
                        pi, pg = (gi - 4) // ICH, (gi - 4) % ICH
                        scalar.wait_ge(sem["pe"], pat(pi, pg + 1))
                    scalar.activation(s_sil[:, (g % 4) * CT:(g % 4 + 1) * CT],
                                      s_xc[:, (g % 2) * CT:(g % 2 + 1) * CT],
                                      AF.Silu).then_inc(sem["s"], 1)
                scalar.wait_ge(sem["v"], vat(i, V_DTA))
                if i >= 1:
                    scalar.wait_ge(sem["v"], vat(i - 1, V_SCAN0 + 1))
                scalar.activation(s_dA[:], s_dt2[:], AF.Exp).then_inc(sem["s"], 1)
                scalar.wait_ge(sem["zl"], 16 * (base.zl + 2 * i + 1))
                for ic in range(ICH):
                    if ic == 16:
                        scalar.wait_ge(sem["zl"], 16 * (base.zl + 2 * i + 2))
                    gi = i * ICH + ic
                    if gi >= 2:
                        pi, pg = (gi - 2) // ICH, (gi - 2) % ICH
                        scalar.wait_ge(sem["v"], vat(pi, V_YG0 + pg))
                    scalar.activation(s_silz[:, (ic % 2) * CT:(ic % 2 + 1) * CT],
                                      s_zr[:, (ic % 16) * CT:(ic % 16 + 1) * CT],
                                      AF.Silu).then_inc(sem["s"], 1)
                for dc in range(1, 16, 2):
                    scalar.wait_ge(sem["pe"], pat(i, 64 + dc + 1))
                    gidx = i * 8 + (dc - 1) // 2
                    if gidx >= 2:
                        scalar.wait_ge(sem["ast1"], 16 * (base.ast1 + gidx - 1))
                    scalar.copy(s_evf[:, (dc % 4) * CT:(dc % 4 + 1) * CT],
                                ps3f[dc % 2][:, :]).then_inc(sem["s"], 1)
        ctx.s += NCH * SP

        @b.tensor
        def _(tensor):
            barrier(tensor, base)
            tensor.wait_ge(sem["h2i"], 16 * (base.h2i + 1))
            for i in range(NCH):
                for g in range(ICH):
                    tensor.wait_ge(sem["s"], sat(i, S_SIL0 + g))
                    if g in (0, 16) and i >= 1:
                        tensor.wait_ge(sem["v"], vat(i - 1, V_DUU))
                    mm = tensor.matmul(ps_xs[g // 16][:, :],
                                       s_ind[:, ((g // 2) % 8) * 128:((g // 2) % 8 + 1) * 128],
                                       s_sil[:, (g % 4) * CT:(g % 4 + 1) * CT],
                                       start=(g % 16 == 0), stop=(g % 16 == 15))
                    mm.then_inc(sem["pe"], 1)
                tensor.wait_ge(sem["v"], vat(i, V_YT))
                for ic in range(ICH):
                    gi = i * ICH + ic
                    if gi >= 4:
                        pi, pg = (gi - 4) // ICH, (gi - 4) % ICH
                        tensor.wait_ge(sem["v"], vat(pi, V_YG0 + pg))
                    for h in range(2):
                        mm = tensor.matmul(ps_y[ic % 4][:, :],
                                           s_h2i[:, h * DI + ic * 128:h * DI + (ic + 1) * 128],
                                           s_Yt[:, h * CT:(h + 1) * CT],
                                           start=(h == 0), stop=(h == 1))
                    mm.then_inc(sem["pe"], 1)
                tensor.wait_ge(sem["v"], vat(i, V_YG0 + ICH - 1))
                for dc in range(16):
                    wsm = "wo0" if dc % 2 == 0 else "wo1"
                    tensor.wait_ge(sem[wsm], 16 * (getattr(base, wsm) + i * 8 + dc // 2 + 1))
                    if dc >= 2:
                        if dc % 2 == 0:
                            tensor.wait_ge(sem["v"], vat(i, 73 + (dc - 2) // 2))
                        else:
                            tensor.wait_ge(sem["s"], sat(i, 67 + (dc - 3) // 2))
                    elif i >= 1:
                        if dc == 0:
                            tensor.wait_ge(sem["v"], vat(i - 1, 80))
                        else:
                            tensor.wait_ge(sem["s"], sat(i - 1, 74))
                    for k in range(ICH):
                        mm = tensor.matmul(ps3f[dc % 2][:, :],
                                           s_wo3[dc % 2][:, k * 128:(k + 1) * 128],
                                           s_yg[:, k * CT:(k + 1) * CT],
                                           start=(k == 0), stop=(k == ICH - 1))
                    mm.then_inc(sem["pe"], 1)
        ctx.pe += NCH * PP
        sta.close()
    st2.close(); _open.remove(st2)

    if nphases < 5:
        with nc.Block("fin") as b:
            @b.gpsimd
            def _(gpsimd):
                barrier(gpsimd, ctx)
        for _st in reversed(_open):
            _st.close()
        stp.close()
        return nc

    st4 = ExitStack(); _open.append(st4)
    sb4 = lambda n_, s_, d_: st4.enter_context(nc.sbuf_tensor(n_, s_, d_))
    s_ap4 = sb4("s_ap4", [128, KC * CT], F32)
    s_x14 = sb4("s_x14", [128, KC * CT], F32)
    s_r4 = sb4("s_r4", [128, KC * CT], F32)
    s_rb4 = sb4("s_rb4", [128, KC * CT], BF16)
    s_x24 = sb4("s_x24", [128, KC * CT], BF16)
    s_mu4 = sb4("s_mu4", [1, 2 * CT], F32)
    s_rs4 = sb4("s_rs4", [1, 2 * CT], F32)
    base = Ctx(ctx)
    V4, S4, P4N = 5, 4, 4
    with nc.Block("p4") as b:
        sta = ExitStack()
        ps_s1 = [sta.enter_context(nc.psum_tensor(f"ps4a{i}", [1, CT], F32)) for i in range(2)]
        ps_s2 = [sta.enter_context(nc.psum_tensor(f"ps4b{i}", [1, CT], F32)) for i in range(2)]
        ps_mb = [sta.enter_context(nc.psum_tensor(f"ps4c{i}", [128, CT], F32)) for i in range(2)]
        ps_rb = [sta.enter_context(nc.psum_tensor(f"ps4d{i}", [128, CT], F32)) for i in range(2)]

        @b.sync
        def _(sync):
            barrier(sync, base)
            for i in range(NCH):
                if i >= 1:
                    sync.wait_ge(sem["v"], base.v + (i - 1) * V4 + 2)
                sync.dma_start(out=s_ap4[:].rearrange("p (c t) -> p c t", c=KC),
                               in_=attn_h[:, i * CT:(i + 1) * CT].rearrange("(c p) t -> p c t", p=128),
                               ).then_inc(sem["p4l"], 16)
                sync.dma_start(out=s_x14[:].rearrange("p (c t) -> p c t", c=KC),
                               in_=x_dev[:, i * CT:(i + 1) * CT].rearrange("(c p) t -> p c t", p=128),
                               ).then_inc(sem["p4l"], 16)
        ctx.p4l += 2 * NCH

        @b.vector
        def _(vector):
            barrier(vector, base)
            for i in range(NCH):
                vector.wait_ge(sem["p4l"], 16 * (base.p4l + 2 * i + 2))
                vector.tensor_tensor(s_r4[:], s_x14[:], s_ap4[:], OP.add).then_inc(sem["v"], 1)
                if i >= 1:
                    vector.wait_ge(sem["pe"], base.pe + (i - 1) * P4N + 2)
                vector.tensor_copy(s_rb4[:], s_r4[:]).then_inc(sem["v"], 1)
                mu = s_mu4[0:1, (i % 2) * CT:(i % 2 + 1) * CT]
                rs = s_rs4[0:1, (i % 2) * CT:(i % 2 + 1) * CT]
                vector.wait_ge(sem["s"], base.s + i * S4 + 2)
                vector.tensor_tensor(rs, mu, mu, OP.mult).then_inc(sem["v"], 1)
                vector.wait_ge(sem["pe"], base.pe + i * P4N + 2)
                vector.scalar_tensor_tensor(rs, ps_s2[i % 2][0:1, :], 1.0 / 2048.0, rs,
                                            OP.mult, OP.subtract).then_inc(sem["v"], 1)
                vector.wait_ge(sem["pe"], base.pe + i * P4N + 4)
                if i >= 1:
                    vector.wait_ge(sem["x2st"], 16 * (base.x2st + i))
                for k in range(KC):
                    vector.tensor_tensor(s_r4[:, k * CT:(k + 1) * CT], s_r4[:, k * CT:(k + 1) * CT],
                                         ps_mb[i % 2][:, :], OP.subtract)
                    op = vector.tensor_tensor(s_x24[:, k * CT:(k + 1) * CT],
                                              s_r4[:, k * CT:(k + 1) * CT],
                                              ps_rb[i % 2][:, :], OP.mult)
                op.then_inc(sem["v"], 1)
        ctx.v += NCH * V4

        @b.scalar
        def _(scalar):
            barrier(scalar, base)
            for i in range(NCH):
                scalar.wait_ge(sem["v"], base.v + i * V4 + 2)
                scalar.wait_ge(sem["pe"], base.pe + i * P4N + 1)
                scalar.activation(s_rb4[:], s_rb4[:], AF.Square).then_inc(sem["s"], 1)
                mu = s_mu4[0:1, (i % 2) * CT:(i % 2 + 1) * CT]
                rs = s_rs4[0:1, (i % 2) * CT:(i % 2 + 1) * CT]
                scalar.activation(mu, ps_s1[i % 2][0:1, :], AF.Identity,
                                  scale=cst1(C_INV2048)).then_inc(sem["s"], 1)
                scalar.wait_ge(sem["v"], base.v + i * V4 + 4)
                scalar.activation(rs, rs, AF.Ln, bias=cst1(C_LNEPS)).then_inc(sem["s"], 1)
                scalar.activation(rs, rs, AF.Exp, scale=cst1(C_NEGHALF)).then_inc(sem["s"], 1)
        ctx.s += NCH * S4

        @b.tensor
        def _(tensor):
            barrier(tensor, base)
            for i in range(NCH):
                tensor.wait_ge(sem["v"], base.v + i * V4 + 2)
                if i >= 2:
                    tensor.wait_ge(sem["s"], base.s + (i - 2) * S4 + 2)
                for k in range(KC):
                    mm = tensor.matmul(ps_s1[i % 2][0:1, :], s_onesb[:, 0:1],
                                       s_rb4[:, k * CT:(k + 1) * CT],
                                       start=(k == 0), stop=(k == KC - 1))
                mm.then_inc(sem["pe"], 1)
                tensor.wait_ge(sem["s"], base.s + i * S4 + 1)
                if i >= 2:
                    tensor.wait_ge(sem["v"], base.v + (i - 2) * V4 + 4)
                for k in range(KC):
                    mm = tensor.matmul(ps_s2[i % 2][0:1, :], s_onesb[:, 0:1],
                                       s_rb4[:, k * CT:(k + 1) * CT],
                                       start=(k == 0), stop=(k == KC - 1))
                mm.then_inc(sem["pe"], 1)
                tensor.wait_ge(sem["s"], base.s + i * S4 + 2)
                if i >= 2:
                    tensor.wait_ge(sem["v"], base.v + (i - 2) * V4 + 5)
                tensor.matmul(ps_mb[i % 2][:, :], s_ones1[0:1, :],
                              s_mu4[0:1, (i % 2) * CT:(i % 2 + 1) * CT],
                              start=True, stop=True).then_inc(sem["pe"], 1)
                tensor.wait_ge(sem["s"], base.s + i * S4 + 4)
                if i >= 2:
                    tensor.wait_ge(sem["v"], base.v + (i - 2) * V4 + 5)
                tensor.matmul(ps_rb[i % 2][:, :], s_ones1[0:1, :],
                              s_rs4[0:1, (i % 2) * CT:(i % 2 + 1) * CT],
                              start=True, stop=True).then_inc(sem["pe"], 1)
        ctx.pe += NCH * P4N

        @b.gpsimd
        def _(gpsimd):
            barrier(gpsimd, base)
            for i in range(NCH):
                gpsimd.wait_ge(sem["v"], base.v + (i + 1) * V4)
                gpsimd.dma_start(out=x2_h[:, i * CT:(i + 1) * CT].rearrange("(c p) t -> p c t", p=128),
                                 in_=s_x24[:].rearrange("p (c t) -> p c t", c=KC)).then_inc(sem["x2st"], 16)
        ctx.x2st += NCH
        sta.close()
    st4.close(); _open.remove(st4)

    if nphases < 7:
        with nc.Block("fin") as b:
            @b.gpsimd
            def _(gpsimd):
                barrier(gpsimd, ctx)
        for _st in reversed(_open):
            _st.close()
        stp.close()
        return nc

    st5 = ExitStack(); _open.append(st5)
    sb5 = lambda n_, s_, d_: st5.enter_context(nc.sbuf_tensor(n_, s_, d_))
    s_x2r = sb5("s_x2r", [128, KC * T], BF16)
    s_pkg = sb5("s_pkg", [128, 1024], U8)
    s_pku = sb5("s_pku", [128, 1024], U8)
    s_pkd = sb5("s_pkd", [128, 2048], U8)
    s_scg = sb5("s_scg", [128, 1024], BF16)
    s_scu = sb5("s_scu", [128, 1024], BF16)
    s_scd = sb5("s_scd", [128, 2048], BF16)
    s_dg = [sb5(f"s_dg{i}", [128, 2048], BF16) for i in range(2)]
    s_dup = [sb5(f"s_dup{i}", [128, 2048], BF16) for i in range(2)]
    s_dd = sb5("s_dd", [128, 4096], BF16)
    s_gel = sb5("s_gel", [128, 4 * CT], BF16)
    s_int = sb5("s_int", [128, T], BF16)
    s_j7 = sb5("s_j7", [128, 1024], BF16)
    s_sg5 = sb5("s_sg5", [128, 1024], BF16)
    s_t1 = sb5("s_t1", [128, 1024], BF16)
    s_t2 = sb5("s_t2", [128, 1024], BF16)
    s_i1 = sb5("s_i1", [128, 1024], U8)

    def it5(c):
        return (4 if c == 0 else 0) + (4 if c < CCH - 1 else 0) + 2 + 8

    VOFF2 = [0]
    for c in range(CCH):
        VOFF2.append(VOFF2[-1] + it5(c))

    def dpos(c):
        return VOFF2[c] + (4 if c == 0 else 0)

    def decg_done(c):
        return VOFF2[0] + 2 if c == 0 else dpos(c - 1) + 2

    def decu_done(c):
        return VOFF2[0] + 4 if c == 0 else dpos(c - 1) + 4

    def ddpos(c):
        return VOFF2[c] + (4 if c == 0 else 0) + (4 if c < CCH - 1 else 0)

    def ipos(c):
        return VOFF2[c] + it5(c) - 8

    def gcnt(c):
        return 4 + (2 if c % 2 == 0 else 0)

    GOFF = [0]
    for c in range(CCH):
        GOFF.append(GOFF[-1] + gcnt(c))

    def stcnt(c):
        return 1 + (2 if c < 64 else 0)

    STOFF = [0]
    for c in range(CCH):
        STOFF.append(STOFF[-1] + stcnt(c))

    def decode_half(vector, pk_ap, shift, sc_ap, out_ap, width):
        j = s_i1[:, 0:width]
        if shift:
            vector.tensor_single_scalar(j, pk_ap, shift, OP.logical_shift_right)
            vector.tensor_single_scalar(j, j, 7, OP.bitwise_and)
        else:
            vector.tensor_single_scalar(j, pk_ap, 7, OP.bitwise_and)
        jb = s_j7[:, 0:width]
        vector.tensor_copy(jb, j)
        vector.tensor_single_scalar(j, pk_ap, shift + 3, OP.logical_shift_right)
        vector.tensor_single_scalar(j, j, 1, OP.bitwise_and)
        vector.tensor_scalar(s_sg5[:, 0:width], j, -2.0, 1.0, OP.mult, OP.add)
        vector.tensor_scalar_max(s_t1[:, 0:width], jb, 4.0)
        vector.tensor_scalar_max(s_t2[:, 0:width], jb, 6.0)
        vector.scalar_tensor_tensor(s_t1[:, 0:width], s_t1[:, 0:width], 0.5,
                                    s_t2[:, 0:width], OP.mult, OP.add)
        vector.scalar_tensor_tensor(s_t1[:, 0:width], jb, 0.5, s_t1[:, 0:width], OP.mult, OP.add)
        vector.scalar_tensor_tensor(s_t1[:, 0:width], s_t1[:, 0:width], -8.0,
                                    s_sg5[:, 0:width], OP.add, OP.mult)
        return vector.tensor_tensor(out_ap, s_t1[:, 0:width], sc_ap, OP.mult)

    base = Ctx(ctx)
    with nc.Block("p5") as b:
        sta = ExitStack()
        ps5 = [sta.enter_context(nc.psum_tensor(f"ps5_{i}", [128, CT], F32)) for i in range(8)]

        @b.sync
        def _(sync):
            barrier(sync, base)
            sync.dma_start(out=s_x2r[:].rearrange("p (c t) -> p c t", c=KC),
                           in_=x2_h[:, :].rearrange("(c p) t -> p c t", p=128)).then_inc(sem["x2r"], 16)
            for c in range(CCH):
                if c % 2 == 1:
                    j = c // 2
                    sync.wait_ge(sem["v"], base.v + ddpos(c) + 2)
                    sync.dma_start(out=ddec_h[j * 128:(j + 1) * 128, :],
                                   in_=s_dd[:, 0:2048]).then_inc(sem["ddst"], 16)
                    sync.dma_start(out=ddec_h[(64 + j) * 128:(64 + j + 1) * 128, :],
                                   in_=s_dd[:, 2048:4096]).then_inc(sem["ddst"], 16)
                sync.wait_ge(sem["v"], base.v + VOFF2[c] + it5(c))
                sync.dma_start(out=inter_h[c * 128:(c + 1) * 128, :],
                               in_=s_int[:, :]).then_inc(sem["intst"], 16)
        ctx.x2r += 1
        ctx.intst += CCH
        ctx.ddst += 128

        @b.gpsimd
        def _(gpsimd):
            barrier(gpsimd, base)
            for c in range(CCH):
                if c == 1:
                    gpsimd.wait_ge(sem["v"], base.v + VOFF2[0] + 4)
                elif c >= 2:
                    gpsimd.wait_ge(sem["v"], base.v + dpos(c - 2) + 4)
                gpsimd.dma_start(out=s_pkg[:],
                                 in_=gate_pT[:, c * 128:(c + 1) * 128].rearrange(
                                     "(a p) n -> p a n", p=128)).then_inc(sem["p5g"], 16)
                gpsimd.dma_start(out=s_pku[:],
                                 in_=up_pT[:, c * 128:(c + 1) * 128].rearrange(
                                     "(a p) n -> p a n", p=128)).then_inc(sem["p5g"], 16)
                gpsimd.dma_start(
                    out=s_scg[:],
                    in_=scgr_h[:, c * 128:(c + 1) * 128].rearrange(
                        "(a p) n -> p a n", p=128)).then_inc(sem["p5g"], 16)
                gpsimd.dma_start(
                    out=s_scu[:],
                    in_=scur_h[:, c * 128:(c + 1) * 128].rearrange(
                        "(a p) n -> p a n", p=128)).then_inc(sem["p5g"], 16)
                if c % 2 == 0:
                    j = c // 2
                    if j >= 1:
                        gpsimd.wait_ge(sem["v"], base.v + ddpos(c - 1) + 2)
                    gpsimd.dma_start(out=s_pkd[:],
                                     in_=down_pT[j * 128:(j + 1) * 128, :]).then_inc(sem["p5g"], 16)
                    gpsimd.dma_start(out=s_scd[:],
                                     in_=scdr_h[j * 128:(j + 1) * 128, :]).then_inc(sem["p5g"], 16)
        ctx.p5g += GOFF[CCH]

        @b.vector
        def _(vector):
            barrier(vector, base)

            def dec_gu(cc):
                vector.wait_ge(sem["p5g"], 16 * (base.p5g + GOFF[cc] + 4))
                if cc >= 2:
                    vector.wait_ge(sem["pe"], base.pe + (cc - 1) * 16)
                decode_half(vector, s_pkg[:, 0:1024], 0, s_scg[:, 0:1024],
                            s_dg[cc % 2][:, 0:1024], 1024).then_inc(sem["v"], 1)
                decode_half(vector, s_pkg[:, 0:1024], 4, s_scg[:, 0:1024],
                            s_dg[cc % 2][:, 1024:2048], 1024).then_inc(sem["v"], 1)
                decode_half(vector, s_pku[:, 0:1024], 0, s_scu[:, 0:1024],
                            s_dup[cc % 2][:, 0:1024], 1024).then_inc(sem["v"], 1)
                decode_half(vector, s_pku[:, 0:1024], 4, s_scu[:, 0:1024],
                            s_dup[cc % 2][:, 1024:2048], 1024).then_inc(sem["v"], 1)

            for c in range(CCH):
                if c == 0:
                    dec_gu(0)
                if c < CCH - 1:
                    dec_gu(c + 1)
                vector.wait_ge(sem["p5g"], 16 * (base.p5g + GOFF[c - (c % 2)] + 6))
                if c % 2 == 0 and c >= 2:
                    vector.wait_ge(sem["ddst"], 16 * (base.ddst + 2 * (c // 2)))
                w0 = (c % 2) * 1024
                decode_half(vector, s_pkd[:, w0:w0 + 1024], 0, s_scd[:, w0:w0 + 1024],
                            s_dd[:, w0:w0 + 1024], 1024).then_inc(sem["v"], 1)
                decode_half(vector, s_pkd[:, w0:w0 + 1024], 4, s_scd[:, w0:w0 + 1024],
                            s_dd[:, 2048 + w0:2048 + w0 + 1024], 1024).then_inc(sem["v"], 1)
                for t in range(NT):
                    h, tq = t // 4, t % 4
                    vector.wait_ge(sem["pe"], base.pe + c * 16 + h * 8 + 4 + tq + 1)
                    vector.wait_ge(sem["s"], base.s + c * 8 + h * 4 + tq + 1)
                    if c >= 1 and t == 0:
                        vector.wait_ge(sem["intst"], 16 * (base.intst + c))
                    vector.tensor_tensor(s_int[:, t * CT:(t + 1) * CT], ps5[4 + tq][:, :],
                                         s_gel[:, tq * CT:(tq + 1) * CT], OP.mult).then_inc(sem["v"], 1)
        ctx.v += VOFF2[CCH]

        @b.scalar
        def _(scalar):
            barrier(scalar, base)
            for c in range(CCH):
                for t in range(NT):
                    h, tq = t // 4, t % 4
                    scalar.wait_ge(sem["pe"], base.pe + c * 16 + h * 8 + tq + 1)
                    if c * 8 + h * 4 + tq >= 4:
                        pidx = c * 8 + h * 4 + tq - 4
                        pc, prem = pidx // 8, pidx % 8
                        scalar.wait_ge(sem["v"], base.v + ipos(pc) + prem + 1)
                    scalar.activation(s_gel[:, tq * CT:(tq + 1) * CT], ps5[tq][:, :],
                                      AF.Gelu_apprx_tanh).then_inc(sem["s"], 1)
        ctx.s += CCH * 8

        @b.tensor
        def _(tensor):
            barrier(tensor, base)
            tensor.wait_ge(sem["x2r"], 16 * (base.x2r + 1))
            for c in range(CCH):
                tensor.wait_ge(sem["v"], base.v + decg_done(c))
                for h in range(2):
                    for tq in range(4):
                        t = 4 * h + tq
                        gidx = c * 8 + h * 4 + tq
                        if gidx >= 4:
                            pidx = gidx - 4
                            pc, prem = pidx // 8, pidx % 8
                            tensor.wait_ge(sem["s"], base.s + pc * 8 + prem + 1)
                        for k in range(KC):
                            mm = tensor.matmul(ps5[tq][:, :], s_dg[c % 2][:, k * 128:(k + 1) * 128],
                                               s_x2r[:, k * T + t * CT:k * T + (t + 1) * CT],
                                               start=(k == 0), stop=(k == KC - 1))
                        mm.then_inc(sem["pe"], 1)
                    if h == 0:
                        tensor.wait_ge(sem["v"], base.v + decu_done(c))
                    for tq in range(4):
                        t = 4 * h + tq
                        uidx = c * 8 + h * 4 + tq
                        if uidx >= 4:
                            pidx = uidx - 4
                            pc, prem = pidx // 8, pidx % 8
                            tensor.wait_ge(sem["v"], base.v + ipos(pc) + prem + 1)
                        for k in range(KC):
                            mm = tensor.matmul(ps5[4 + tq][:, :],
                                               s_dup[c % 2][:, k * 128:(k + 1) * 128],
                                               s_x2r[:, k * T + t * CT:k * T + (t + 1) * CT],
                                               start=(k == 0), stop=(k == KC - 1))
                        mm.then_inc(sem["pe"], 1)
        ctx.pe += CCH * 16
        sta.close()
    st5.close(); _open.remove(st5)

    if nphases < 8:
        with nc.Block("fin") as b:
            @b.gpsimd
            def _(gpsimd):
                barrier(gpsimd, ctx)
        for _st in reversed(_open):
            _st.close()
        stp.close()
        return nc

    st6 = ExitStack(); _open.append(st6)
    sb6 = lambda n_, s_, d_: st6.enter_context(nc.sbuf_tensor(n_, s_, d_))
    s_intr = [sb6(f"s_intr{i}", [128, 64 * CT], BF16) for i in range(2)]
    s_tmp6 = sb6("s_tmp6", [128, KC * CT], BF16)
    s_r6 = sb6("s_r6", [128, KC * CT], BF16)
    s_ddl = [sb6(f"s_ddl{i}", [128, 4 * 1024], BF16) for i in range(2)]
    s_mu6 = sb6("s_mu6", [1, CT], F32)
    s_rs6 = sb6("s_rs6", [1, CT], F32)
    s_st6 = sb6("s_st6", [128, 8], F32)
    s_rt = sb6("s_rt", [128, 2048], BF16)
    s_ot = [sb6(f"s_ot{i}", [128, 2048], BF16) for i in range(2)]

    base = Ctx(ctx)
    PE6, V6, S6, LD6 = 138, 55, 35, 67
    with nc.Block("p6") as b:
        sta = ExitStack()
        ps6 = [sta.enter_context(nc.psum_tensor(f"ps6_{i}", [128, CT], F32)) for i in range(8)]
        tp_alias = [ps6[3 + k][:, 0:64].bitcast(BF16) for k in range(4)]

        def pe6(tau, m):
            return base.pe + tau * PE6 + m

        def v6(tau, m):
            return base.v + tau * V6 + m

        def s6(tau, m):
            return base.s + tau * S6 + m

        @b.sync
        def _(sync):
            barrier(sync, base)
            for tau in range(NT):
                ldb = base.ld + tau * LD6
                if tau >= 1:
                    sync.wait_ge(sem["pe"], pe6(tau - 1, 66))
                sync.dma_start(out=s_tmp6[:].rearrange("p (c t) -> p c t", c=KC),
                               in_=x2_h[:, tau * CT:(tau + 1) * CT].rearrange(
                                   "(c p) t -> p c t", p=128)).then_inc(sem["x2t"], 16)
                if tau >= 1:
                    sync.wait_ge(sem["pe"], pe6(tau - 1, 48))
                sync.dma_start(out=s_intr[0][:].rearrange("p (c t) -> p c t", c=64),
                               in_=inter_h[0:8192, tau * CT:(tau + 1) * CT].rearrange(
                                   "(c p) t -> p c t", p=128)).then_inc(sem["inA"], 16)
                if tau >= 1:
                    sync.wait_ge(sem["pe"], pe6(tau - 1, 64))
                sync.dma_start(out=s_intr[1][:].rearrange("p (c t) -> p c t", c=64),
                               in_=inter_h[8192:NM, tau * CT:(tau + 1) * CT].rearrange(
                                   "(c p) t -> p c t", p=128)).then_inc(sem["inB"], 16)
                for grp in range(2):
                    for G in range(32):
                        gg = tau * 64 + grp * 32 + G
                        if gg >= 2:
                            pgg = gg - 2
                            pt, prem = pgg // 64, pgg % 64
                            sync.wait_ge(sem["pe"], pe6(pt, prem + 1))
                        sync.dma_start(
                            out=s_ddl[G % 2][:].rearrange("p (c n) -> p c n", c=4),
                            in_=ddec_h[G * 512:(G + 1) * 512,
                                       grp * 1024:(grp + 1) * 1024].rearrange(
                                "(c p) n -> p c n", p=128)
                        ).then_inc(sem["ddl0" if gg % 2 == 0 else "ddl1"], 16)
        ctx.x2t += NT
        ctx.inA += NT
        ctx.inB += NT
        ctx.ddl0 += NT * 32
        ctx.ddl1 += NT * 32

        @b.tensor
        def _(tensor):
            barrier(tensor, base)
            for tau in range(NT):
                ldb = base.ld + tau * LD6
                for grp in range(2):
                    if tau >= 1 and grp == 0:
                        tensor.wait_ge(sem["v"], v6(tau - 1, V6))
                        tensor.wait_ge(sem["s"], s6(tau - 1, S6))
                    if grp == 1:
                        tensor.wait_ge(sem["v"], v6(tau, 8))
                    tensor.wait_ge(sem["inA" if grp == 0 else "inB"],
                                   16 * (getattr(base, "inA" if grp == 0 else "inB") + tau + 1))
                    for G in range(32):
                        gg = tau * 64 + grp * 32 + G
                        dsm = "ddl0" if gg % 2 == 0 else "ddl1"
                        tensor.wait_ge(sem[dsm], 16 * (getattr(base, dsm) + gg // 2 + 1))
                        for cc in range(4):
                            c = G * 4 + cc
                            intr = s_intr[0] if c < 64 else s_intr[1]
                            for dc in range(8):
                                mm = tensor.matmul(
                                    ps6[dc][:, :],
                                    s_ddl[G % 2][:, cc * 1024 + dc * 128:cc * 1024 + (dc + 1) * 128],
                                    intr[:, (c % 64) * CT:(c % 64 + 1) * CT],
                                    start=(c == 0), stop=(c == CCH - 1))
                        mm.then_inc(sem["pe"], 1)
                tensor.wait_ge(sem["v"], v6(tau, 16))
                for k in range(KC):
                    mm = tensor.matmul(ps6[0][0:1, :], s_onesb[:, 0:1],
                                       s_r6[:, k * CT:(k + 1) * CT],
                                       start=(k == 0), stop=(k == KC - 1))
                mm.then_inc(sem["pe"], 1)
                tensor.wait_ge(sem["s"], s6(tau, 2))
                for k in range(KC):
                    mm = tensor.matmul(ps6[1][0:1, :], s_onesb[:, 0:1],
                                       s_tmp6[:, k * CT:(k + 1) * CT],
                                       start=(k == 0), stop=(k == KC - 1))
                mm.then_inc(sem["pe"], 1)
                tensor.wait_ge(sem["s"], s6(tau, 1))
                for j in range(4):
                    tensor.matmul(ps6[2][:, j:j + 1],
                                  s_mu6[0:1, j * 128:(j + 1) * 128], s_ones1[0:1, 0:1],
                                  start=True, stop=True).then_inc(sem["pe"], 1)
                tensor.wait_ge(sem["s"], s6(tau, 3))
                for j in range(4):
                    tensor.matmul(ps6[2][:, 4 + j:5 + j],
                                  s_rs6[0:1, j * 128:(j + 1) * 128], s_ones1[0:1, 0:1],
                                  start=True, stop=True).then_inc(sem["pe"], 1)
                cnt = 0
                for j in range(4):
                    for dc in range(16):
                        if cnt >= 4:
                            prev = cnt - 4
                            pj, pdc = prev // 16, prev % 16
                            if pdc % 2 == 0:
                                tensor.wait_ge(sem["v"], v6(tau, 19 + pj * 9 + pdc // 2 + 1))
                            else:
                                tensor.wait_ge(sem["s"], s6(tau, 3 + pj * 8 + (pdc - 1) // 2 + 1))
                        tensor.transpose(tp_alias[cnt % 4],
                                         s_r6[:, dc * CT + j * 128:dc * CT + (j + 1) * 128],
                                         s_identb[:, :]).then_inc(sem["pe"], 1)
                        cnt += 1
        ctx.pe += NT * PE6

        @b.vector
        def _(vector):
            barrier(vector, base)
            for tau in range(NT):
                vector.wait_ge(sem["x2t"], 16 * (base.x2t + tau + 1))
                if debug and tau >= 1:
                    vector.wait_ge(sem["dbgst"], 16 * (base.dbgst + 2 * tau))
                for grp in range(2):
                    vector.wait_ge(sem["pe"], pe6(tau, 32 * (grp + 1)))
                    for dc in range(8):
                        vector.tensor_tensor(
                            s_r6[:, (grp * 8 + dc) * CT:(grp * 8 + dc + 1) * CT],
                            ps6[dc][:, :],
                            s_tmp6[:, (grp * 8 + dc) * CT:(grp * 8 + dc + 1) * CT],
                            OP.add).then_inc(sem["v"], 1)
                vector.wait_ge(sem["s"], s6(tau, 1))
                vector.tensor_tensor(s_rs6[0:1, :], s_mu6[0:1, :], s_mu6[0:1, :],
                                     OP.mult).then_inc(sem["v"], 1)
                vector.wait_ge(sem["pe"], pe6(tau, 66))
                vector.scalar_tensor_tensor(s_rs6[0:1, :], ps6[1][0:1, :], 1.0 / 2048.0,
                                            s_rs6[0:1, :], OP.mult, OP.subtract
                                            ).then_inc(sem["v"], 1)
                vector.wait_ge(sem["pe"], pe6(tau, 74))
                vector.tensor_copy(s_st6[:, 0:8], ps6[2][:, 0:8]).then_inc(sem["v"], 1)
                for j in range(4):
                    for dc in range(0, 16, 2):
                        cnt = j * 16 + dc
                        vector.wait_ge(sem["pe"], pe6(tau, 74 + cnt + 1))
                        vector.tensor_copy(s_rt[:, dc * 128:(dc + 1) * 128],
                                           tp_alias[cnt % 4]).then_inc(sem["v"], 1)
                    vector.wait_ge(sem["s"], s6(tau, 3 + (j + 1) * 8))
                    gj = tau * 4 + j
                    if gj >= 2:
                        osm = "ost0" if gj % 2 == 0 else "ost1"
                        vector.wait_ge(sem[osm], 16 * (getattr(base, osm) + (gj - 2) // 2 + 1))
                    vector.tensor_scalar(s_ot[j % 2][:, :], s_rt[:, :],
                                         s_st6[:, j:j + 1], s_st6[:, 4 + j:5 + j],
                                         OP.subtract, OP.mult).then_inc(sem["v"], 1)
        ctx.v += NT * V6

        @b.scalar
        def _(scalar):
            barrier(scalar, base)
            for tau in range(NT):
                scalar.wait_ge(sem["pe"], pe6(tau, 65))
                scalar.activation(s_mu6[0:1, :], ps6[0][0:1, :], AF.Identity,
                                  scale=cst1(C_INV2048)).then_inc(sem["s"], 1)
                scalar.wait_ge(sem["v"], v6(tau, 16))
                scalar.activation(s_tmp6[:], s_r6[:], AF.Square).then_inc(sem["s"], 1)
                scalar.wait_ge(sem["v"], v6(tau, 18))
                scalar.activation(s_rs6[0:1, :], s_rs6[0:1, :], AF.Ln, bias=cst1(C_LNEPS))
                scalar.activation(s_rs6[0:1, :], s_rs6[0:1, :], AF.Exp,
                                  scale=cst1(C_NEGHALF)).then_inc(sem["s"], 1)
                for j in range(4):
                    for dc in range(1, 16, 2):
                        cnt = j * 16 + dc
                        if j >= 1 and dc == 1:
                            scalar.wait_ge(sem["v"], v6(tau, 28 + 9 * (j - 1)))
                        scalar.wait_ge(sem["pe"], pe6(tau, 74 + cnt + 1))
                        scalar.copy(s_rt[:, dc * 128:(dc + 1) * 128],
                                    tp_alias[cnt % 4]).then_inc(sem["s"], 1)
        ctx.s += NT * S6

        @b.gpsimd
        def _(gpsimd):
            barrier(gpsimd, base)
            for tau in range(NT):
                if debug:
                    gpsimd.wait_ge(sem["v"], v6(tau, 16))
                    gpsimd.dma_start(out=dbg_r6[:, tau * 8192:(tau + 1) * 8192],
                                     in_=s_r6[:]).then_inc(sem["dbgst"], 16)
                    gpsimd.wait_ge(sem["v"], v6(tau, 19))
                    gpsimd.dma_start(out=dbg_st6[:, tau * 8:(tau + 1) * 8],
                                     in_=s_st6[:, 0:8]).then_inc(sem["dbgst"], 16)
                for j in range(4):
                    gj = tau * 4 + j
                    gpsimd.wait_ge(sem["v"], v6(tau, 28 + 9 * j))
                    gpsimd.dma_start(out=out_dev[tau * CT + j * 128:tau * CT + (j + 1) * 128, :],
                                     in_=s_ot[j % 2][:, :]
                                     ).then_inc(sem["ost0" if gj % 2 == 0 else "ost1"], 16)
            gpsimd.wait_ge(sem["ost0"], 16 * (base.ost0 + NT * 2))
            gpsimd.wait_ge(sem["ost1"], 16 * (base.ost1 + NT * 2))
            if debug:
                gpsimd.wait_ge(sem["dbgst"], 16 * (base.dbgst + NT * 2))
        ctx.ost0 += NT * 2
        ctx.ost1 += NT * 2
        if debug:
            ctx.dbgst += NT * 2
        sta.close()
    st6.close(); _open.remove(st6)
    stp.close()
    return nc


def prep_shared(inp):
    perm = np.concatenate([np.arange(0, D, 2), np.arange(1, D, 2)])
    perm2 = np.concatenate([np.arange(0, NM, 2), np.arange(1, NM, 2)])
    W_in = np.asarray(inp["L0_attn_W_in"], np.float32)
    Wf = np.concatenate([
        W_in[0:4096],
        W_in[4096:8192],
        np.repeat(W_in[8192:8208], 16, axis=0),
        W_in[8208:8464],
        W_in[8464:8720],
    ], axis=0)
    d = {}
    d["w_inT"] = np.ascontiguousarray(Wf.T[perm])
    d["w_outT"] = np.ascontiguousarray(np.asarray(inp["L0_attn_W_out"], np.float32).T[:, perm])
    d["w_h2i"] = np.ascontiguousarray(np.repeat(np.asarray(inp["L0_attn_W_h2i"], np.float32).T, 16, axis=0))
    d["wnorm"] = np.ascontiguousarray(np.asarray(inp["L0_attn_w_norm"], np.float32)[perm].reshape(KC, 128).T)
    d["dtb"] = np.ascontiguousarray(np.repeat(np.asarray(inp["L0_attn_dt_bias"], np.float32), 16).reshape(2, 128).T)
    d["alog"] = np.ascontiguousarray(np.repeat(np.asarray(inp["L0_attn_A_log"], np.float32), 16).reshape(2, 128).T)
    d["wcv"] = np.ascontiguousarray(
        np.asarray(inp["L0_attn_w_conv"], np.float32).reshape(DI, 4).reshape(ICH, 128, 4)
        .transpose(1, 0, 2).reshape(128, 128))
    d["gate_pT"] = np.ascontiguousarray(np.asarray(inp["L0_ffn_gate_W_packed"], np.int32).T[:, perm2])
    d["up_pT"] = np.ascontiguousarray(np.asarray(inp["L0_ffn_up_W_packed"], np.int32).T[:, perm2])
    d["down_pT"] = np.ascontiguousarray(np.asarray(inp["L0_ffn_down_W_packed"], np.int32).T[:, perm])
    d["gate_scT"] = np.ascontiguousarray(np.asarray(inp["L0_ffn_gate_scales"], np.float32).T[:, perm2])
    d["up_scT"] = np.ascontiguousarray(np.asarray(inp["L0_ffn_up_scales"], np.float32).T[:, perm2])
    d["down_scT"] = np.ascontiguousarray(np.asarray(inp["L0_ffn_down_scales"], np.float32).T[:, perm])
    cstv = np.zeros(12, np.float32)
    cstv[C_ONE] = 1.0
    cstv[C_NEG1] = -1.0
    cstv[C_NEGHALF] = -0.5
    cstv[C_INV2048] = 1.0 / 2048.0
    cstv[C_RMSEPS] = RMS_EPS
    cstv[C_LNEPS] = LN_EPS
    cstv[C_7P5] = 7.5
    cstv[C_N8] = -8.0
    cstv[C_N10] = -10.0
    d["cst_in"] = np.tile(cstv, (128, 1))
    d["ones1"] = np.ones((1, 128), np.float32)
    d["onesb"] = np.ones((128, 1), ml_dtypes.bfloat16)
    ind = np.zeros((128, 8, 128), np.float32)
    for bblk in range(8):
        ind[:, bblk, 16 * bblk:16 * bblk + 16] = 1.0 / 256.0
    d["ind_in"] = ind.reshape(128, 1024).astype(ml_dtypes.bfloat16)
    d["identb_in"] = np.eye(128, dtype=ml_dtypes.bfloat16)
    d["identf_in"] = np.eye(128, dtype=np.float32)
    d["_perm"] = perm
    return d


def prep_core_x(inp, core, perm):
    x = np.asarray(inp["L0_attn_x"], np.float32)[core]
    return np.ascontiguousarray(x.T[perm])


def unpermute_out(res_out, perm):
    inv = np.empty_like(perm)
    inv[perm] = np.arange(D)
    return np.ascontiguousarray(np.asarray(res_out, np.float32)[:, inv])


def kernel(**inputs):
    from concourse.bass_utils import run_bass_kernel_spmd
    shared = prep_shared(inputs)
    perm = shared.pop("_perm")
    nc = build_nc(debug=False)
    in_maps = []
    for core in range(8):
        m = dict(shared)
        m["x_dev"] = prep_core_x(inputs, core, perm)
        in_maps.append(m)
    res = run_bass_kernel_spmd(nc, in_maps, core_ids=list(range(8)))
    out = np.stack([unpermute_out(r["out_dev"], perm) for r in res.results], axis=0)
    return out
